# version 11
# baseline (speedup 1.0000x reference)
import os
from contextlib import ExitStack

import numpy as np
import ml_dtypes

import concourse.bass as bass
import concourse.mybir as mybir
from concourse.bass_utils import run_bass_kernel_spmd

F32 = mybir.dt.float32
BF16 = mybir.dt.bfloat16
AF = mybir.ActivationFunctionType
ALU = mybir.AluOpType
AXX = mybir.AxisListType.X

MAGIC = 12582912.0
TPC = 1024
NCH = 16
D = 2048
SEQ = 4096
NH = 16
HFF = 8192
HCH = 64
ISH = float(1.0 / np.sqrt(128.0))
NEG_HALF_ISH = float(-0.5 / np.sqrt(128.0))
INV_SQRT_HFF = float(1.0 / np.sqrt(8192.0))
LN_EPS = 1e-5
RMS_EPS = 1e-6

STAGE = int(os.environ.get("MK_STAGE", "4"))


class Ctr:
    def __init__(self, sem, is_dma=False):
        self.sem = sem
        self.n = 0
        self.step = 16 if is_dma else 1

    def snap(self):
        return (self.sem, self.n)


class MK:
    def __init__(self, stage=STAGE):
        self.stage = stage
        self.nc = bass.Bass("TRN2", target_bir_lowering=False)
        self.es = ExitStack()
        self.streams = {k: [] for k in ("sync", "gpsimd", "tensor", "scalar", "vector")}
        from collections import deque
        self.bg_tasks = deque()
        self._declare()
        self._alloc()

    def _declare(self):
        nc = self.nc
        dt = nc.dram_tensor
        I = lambda n, s, d=F32: dt(n, s, d, kind="ExternalInput")
        self.x_in = I("x_in", [TPC, D])
        self.wq0c = I("wq0c", [D, D]); self.wk0c = I("wk0c", [D, D])
        self.wv0c = I("wv0c", [D, D]); self.wo0c = I("wo0c", [D, D])
        self.sq0 = I("sq0", [D, D]); self.sk0 = I("sk0", [D, D])
        self.sv0 = I("sv0", [D, D]); self.so0 = I("so0", [D, D])
        self.w0T = I("w0T", [D, D])
        self.wq1 = I("wq1", [D, D]); self.wk1 = I("wk1", [D, 512])
        self.wv1 = I("wv1", [D, 512]); self.wo1 = I("wo1", [D, D])
        self.w1T = I("w1T", [D, HFF]); self.w3T = I("w3T", [D, HFF])
        self.w2T = I("w2T", [HFF, D])
        self.c_idf = I("c_idf", [128, 128])
        self.c_ones_col_b = I("c_ones_col_b", [128, 1], BF16)
        self.c_ones_row_b = I("c_ones_row_b", [1, 128], BF16)
        self.c_ones_row_f = I("c_ones_row_f", [1, 128])
        self.c_h128 = I("c_h128", [128, 128], BF16)
        self.c_bias0 = I("c_bias0", [128, 64])
        self.c_bias1 = I("c_bias1", [128, 48])
        self.c_gamma = I("c_gamma", [128, 16])
        self.c_bvrow0 = I("c_bvrow0", [1, 2048])
        self.c_bvrow1 = I("c_bvrow1", [1, 512])

        self.out = dt("out", [TPC, D], F32, kind="ExternalOutput")

        self.wq0b = dt("wq0b", [D, D], BF16)
        self.w0qb = dt("w0qb", [D, D], BF16)
        self.kin0 = dt("kin0", [2048, 1024], BF16)
        self.kout0g = [dt(f"kout0g{g}", [2048, 1024], BF16) for g in range(4)]
        self.vin0g = [dt(f"vin0g{g}", [1024, 512], BF16) for g in range(4)]
        self.vout0g = [dt(f"vout0g{g}", [4096, 512], BF16) for g in range(4)]
        self.kin1 = dt("kin1", [512, 1024], BF16)
        self.kout1 = dt("kout1", [2048, 1024], BF16)
        self.vin1 = dt("vin1", [1024, 512], BF16)
        self.vout1 = dt("vout1", [4096, 512], BF16)
        self.ar0_in = dt("ar0_in", [1, 32], F32)
        self.ar0_out = dt("ar0_out", [1, 32], F32)
        self.ar1_in = dt("ar1_in", [1, 8], F32)
        self.ar1_out = dt("ar1_out", [1, 8], F32)
        self.xq_sp = dt("xq_sp", [2048, 1024], BF16)
        self.z2_sp = dt("z2_sp", [2048, 1024], F32)
        self.w1b = dt("w1b", [D, HFF], BF16)
        self.w3b = dt("w3b", [D, HFF], BF16)
        self.w2b = dt("w2b", [HFF, D], BF16)

    def _alloc(self):
        nc, E = self.nc, self.es.enter_context
        T = lambda n, s, d: E(nc.sbuf_tensor(n, s, d))
        self.idf = T("idf", [128, 128], F32)
        self.ones_col_b = T("ones_col_b", [128, 1], BF16)
        self.ones_row_b = T("ones_row_b", [1, 128], BF16)
        self.ones_row_f = T("ones_row_f", [1, 128], F32)
        self.h128 = T("h128", [128, 128], BF16)
        self.bias0 = T("bias0", [128, 64], F32)
        self.bias1 = T("bias1", [128, 48], F32)
        self.gamma = T("gamma", [128, 16], F32)
        self.rowf = T("rowf", [1, 2048], F32)
        self.rowf2 = T("rowf2", [1, 128], F32)
        self.misc_pp = T("misc_pp", [128, 8], F32)
        self.negb = T("negb", [128, 16], F32)
        self.wacc = T("wacc", [128, 2], F32)

        self.x = T("x", [128, NCH * TPC], F32)
        self.xnb = T("xnb", [128, NCH * TPC], BF16)
        self.att = T("att", [128, NCH * TPC], BF16)
        self.PF = T("PF", [128, 3072], F32)
        self.PB = T("PB", [128, 26624], BF16)
        self.gscaled = T("gscaled", [128, 16], F32)

        self.pk = [E(nc.psum_tensor(f"pk{i}", [128, 512], F32)) for i in range(8)]
        self.prow = {}
        for i in (4, 5, 6, 7):
            v = nc.virtual_psum_tensor(f"prow{i}", [1, 512], F32)
            nc.back_virtual_tensor_with(v, self.pk[i])
            self.prow[i] = v

        self.dS = Ctr(E(nc.semaphore("dS")), True)
        self.dQ = Ctr(E(nc.semaphore("dQ")), True)
        self.dG = Ctr(E(nc.semaphore("dG")), True)
        self.rS = Ctr(E(nc.semaphore("rS")))
        self.rQ = Ctr(E(nc.semaphore("rQ")))
        self.rG = Ctr(E(nc.semaphore("rG")))
        self._qmap = {self.dS.sem: ("sync", self.dS, self.rS),
                      self.dQ.sem: ("scalar", self.dQ, self.rQ),
                      self.dG.sem: ("gpsimd", self.dG, self.rG)}
        self._flush_cache = {}
        self.cc = Ctr(E(nc.semaphore("cc")))
        self.tE = Ctr(E(nc.semaphore("tE")))
        self.sE = Ctr(E(nc.semaphore("sE")))
        self.vE = Ctr(E(nc.semaphore("vE")))

    def flush(self, qeng, at=None):
        qsem, ctr, relay = {"sync": (self.dS.sem, self.dS, self.rS),
                            "scalar": (self.dQ.sem, self.dQ, self.rQ),
                            "gpsimd": (self.dG.sem, self.dG, self.rG)}[qeng]
        need = ctr.n if at is None else at
        for cnt, snap in self._flush_cache.setdefault(qeng, []):
            if cnt >= need:
                return snap
        val = ctr.n
        relay.n += 1
        rsnap = (relay.sem, relay.n)

        def run(engine):
            engine.wait_ge(qsem, val)
            engine.nop().then_inc(relay.sem, 1)
        self.streams[qeng].append(run)
        self._flush_cache[qeng].append((val, rsnap))
        return rsnap

    def em(self, eng, waits, fn, sig=None):
        ws = []
        for w in waits:
            if w is None or w[1] <= 0:
                continue
            q = self._qmap.get(w[0])
            if q is not None and q[0] != eng:
                ws.append(self.flush(q[0], at=w[1]))
            elif q is not None:
                ws.append((w[0], q[1].n))
            else:
                ws.append(w)
        ws = [w for w in ws if w[1] > 0]
        if sig is not None:
            sig.n += sig.step
            sem, val, step = sig.sem, sig.n, sig.step

        def run(engine):
            for s, v in ws:
                engine.wait_ge(s, v)
            inst = fn(engine)
            if sig is not None:
                assert inst is not None
                inst.then_inc(sem, step)
        self.streams[eng].append(run)
        return sig.snap() if sig is not None else None

    def dma(self, eng, out, in_, waits=()):
        ctr = {"sync": self.dS, "scalar": self.dQ, "gpsimd": self.dG}[eng]
        return self.em(eng, waits, lambda e: e.dma_start(out=out, in_=in_), sig=ctr)

    def barrier(self):
        return [self.dS.snap(), self.dQ.snap(), self.dG.snap(),
                self.tE.snap(), self.sE.snap(), self.vE.snap(), self.cc.snap()]

    def pf(self, off, cols):
        assert off % 4 == 0
        return self.PF[:, off // 4: off // 4 + cols]

    def pb(self, off, cols):
        assert off % 2 == 0
        return self.PB[:, off // 2: off // 2 + cols]

    def bvrow0(self):
        return self.PB[0:1, 14336:16384]

    def bvrow1(self):
        return self.PB[0:1, 16384:16896]

    def hid(self, c):
        if c < 32:
            return self.pb(20480 + c * 1024, 512)
        return self.att[:, (c - 32) * 512:(c - 32 + 1) * 512]

    def finalize(self):
        nc = self.nc
        with self.es:
            with nc.Block() as block:
                @block.sync
                def _(e):
                    for f in self.streams["sync"]:
                        f(e)

                @block.gpsimd
                def _(e):
                    for f in self.streams["gpsimd"]:
                        f(e)

                @block.tensor
                def _(e):
                    for f in self.streams["tensor"]:
                        f(e)

                @block.scalar
                def _(e):
                    for f in self.streams["scalar"]:
                        f(e)

                @block.vector
                def _(e):
                    for f in self.streams["vector"]:
                        f(e)
        return nc



def ph_prologue(mk):
    for dst, src in [
        (mk.idf[:], mk.c_idf.ap()),
        (mk.ones_col_b[:], mk.c_ones_col_b.ap()),
        (mk.ones_row_b[:], mk.c_ones_row_b.ap()),
        (mk.ones_row_f[:], mk.c_ones_row_f.ap()),
        (mk.h128[:], mk.c_h128.ap()),
        (mk.bias0[:], mk.c_bias0.ap()), (mk.bias1[:], mk.c_bias1.ap()),
        (mk.gamma[:], mk.c_gamma.ap()),
    ]:
        mk.consts_ready = mk.dma("sync", dst, src)
    mk.em("vector", [], lambda e: e.memset(mk.rowf2[0:1, 0:128], 0.0), sig=mk.vE)
    mk.ar_zeroed = mk.vE.snap()

    xr = mk.x_in.ap().rearrange("(t p) d -> t p d", p=128)
    xv = mk.x[:, :].rearrange("p (c t) -> p c t", c=NCH)
    st = {}
    for tt in range(8):
        stage = mk.pf(0, 2048)
        ld = mk.dma("scalar", stage, xr[tt], [st.get("ev")])
        for cg in range(4):
            bank = mk.pk[cg % 2]

            def tfn(e, stage=stage, cg=cg, bank=bank):
                last = None
                for k in range(4):
                    last = e.transpose(bank[:, k * 128:(k + 1) * 128],
                                       stage[:, (cg * 4 + k) * 128:(cg * 4 + k + 1) * 128],
                                       mk.idf[:])
                return last
            ts = mk.em("tensor", [ld, mk.consts_ready, st.get(("bank", cg % 2))],
                       tfn, sig=mk.tE)
            dst3 = xv[:, cg * 4:(cg + 1) * 4, tt * 128:(tt + 1) * 128]
            ev = mk.em("vector", [ts],
                       lambda e, dst3=dst3, bank=bank: e.tensor_copy(dst3, bank[:, 0:512]),
                       sig=mk.vE)
            st[("bank", cg % 2)] = ev
        st["ev"] = ev
    mk.x_ready = mk.vE.snap()


def _wjit(mk, codes, scexp, col0, ncols, dst_bf, extra_waits, qa="sync", qb="scalar"):
    assert ncols == 128
    st = mk._wjit_state = getattr(mk, "_wjit_state", {})
    cap = codes.ap().rearrange("(c p) o -> p c o", p=128)[:, :, col0:col0 + ncols]
    sap = scexp.ap().rearrange("(c p) o -> p c o", p=128)[:, :, col0:col0 + ncols]
    dst3 = dst_bf if len(dst_bf.shape) == 3 else dst_bf.rearrange("p (c o) -> p c o", c=16)
    a = mk.pf(0, 1024)
    bsl = [mk.pb(47104, 1024), mk.pb(49152, 1024)]
    tt = None
    for ich in range(2):
        ew = list(extra_waits) if ich == 0 else []
        la = mk.dma(qa, a, cap[:, ich * 8:(ich + 1) * 8, :], [st.get("tt")] + ew)
        lb = mk.dma("gpsimd", bsl[ich], sap[:, ich * 8:(ich + 1) * 8, :],
                    [st.get("tt")] + ew)
        tt = mk.em("vector", [la, lb],
                   lambda e, ich=ich: e.tensor_tensor(
                       dst3[:, ich * 8:(ich + 1) * 8, :], a,
                       bsl[ich].rearrange("p (c o) -> p c o", c=8), ALU.mult), sig=mk.vE)
        st["tt"] = tt
    return tt


def ph_ln(mk, layer):
    xv = mk.x[:, :].rearrange("p (c t) -> p c t", c=NCH)
    bar = mk.barrier()
    prev_mm = None
    for c in range(16):
        xb = mk.pb((c % 2) * 2048, 1024)
        sq = mk.pb(4096 + (c % 2) * 2048, 1024)
        wx = ([mk.x_ready] if layer == 0 else [mk.ln1_start, mk.ln1_start_t])
        if c < 2:
            wx = wx + bar
        cast = mk.em("vector", wx + [prev_mm],
                     lambda e, xb=xb, c=c: e.tensor_copy(xb, xv[:, c, :]), sig=mk.vE)
        sqd = mk.em("scalar", wx + [prev_mm],
                    lambda e, sq=sq, c=c: e.activation(sq, xv[:, c, :], AF.Square),
                    sig=mk.sE)

        def mfn(e, xb=xb, sq=sq, c=c):
            last = None
            for th in range(2):
                last = e.matmul(mk.prow[4 + th][:], mk.ones_col_b[:],
                                xb[:, th * 512:(th + 1) * 512],
                                start=(c == 0), stop=(c == 15))
                last = e.matmul(mk.prow[6 + th][:], mk.ones_col_b[:],
                                sq[:, th * 512:(th + 1) * 512],
                                start=(c == 0), stop=(c == 15))
            return last
        prev_mm = mk.em("tensor", [cast, sqd, mk.consts_ready], mfn, sig=mk.tE)
    def s1(e):
        e.activation(mk.rowf[0:1, 0:512], mk.prow[4][:], AF.Copy, scale=1.0 / 2048)
        e.activation(mk.rowf[0:1, 512:1024], mk.prow[5][:], AF.Copy, scale=1.0 / 2048)
        e.activation(mk.rowf[0:1, 1024:1536], mk.prow[6][:], AF.Copy, scale=1.0 / 2048)
        return e.activation(mk.rowf[0:1, 1536:2048], mk.prow[7][:], AF.Copy, scale=1.0 / 2048)
    rows = mk.em("scalar", [prev_mm], s1, sig=mk.sE)

    def v1(e):
        e.tensor_tensor(mk.PF[0:1, 0:1024], mk.rowf[0:1, 0:1024], mk.rowf[0:1, 0:1024], ALU.mult)
        e.drain()
        e.tensor_tensor(mk.rowf[0:1, 1024:2048], mk.rowf[0:1, 1024:2048],
                        mk.PF[0:1, 0:1024], ALU.subtract)
        e.drain()
        return e.tensor_scalar(mk.rowf[0:1, 1024:2048], mk.rowf[0:1, 1024:2048],
                               LN_EPS, None, ALU.add)
    var = mk.em("vector", [rows], v1, sig=mk.vE)

    def s2(e):
        e.activation(mk.rowf[0:1, 1024:2048], mk.rowf[0:1, 1024:2048], AF.Ln)
        e.drain()
        return e.activation(mk.rowf[0:1, 1024:2048], mk.rowf[0:1, 1024:2048], AF.Exp, scale=-0.5)
    istd = mk.em("scalar", [var], s2, sig=mk.sE)

    mbc = mk.pf(0, 1024)
    ibc = mk.pf(4096, 1024)

    def b1(e):
        e.matmul(mk.pk[0][:], mk.ones_row_f[:], mk.rowf[0:1, 0:512], start=True, stop=True)
        e.matmul(mk.pk[1][:], mk.ones_row_f[:], mk.rowf[0:1, 512:1024], start=True, stop=True)
        e.matmul(mk.pk[2][:], mk.ones_row_f[:], mk.rowf[0:1, 1024:1536], start=True, stop=True)
        return e.matmul(mk.pk[3][:], mk.ones_row_f[:], mk.rowf[0:1, 1536:2048], start=True, stop=True)
    bc = mk.em("tensor", [istd, rows, var], b1, sig=mk.tE)

    def e1(e):
        e.tensor_copy(mbc[:, 0:512], mk.pk[0][:])
        e.tensor_copy(mbc[:, 512:1024], mk.pk[1][:])
        e.tensor_copy(ibc[:, 0:512], mk.pk[2][:])
        return e.tensor_copy(ibc[:, 512:1024], mk.pk[3][:])
    bce = mk.em("vector", [bc], e1, sig=mk.vE)

    st = {}
    prev_mm = None
    for c in range(16):
        if layer == 0:
            z2r = mk.pf(8192, 1024)
            sq = mk.pb(4096 + (c % 2) * 2048, 1024)

            def a1(e, c=c, z2r=z2r):
                e.tensor_tensor(z2r, xv[:, c, :], mbc, ALU.subtract)
                e.drain()
                return e.tensor_tensor(z2r, z2r, ibc, ALU.mult)
            ap = mk.em("vector", [bce, st.get("dma"), st.get("mm"),
                                  st.get("sc")], a1, sig=mk.vE)

            def c1(e, c=c, z2r=z2r, sq=sq):
                e.activation(mk.xnb[:, c * TPC:(c + 1) * TPC], z2r, AF.Copy)
                return e.activation(sq, z2r, AF.Square)
            sc = mk.em("scalar", [ap, rows], c1, sig=mk.sE)
            st["sc"] = sc

            def m1(e, sq=sq, c=c):
                e.matmul(mk.prow[4][:], mk.ones_col_b[:], sq[:, 0:512],
                         start=(c == 0), stop=(c == 15))
                return e.matmul(mk.prow[5][:], mk.ones_col_b[:], sq[:, 512:1024],
                                start=(c == 0), stop=(c == 15))
            prev_mm = mk.em("tensor", [sc], m1, sig=mk.tE)
            st["mm"] = prev_mm
            st["dma"] = mk.dma("gpsimd", mk.z2_sp.ap()[c * 128:(c + 1) * 128, :],
                               z2r, [sc])
        else:
            z2r = mk.pf(8192, 1024)
            st["dummy"] = None

            def a2(e, c=c, z2r=z2r):
                e.drain()
                e.tensor_tensor(z2r, xv[:, c, :], mbc, ALU.subtract)
                e.drain()
                return e.tensor_tensor(mk.xnb[:, c * TPC:(c + 1) * TPC], z2r, ibc, ALU.mult)
            mk.em("vector", [bce], a2, sig=mk.vE)
    mk.xn_ready_v = mk.vE.snap()
    mk.xn_ready_s = mk.sE.snap()
    if layer == 0:
        mk.rms_mm = prev_mm
        mk.z2_spilled = mk.dG.snap()


def ph_quantpre(mk):
    bar = mk.barrier()
    def s1(e):
        e.activation(mk.rowf[0:1, 0:512], mk.prow[4][:], AF.Copy, scale=1.0 / 2048)
        return e.activation(mk.rowf[0:1, 512:1024], mk.prow[5][:], AF.Copy, scale=1.0 / 2048)
    ms = mk.em("scalar", [mk.rms_mm], s1, sig=mk.sE)
    eps = mk.em("vector", [ms],
                lambda e: e.tensor_scalar(mk.rowf[0:1, 0:1024], mk.rowf[0:1, 0:1024],
                                          RMS_EPS, None, ALU.add), sig=mk.vE)

    def s2(e):
        e.activation(mk.rowf[0:1, 0:1024], mk.rowf[0:1, 0:1024], AF.Ln)
        e.drain()
        return e.activation(mk.rowf[0:1, 0:1024], mk.rowf[0:1, 0:1024], AF.Exp, scale=-0.5)
    irow = mk.em("scalar", [eps], s2, sig=mk.sE)
    irbc = mk.pf(8192, 1024)

    def b1(e):
        e.matmul(mk.pk[0][:], mk.ones_row_f[:], mk.rowf[0:1, 0:512], start=True, stop=True)
        return e.matmul(mk.pk[1][:], mk.ones_row_f[:], mk.rowf[0:1, 512:1024], start=True, stop=True)
    bc = mk.em("tensor", [irow, mk.z2_spilled], b1, sig=mk.tE)

    def e1(e):
        e.tensor_copy(irbc[:, 0:512], mk.pk[0][:])
        return e.tensor_copy(irbc[:, 512:1024], mk.pk[1][:])
    bce = mk.em("vector", [bc], e1, sig=mk.vE)

    lm = None
    st = {}
    for c in range(16):
        z2r = mk.pf(4096, 1024)
        ld = mk.dma("sync", z2r, mk.z2_sp.ap()[c * 128:(c + 1) * 128, :],
                    [mk.z2_spilled, st.get("rd")] + (bar if c == 0 else []))

        def q1(e, c=c, z2r=z2r):
            e.scalar_tensor_tensor(z2r, z2r, mk.gamma[:, c:c + 1], irbc, ALU.mult, ALU.mult)
            e.drain()
            e.tensor_reduce(mk.misc_pp[:, 3:4], z2r, AXX, ALU.max, apply_absolute_value=True)
            e.drain()
            if c == 0:
                return e.tensor_copy(mk.wacc[:, 0:1], mk.misc_pp[:, 3:4])
            return e.tensor_tensor(mk.wacc[:, 0:1], mk.wacc[:, 0:1], mk.misc_pp[:, 3:4], ALU.max)
        lm = mk.em("vector", [ld, bce], q1, sig=mk.vE)
        st["rd"] = lm
    tr = mk.em("tensor", [lm],
               lambda e: e.transpose(mk.pk[2][0:1, 0:128], mk.wacc[:, 0:1], mk.idf[:]),
               sig=mk.tE)
    mk.z2_done = mk.em("vector", [tr, mk.ar_zeroed],
                       lambda e: e.tensor_reduce(mk.rowf2[0:1, 0:1], mk.pk[2][0:1, 0:128],
                                                 AXX, ALU.max), sig=mk.vE)
    mk.quantpre_v = mk.vE.snap()


def ph_kvproj0(mk):
    xnv = mk.xnb[:, :].rearrange("p (c t) -> p c t", c=NCH)
    bar = mk.barrier()
    bvs = mk.dma("sync", mk.PF[0:1, 0:2048], mk.c_bvrow0.ap(), bar)
    cv0 = mk.em("vector", [bvs],
                lambda e: e.tensor_copy(mk.bvrow0(), mk.PF[0:1, 0:2048]), sig=mk.vE)
    bvs1 = mk.dma("sync", mk.PF[0:1, 2048:2560], mk.c_bvrow1.ap(), [cv0])
    cv1 = mk.em("vector", [bvs1],
                lambda e: e.tensor_copy(mk.bvrow1(), mk.PF[0:1, 2048:2560]), sig=mk.vE)
    mk.bvrows_ready = mk.vE.snap()

    wkp = mk.pb(0, 2048)
    kt_off = (20480, 22528)
    ksq = mk.pb(26624, 1024)
    g4 = [[0, 1, 2, 3], [4, 5, 6, 7]]
    g8 = [list(range(8))]
    wvg_full = mk.pb(4096, 16 * 512)
    wvg3 = wvg_full.rearrange("p (c o) -> p c o", c=16)
    vt_off = (24576, 25600)
    mk.cc_agk0g, mk.cc_agv0g = [], []
    st = {}
    for g in range(4):
        for h in range(4 * g, 4 * g + 4):
            jit = _wjit(mk, mk.wk0c, mk.sk0, h * 128, 128, wkp,
                        [st.get("wk_used")] + (bar + [mk.bvrows_ready] if h == 0 else []))
            ktile = mk.pb(kt_off[h % 2], 1024)

            def mfn(e, h=h):
                last = None
                for th in range(2):
                    for ic in range(16):
                        last = e.matmul(mk.pk[th][:], wkp[:, ic * 128:(ic + 1) * 128],
                                        xnv[:, ic, th * 512:(th + 1) * 512],
                                        start=(ic == 0), stop=(ic == 15))
                return last
            mm = mk.em("tensor", [jit, mk.xn_ready_s, st.get(("ev", h % 2))], mfn, sig=mk.tE)
            st["wk_used"] = mm

            def efn(e, h=h, ktile=ktile):
                e.activation(ktile[:, 0:512], mk.pk[0][:], AF.Identity,
                             bias=mk.bias0[:, 16 + h:17 + h])
                return e.activation(ktile[:, 512:1024], mk.pk[1][:], AF.Identity,
                                    bias=mk.bias0[:, 16 + h:17 + h])
            ev = mk.em("scalar", [mm, st.get(("dma", h % 2))], efn, sig=mk.sE)
            st[("ev", h % 2)] = ev
            sqv = mk.em("vector", [ev, st.get("knorm_mm")],
                        lambda e, ktile=ktile: e.tensor_tensor(ksq, ktile, ktile, ALU.mult),
                        sig=mk.vE)

            def nfn(e):
                e.matmul(mk.prow[6][:], mk.ones_col_b[:], ksq[:, 0:512], start=True, stop=True)
                return e.matmul(mk.prow[7][:], mk.ones_col_b[:], ksq[:, 512:1024],
                                start=True, stop=True)
            nm = mk.em("tensor", [sqv, st.get("knorm_rd")], nfn, sig=mk.tE)
            st["knorm_mm"] = nm

            def rfn(e, h=h):
                e.tensor_reduce(mk.rowf2[0:1, 1 + h:2 + h], mk.prow[6][:], AXX, ALU.max)
                e.tensor_reduce(mk.rowf2[0:1, 60:61], mk.prow[7][:], AXX, ALU.max)
                e.drain()
                return e.tensor_tensor(mk.rowf2[0:1, 1 + h:2 + h], mk.rowf2[0:1, 1 + h:2 + h],
                                       mk.rowf2[0:1, 60:61], ALU.max)
            st["knorm_rd"] = mk.em("vector", [nm], rfn, sig=mk.vE)
            st[("dma", h % 2)] = mk.dma("gpsimd", mk.kin0.ap()[h * 128:(h + 1) * 128, :],
                                        ktile, [ev])
            for _ in range(2):
                if mk.bg_tasks:
                    mk.bg_tasks.popleft()()
        kg_done = mk.dG.snap()
        mk.em("gpsimd", [kg_done],
              lambda e, g=g: e.collective_compute(
                  "AllGather", ALU.bypass, replica_groups=g4,
                  ins=[mk.kin0.ap()[g * 512:(g + 1) * 512, :].opt()],
                  outs=[mk.kout0g[g].ap().opt()]),
              sig=mk.cc)
        mk.cc_agk0g.append(mk.cc.snap())

        if g == 3:
            mk.knorms_done = mk.vE.snap()
            arld = mk.dma("gpsimd", mk.ar0_in.ap(), mk.rowf2[0:1, 0:32],
                          [mk.z2_done, mk.knorms_done])
            mk.em("gpsimd", [arld],
                  lambda e: e.collective_compute("AllReduce", ALU.max, replica_groups=g8,
                                                 ins=[mk.ar0_in.ap().opt()],
                                                 outs=[mk.ar0_out.ap().opt()]),
                  sig=mk.cc)
            mk.cc_ar0 = mk.cc.snap()

        jits = []
        for sf in range(4):
            dst = wvg3[:, :, sf * 128:(sf + 1) * 128]
            jits.append(_wjit(mk, mk.wv0c, mk.sv0, g * 512 + sf * 128, 128, dst,
                              [st.get("wv_used") or st.get("wk_used")]))
        for tt in range(8):
            def mfn(e, g=g, tt=tt):
                e.matmul(mk.pk[2 + tt % 2][:], mk.ones_row_b[:],
                         mk.bvrow0()[0:1, g * 512:(g + 1) * 512], start=True, stop=False)
                last = None
                for ic in range(16):
                    last = e.matmul(mk.pk[2 + tt % 2][:],
                                    xnv[:, ic, tt * 128:(tt + 1) * 128],
                                    wvg_full[:, ic * 512:(ic + 1) * 512],
                                    start=False, stop=(ic == 15))
                return last
            mm = mk.em("tensor", jits + [mk.bvrows_ready, st.get(("vev", tt % 2))],
                       mfn, sig=mk.tE)
            st["wv_used"] = mm
            vtile = mk.pb(vt_off[tt % 2], 512)
            ev = mk.em("scalar", [mm, st.get(("vdma", tt % 2))],
                       lambda e, vtile=vtile, tt=tt: e.activation(vtile, mk.pk[2 + tt % 2][:],
                                                                  AF.Copy),
                       sig=mk.sE)
            st[("vev", tt % 2)] = ev
            st[("vdma", tt % 2)] = mk.dma(
                "gpsimd", mk.vin0g[g].ap()[tt * 128:(tt + 1) * 128, :],
                vtile, [ev])
            for _ in range(2):
                if mk.bg_tasks:
                    mk.bg_tasks.popleft()()
        vg_done = mk.dG.snap()
        mk.em("gpsimd", [vg_done],
              lambda e, g=g: e.collective_compute(
                  "AllGather", ALU.bypass, replica_groups=g4,
                  ins=[mk.vin0g[g].ap().opt()], outs=[mk.vout0g[g].ap().opt()]),
              sig=mk.cc)
        mk.cc_agv0g.append(mk.cc.snap())
    mk.cc_agv0 = mk.cc_agv0g[3]
    mk.pool_entry_t = mk.tE.snap()
    mk.pool_entry_v = mk.vE.snap()
    mk.pool_entry_s = mk.sE.snap()


def ph_quantfin(mk):
    bar = mk.barrier()
    ld = mk.dma("sync", mk.rowf2[0:1, 0:32], mk.ar0_out.ap(), [mk.cc_ar0])
    mk.ar0_row_loaded = ld

    def v1(e):
        e.tensor_scalar(mk.rowf2[0:1, 40:41], mk.rowf2[0:1, 0:1], 1e-5, None, ALU.max)
        e.drain()
        e.reciprocal(mk.rowf2[0:1, 41:42], mk.rowf2[0:1, 40:41])
        e.drain()
        return e.tensor_scalar(mk.rowf2[0:1, 41:42], mk.rowf2[0:1, 41:42], 127.0, None, ALU.mult)
    r127 = mk.em("vector", [ld], v1, sig=mk.vE)
    bc = mk.em("tensor", [r127, mk.pool_entry_v, mk.pool_entry_s],
               lambda e: e.matmul(mk.pk[3][:, 0:1], mk.ones_row_f[:],
                                  mk.rowf2[0:1, 41:42], start=True, stop=True), sig=mk.tE)
    bce = mk.em("vector", [bc],
                lambda e: e.tensor_copy(mk.misc_pp[:, 2:3], mk.pk[3][:, 0:1]), sig=mk.vE)
    mk.quantfin_bce = bce
    gsc = mk.em("vector", [bce],
                lambda e: e.tensor_scalar(mk.gscaled[:, :], mk.gamma[:, :],
                                          mk.misc_pp[:, 2:3], None, ALU.mult), sig=mk.vE)
    irbc = mk.pf(8192, 1024)

    def b1(e):
        e.matmul(mk.pk[0][:], mk.ones_row_f[:], mk.rowf[0:1, 0:512], start=True, stop=True)
        return e.matmul(mk.pk[1][:], mk.ones_row_f[:], mk.rowf[0:1, 512:1024], start=True, stop=True)
    bc2 = mk.em("tensor", [mk.pool_entry_v, r127] + bar, b1, sig=mk.tE)

    def e1(e):
        e.tensor_copy(irbc[:, 0:512], mk.pk[0][:])
        return e.tensor_copy(irbc[:, 512:1024], mk.pk[1][:])
    bce2 = mk.em("vector", [bc2] + bar, e1, sig=mk.vE)

    st = {}
    z2s = [mk.pf(0, 1024), mk.pf(4096, 1024)]
    stgs = [mk.pb(47104, 1024), mk.pb(49152, 1024)]
    for c in range(16):
        z2r = z2s[c % 2]
        stg = stgs[c % 2]
        ld = mk.dma("gpsimd", z2r, mk.z2_sp.ap()[c * 128:(c + 1) * 128, :],
                    [st.get(("q", c % 2))] + (bar if c < 2 else []))

        def qfn(e, c=c, z2r=z2r, stg=stg):
            e.scalar_tensor_tensor(z2r, z2r, mk.gscaled[:, c:c + 1], irbc, ALU.mult, ALU.mult)
            e.drain()
            e.tensor_scalar(z2r, z2r, 127.0, -128.0, ALU.min, ALU.max)
            e.drain()
            return e.tensor_scalar(stg, z2r, MAGIC, -MAGIC, ALU.add, ALU.add)
        qd = mk.em("vector", [ld, gsc, bce2, st.get(("dma", c % 2)),
                              mk.pool_entry_t if c < 2 else None], qfn, sig=mk.vE)
        st[("q", c % 2)] = qd
        st[("dma", c % 2)] = mk.dma("gpsimd", mk.xq_sp.ap()[c * 128:(c + 1) * 128, :], stg, [qd])
    mk.xq_spilled = mk.dG.snap()
    mk.quantfin_v = mk.vE.snap()


def ph_wq0prep_tasks(mk):
    bar = mk.barrier()
    aslot = [mk.pf(8192, 512), mk.pf(10240, 512)]
    bslot = [mk.pb(45056, 512), mk.pb(46080, 512)]
    wslot = [mk.pb(36864, 2048), mk.pb(40960, 2048)]
    st = mk._wq_st = {}

    def task(ic, q):
        def run(ic=ic, q=q):
            r = q % 2
            first = bar if (ic == 0 and q < 2) else []
            la = mk.dma("sync", aslot[r],
                        mk.wq0c.ap()[ic * 128:(ic + 1) * 128, q * 512:(q + 1) * 512],
                        [st.get(("tt", r))] + first)
            lb = mk.dma("gpsimd", bslot[r],
                        mk.sq0.ap()[ic * 128:(ic + 1) * 128, q * 512:(q + 1) * 512],
                        [st.get(("tt", r))] + first)
            w = wslot[ic % 2]
            tt = mk.em("vector", [la, lb, st.get(("wsp", ic % 2)) if q == 0 else None],
                       lambda e, w=w, q=q, r=r: e.tensor_tensor(
                           w[:, q * 512:(q + 1) * 512], aslot[r], bslot[r], ALU.mult),
                       sig=mk.vE)
            st[("tt", r)] = tt
            if q == 3:
                st[("wsp", ic % 2)] = mk.dma(
                    "gpsimd", mk.wq0b.ap()[ic * 128:(ic + 1) * 128, :], w, [tt])
        return run

    for ic in range(16):
        for q in range(4):
            mk.bg_tasks.append(task(ic, q))


def ph_passA_tasks(mk):
    bar = mk.barrier()
    st = mk._pa_st = {}

    def task(it):
        def run(it=it):
            r = it % 2
            a = mk.pf(4096 + 2048 * r, 512)
            la = mk.dma("sync", a,
                        mk.w0T.ap()[(it // 4) * 128:(it // 4 + 1) * 128,
                                    (it % 4) * 512:(it % 4 + 1) * 512],
                        [st.get(("rd", r))] + (bar if it < 2 else []))

            def rfn(e, it=it, a=a):
                e.tensor_reduce(mk.misc_pp[:, 3:4], a, AXX, ALU.add,
                                apply_absolute_value=True)
                e.drain()
                if it == 0:
                    return e.tensor_copy(mk.wacc[:, 1:2], mk.misc_pp[:, 3:4])
                return e.tensor_tensor(mk.wacc[:, 1:2], mk.wacc[:, 1:2],
                                       mk.misc_pp[:, 3:4], ALU.add)
            st[("rd", r)] = mk.em("vector", [la, st.get("acc")], rfn, sig=mk.vE)
            st["acc"] = st[("rd", r)]
        return run

    return [task(it) for it in range(64)]


def ph_ffn0passA(mk):
    st = mk._pa_st
    tr = mk.em("tensor", [st["acc"], mk.quantfin_bce],
               lambda e: e.transpose(mk.pk[3][0:1, 0:128], mk.wacc[:, 1:2], mk.idf[:]),
               sig=mk.tE)

    def v2(e):
        e.tensor_reduce(mk.rowf2[0:1, 42:43], mk.pk[3][0:1, 0:128], AXX, ALU.add)
        e.drain()
        e.tensor_scalar(mk.rowf2[0:1, 42:43], mk.rowf2[0:1, 42:43], 1.0 / (2048.0 * 2048.0),
                        None, ALU.mult)
        e.drain()
        e.tensor_scalar(mk.rowf2[0:1, 42:43], mk.rowf2[0:1, 42:43], 1e-5, None, ALU.max)
        e.drain()
        e.reciprocal(mk.rowf2[0:1, 43:44], mk.rowf2[0:1, 42:43])
        e.tensor_tensor(mk.rowf2[0:1, 44:45], mk.rowf2[0:1, 40:41], mk.rowf2[0:1, 42:43], ALU.mult)
        e.drain()
        return e.tensor_scalar(mk.rowf2[0:1, 44:45], mk.rowf2[0:1, 44:45], 1.0 / 127.0,
                               None, ALU.mult)
    sc = mk.em("vector", [tr], v2, sig=mk.vE)

    def b1(e):
        e.matmul(mk.pk[3][:, 0:1], mk.ones_row_f[:], mk.rowf2[0:1, 43:44], start=True, stop=True)
        return e.matmul(mk.pk[3][:, 1:2], mk.ones_row_f[:], mk.rowf2[0:1, 44:45], start=True, stop=True)
    bc = mk.em("tensor", [sc], b1, sig=mk.tE)

    def e1(e):
        e.tensor_copy(mk.misc_pp[:, 0:1], mk.pk[3][:, 0:1])
        return e.tensor_copy(mk.misc_pp[:, 1:2], mk.pk[3][:, 1:2])
    mk.ffn0_scales_ready = mk.em("vector", [bc], e1, sig=mk.vE)


def ph_attn(mk, layer):
    xnv = mk.xnb[:, :].rearrange("p (c t) -> p c t", c=NCH)
    bar = mk.barrier()
    if layer == 0:
        bar = [mk.wq0b_done, mk.pool_entry_t, mk.xn_ready_s]
    ksb = [mk.pb(0, 4096), mk.pb(8192, 4096)]
    vsb = [mk.pb(16384, 4096), mk.pb(24576, 4096)]
    P = [mk.pb(32768 + 1024 * i, 512) for i in range(4)]
    qsb = [mk.pb(36864, 1024), mk.pb(38912, 1024)]
    qw = [mk.pb(40960, 2048), mk.pb(47104, 2048)]
    qsq = mk.pb(45056, 1024)
    sbc = mk.pf(0, 1024)

    if layer == 0:
        kv_of = lambda h: h
        kviews = [t.ap().rearrange("(r h p) k -> r h p k", r=4, h=4) for t in mk.kout0g]
        vviews = [t.ap().rearrange("(r t p) f -> r p t f", r=4, t=8) for t in mk.vout0g]
        kview_of = lambda h: kviews[h // 4][:, h % 4]
        vview_of = lambda h: vviews[h // 4][:, :, :, (h % 4) * 128:(h % 4 + 1) * 128]
        ccK_of = lambda h: mk.cc_agk0g[h // 4]
        ccV_of = lambda h: mk.cc_agv0g[h // 4]
        kk_slot = lambda h: 1 + h
        bias_q = lambda h: mk.bias0[:, h:h + 1]
        ar_ld = mk.ar0_row_loaded
    else:
        kv_of = lambda h: h // 4
        kview1 = mk.kout1.ap().rearrange("(r h p) k -> r h p k", r=4, h=4)
        vview1 = mk.vout1.ap().rearrange("(r t p) f -> r p t f", r=4, t=8)
        kview_of = lambda h: kview1[:, h // 4]
        vview_of = lambda h: vview1[:, :, :, (h // 4) * 128:(h // 4 + 1) * 128]
        ccK_of = lambda h: mk.cc_agk1
        ccV_of = lambda h: mk.cc_agv1
        kk_slot = lambda h: 32 + h // 4
        bias_q = lambda h: mk.bias1[:, h:h + 1]
        ar_ld = mk.ar1_row_loaded

    st = {}

    def prep_loads(h):
        kvh = kv_of(h)
        kb = (h % 2) if layer == 0 else (kvh % 2)
        new_kv = (layer == 0) or (h % 4 == 0)
        if new_kv:
            entry = [] if ("kfree", kb) in st else bar
            kview = kview_of(h)
            vview = vview_of(h)
            kload = None
            for r in range(4):
                kload = mk.dma("sync", ksb[kb][:, r * 1024:(r + 1) * 1024],
                               kview[r], [ccK_of(h), st.get(("kfree", kb))] + entry)
            vload = None
            for r in range(4):
                dst = vsb[kb][:, r * 1024:(r + 1) * 1024].rearrange("p (t d) -> p t d", t=8)
                vload = mk.dma("sync", dst, vview[r],
                               [ccV_of(h), st.get(("vfree", kb))] + entry)
            st[("kload", kb)] = kload
            st[("vload", kb)] = vload
        qsl = lambda w: w.ap().rearrange("(c p) o -> p c o", p=128)[:, :, h * 128:(h + 1) * 128]
        entry = [] if h >= 2 else bar
        qfree = st.get(("qw_used", h % 2))
        if layer == 0:
            st[("qwl", h % 2)] = mk.dma("sync", qw[h % 2], qsl(mk.wq0b),
                                        [mk.wq0b_done, qfree,
                                         mk.xq_spilled if h == 1 else None] + entry)
        else:
            st[("qwl", h % 2)] = mk.dma("gpsimd", qw[h % 2], qsl(mk.wq1), [qfree] + entry)
            mk.flush("gpsimd")
        mk.flush("sync")

    def prep_qproj(h):
        def qfn(e, h=h):
            last = None
            for th in range(2):
                for ic in range(16):
                    last = e.matmul(mk.pk[6 + th][:], qw[h % 2][:, ic * 128:(ic + 1) * 128],
                                    xnv[:, ic, th * 512:(th + 1) * 512],
                                    start=(ic == 0), stop=(ic == 15))
            return last
        qmm = mk.em("tensor", [st[("qwl", h % 2)], st.get("qq_read"),
                               st.get("sbe_tok")], qfn, sig=mk.tE)
        st[("qw_used", h % 2)] = qmm
        st["qmm"] = qmm

    def prep_qev(h):
        def qev(e, h=h):
            e.activation(qsb[h % 2][:, 0:512], mk.pk[6][:], AF.Identity, bias=bias_q(h))
            return e.activation(qsb[h % 2][:, 512:1024], mk.pk[7][:], AF.Identity,
                                bias=bias_q(h))
        st[("qe", h % 2)] = mk.em("scalar", [st["qmm"], st.get(("qfree", h % 2))],
                                  qev, sig=mk.sE)

    def prep_bound(h):
        qe = st[("qe", h % 2)]
        qq1 = mk.em("vector", [qe, st.get("qsq_read")],
                    lambda e, h=h: e.tensor_tensor(qsq, qsb[h % 2], qsb[h % 2], ALU.mult),
                    sig=mk.vE)

        def qqm(e):
            e.matmul(mk.prow[6][:], mk.ones_col_b[:], qsq[:, 0:512], start=True, stop=True)
            return e.matmul(mk.prow[7][:], mk.ones_col_b[:], qsq[:, 512:1024], start=True, stop=True)
        qq2 = mk.em("tensor", [qq1, qe], qqm, sig=mk.tE)
        st["qsq_read"] = qq2

        def qq3(e, h=h):
            e.tensor_reduce(mk.rowf2[0:1, 45:46], mk.prow[6][:], AXX, ALU.max)
            e.tensor_reduce(mk.rowf2[0:1, 46:47], mk.prow[7][:], AXX, ALU.max)
            e.drain()
            e.tensor_tensor(mk.rowf2[0:1, 45:46], mk.rowf2[0:1, 45:46],
                            mk.rowf2[0:1, 46:47], ALU.max)
            e.drain()
            e.tensor_tensor(mk.rowf2[0:1, 45:46], mk.rowf2[0:1, 45:46],
                            mk.rowf2[0:1, kk_slot(h):kk_slot(h) + 1], ALU.add)
            e.drain()
            return e.tensor_scalar(mk.rowf2[0:1, 47:48], mk.rowf2[0:1, 45:46],
                                   NEG_HALF_ISH, None, ALU.mult)
        qq4 = mk.em("vector", [qq2, ar_ld], qq3, sig=mk.vE)
        nb1 = mk.em("tensor", [qq4],
                    lambda e: e.matmul(mk.pk[6][:, 0:1], mk.ones_row_f[:],
                                       mk.rowf2[0:1, 47:48], start=True, stop=True), sig=mk.tE)
        nb2 = mk.em("vector", [nb1],
                    lambda e, h=h: e.tensor_copy(mk.negb[:, h:h + 1], mk.pk[6][:, 0:1]),
                    sig=mk.vE)
        st["qq_read"] = nb2
        st[("nb", h % 2)] = nb2

    prep_loads(0)
    prep_qproj(0)
    prep_qev(0)
    prep_bound(0)

    for h in range(NH):
        kvh = kv_of(h)
        kb = (h % 2) if layer == 0 else (kvh % 2)
        kload, vload = st[("kload", kb)], st[("vload", kb)]
        qe, nb2 = st[("qe", h % 2)], st[("nb", h % 2)]

        s_snap, e_snap, p_snap = {}, {}, {}
        srow = [mk.pk[5][0:1, :], mk.pk[5][32:33, :]]

        def emit_S(i):
            kt, qh = i // 2, i % 2

            def sfn(e, kt=kt, qh=qh, kb=kb, h=h, i=i):
                return e.matmul(mk.pk[i % 3][:], ksb[kb][:, kt * 128:(kt + 1) * 128],
                                qsb[h % 2][:, qh * 512:(qh + 1) * 512],
                                start=True, stop=True)
            sw = [e_snap.get(i - 3)]
            if i < 3:
                sw += [kload, qe, st.get(("exp_tail", i))]
            s_snap[i] = mk.em("tensor", sw, sfn, sig=mk.tE)

        def emit_exp(i):
            def efn(e, i=i, h=h):
                return e.activation(P[i % 4][:], mk.pk[i % 3][:], AF.Exp,
                                    bias=mk.negb[:, h:h + 1], scale=ISH)
            ew = [s_snap[i], p_snap.get(i - 4)]
            if i == 0:
                ew.append(nb2)
            e_snap[i] = mk.em("scalar", ew, efn, sig=mk.sE)

        def emit_PV(i):
            kt, qh = i // 2, i % 2

            def pfn(e, kt=kt, qh=qh, i=i, kb=kb):
                e.matmul(mk.pk[3 + qh][:], vsb[kb][:, kt * 128:(kt + 1) * 128],
                         P[i % 4][:], start=(kt == 0), stop=(kt == 31))
                return e.matmul(srow[qh], mk.ones_col_b[:],
                                P[i % 4][:], start=(kt == 0), stop=(kt == 31))
            pw = [e_snap[i]]
            if kt == 0:
                pw += [vload, st.get("pv_free"), st.get("sums_free")]
            p_snap[i] = mk.em("tensor", pw, pfn, sig=mk.tE)

        emit_S(0)
        emit_exp(0)
        for i in range(1, 64):
            emit_S(i)
            emit_exp(i)
            emit_PV(i - 1)

            if h + 1 < NH:
                if i == 16:
                    prep_loads(h + 1)
                elif i == 24:
                    prep_qproj(h + 1)
                elif i == 28:
                    prep_qev(h + 1)
                elif i == 32:
                    prep_bound(h + 1)
            if h >= 2 and i in (8, 20, 36, 52) and mk.bg_tasks:
                mk.bg_tasks.popleft()()
        emit_PV(63)

        def sg1(e):
            e.tensor_copy(mk.rowf[0:1, 1024:1536], srow[0])
            e.tensor_copy(mk.rowf[0:1, 1536:2048], srow[1])
            e.drain()
            return e.reciprocal(mk.rowf[0:1, 1024:2048], mk.rowf[0:1, 1024:2048])
        sg = mk.em("vector", [p_snap[63]], sg1, sig=mk.vE)
        st["sums_free"] = sg

        def sb1(e):
            e.matmul(mk.pk[6][:], mk.ones_row_f[:], mk.rowf[0:1, 1024:1536],
                     start=True, stop=True)
            return e.matmul(mk.pk[7][:], mk.ones_row_f[:], mk.rowf[0:1, 1536:2048],
                            start=True, stop=True)
        sbm = mk.em("tensor", [sg], sb1, sig=mk.tE)

        def sb2(e):
            e.tensor_copy(sbc[:, 0:512], mk.pk[6][:])
            return e.tensor_copy(sbc[:, 512:1024], mk.pk[7][:])
        sbe = mk.em("vector", [sbm], sb2, sig=mk.vE)
        st["sbe_tok"] = sbe

        def nrm(e, h=h):
            e.tensor_tensor(mk.att[:, h * TPC:h * TPC + 512], mk.pk[3][:],
                            sbc[:, 0:512], ALU.mult)
            return e.tensor_tensor(mk.att[:, h * TPC + 512:(h + 1) * TPC], mk.pk[4][:],
                                   sbc[:, 512:1024], ALU.mult)
        nr = mk.em("vector", [sbe], nrm, sig=mk.vE)
        st["pv_free"] = nr
        st[("kfree", kb)] = s_snap[63]
        st[("vfree", kb)] = p_snap[63]
        st[("qfree", h % 2)] = s_snap[63]
        st[("exp_tail", 0)] = e_snap[63]
        st[("exp_tail", 1)] = e_snap[61]
        st[("exp_tail", 2)] = e_snap[62]
    mk.attn_done = mk.vE.snap()


def ph_outproj(mk, layer):
    attv = mk.att[:, :].rearrange("p (c t) -> p c t", c=NCH)
    bar = mk.barrier()
    xv = mk.x[:, :].rearrange("p (c t) -> p c t", c=NCH)
    wo_sb = mk.pb(0, 2048)
    st = {}
    for oc in range(16):
        if layer == 0:
            jit = _wjit(mk, mk.wo0c, mk.so0, oc * 128, 128, wo_sb,
                        [st.get("wo_used")] + (bar if oc == 0 else []))
        else:
            wsl = mk.wo1.ap().rearrange("(c p) o -> p c o", p=128)[:, :, oc * 128:(oc + 1) * 128]
            jit = mk.dma("gpsimd", wo_sb, wsl,
                         [st.get("wo_used")] + (bar if oc == 0 else []))

        def mfn(e):
            last = None
            for th in range(2):
                for ic in range(16):
                    last = e.matmul(mk.pk[th][:], wo_sb[:, ic * 128:(ic + 1) * 128],
                                    attv[:, ic, th * 512:(th + 1) * 512],
                                    start=(ic == 0), stop=(ic == 15))
            return last
        mm = mk.em("tensor", [jit, st.get("stt"),
                              mk.attn_done if oc == 0 else None], mfn, sig=mk.tE)
        st["wo_used"] = mm
        bcol = (mk.bias0[:, 48 + oc:49 + oc] if layer == 0
                else mk.bias1[:, 32 + oc:33 + oc])

        def afn(e, oc=oc, bcol=bcol):
            e.scalar_tensor_tensor(xv[:, oc, 0:512], mk.pk[0][:], bcol,
                                   xv[:, oc, 0:512], ALU.add, ALU.add)
            return e.scalar_tensor_tensor(xv[:, oc, 512:1024], mk.pk[1][:], bcol,
                                          xv[:, oc, 512:1024], ALU.add, ALU.add)
        st["stt"] = mk.em("vector", [mm], afn, sig=mk.vE)
    mk.x_after_attn = mk.vE.snap()
    mk.outproj_t = mk.tE.snap()


def ph_ffn0mm(mk):
    xv = mk.x[:, :].rearrange("p (c t) -> p c t", c=NCH)
    bar = mk.barrier()
    xqv = mk.xq_sp.ap().rearrange("(c p) t -> p c t", p=128)
    xnb3 = mk.xnb[:, :].rearrange("p (c t) -> p c t", c=NCH)
    xqld = mk.dma("sync", xnb3, xqv, [mk.xq_spilled, mk.outproj_t])
    st = {}
    w0qb_done = mk.w0qb_done
    wv = mk.w0qb.ap().rearrange("(c p) o -> p c o", p=128)
    for otp in range(8):
        wq_sb = mk.pb(8192 + (otp % 2) * 8192, 16 * 256)
        ld = mk.dma("sync", wq_sb, wv[:, :, otp * 256:(otp + 1) * 256],
                    [w0qb_done, st.get(("used", otp % 2))] + (bar if otp == 0 else []))
        for oc2 in range(2):
            oc = otp * 2 + oc2

            def mfn(e, wq_sb=wq_sb, oc2=oc2):
                last = None
                for th in range(2):
                    for ic in range(16):
                        last = e.matmul(mk.pk[th][:],
                                        wq_sb[:, ic * 256 + oc2 * 128:ic * 256 + oc2 * 128 + 128],
                                        xnb3[:, ic, th * 512:(th + 1) * 512],
                                        start=(ic == 0), stop=(ic == 15))
                return last
            mm = mk.em("tensor", [ld, xqld, st.get("stt2")], mfn, sig=mk.tE)
            st[("used", otp % 2)] = mm

            def afn(e, oc=oc):
                e.scalar_tensor_tensor(xv[:, oc, 0:512], mk.pk[0][:], mk.misc_pp[:, 1:2],
                                       xv[:, oc, 0:512], ALU.mult, ALU.add)
                return e.scalar_tensor_tensor(xv[:, oc, 512:1024], mk.pk[1][:],
                                              mk.misc_pp[:, 1:2],
                                              xv[:, oc, 512:1024], ALU.mult, ALU.add)
            st["stt2"] = mk.em("vector", [mm], afn, sig=mk.vE)
    mk.ln1_start = mk.vE.snap()
    mk.ln1_start_t = mk.tE.snap()


def ph_kvproj1(mk):
    xnv = mk.xnb[:, :].rearrange("p (c t) -> p c t", c=NCH)
    bar = mk.barrier()
    t0 = mk.tE.snap()
    v0 = mk.vE.snap()
    wk_sb = mk.pb(0, 16 * 512)
    wv_sb = mk.pb(16384, 16 * 512)
    wkl = mk.dma("gpsimd", wk_sb,
                 mk.wk1.ap().rearrange("(c p) o -> p c o", p=128), bar)
    wvl = mk.dma("gpsimd", wv_sb,
                 mk.wv1.ap().rearrange("(c p) o -> p c o", p=128), bar)
    bvs1 = mk.dma("sync", mk.PF[0:1, 0:512], mk.c_bvrow1.ap(), bar)
    cv1 = mk.em("vector", [bvs1],
                lambda e: e.tensor_copy(mk.PB[0:1, 19968:20480], mk.PF[0:1, 0:512]), sig=mk.vE)
    zr = mk.em("vector", [], lambda e: e.memset(mk.rowf2[0:1, 48:56], 0.0), sig=mk.vE)
    ksq = mk.pb(38912, 1024) if False else mk.pb(40960, 1024)
    kt_off = (32768, 34816)
    st = {}
    for h in range(4):
        ktile = mk.pb(kt_off[h % 2], 1024)

        def mfn(e, h=h):
            last = None
            for th in range(2):
                for ic in range(16):
                    last = e.matmul(mk.pk[th][:],
                                    wk_sb[:, ic * 512 + h * 128:ic * 512 + h * 128 + 128],
                                    xnv[:, ic, th * 512:(th + 1) * 512],
                                    start=(ic == 0), stop=(ic == 15))
            return last
        mm = mk.em("tensor", [wkl, mk.xn_ready_v, st.get(("ev", h % 2))], mfn, sig=mk.tE)

        def efn(e, h=h, ktile=ktile):
            e.activation(ktile[:, 0:512], mk.pk[0][:], AF.Identity, bias=mk.bias1[:, 16 + h:17 + h])
            return e.activation(ktile[:, 512:1024], mk.pk[1][:], AF.Identity,
                                bias=mk.bias1[:, 16 + h:17 + h])
        ev = mk.em("scalar", [mm, st.get(("dma", h % 2))], efn, sig=mk.sE)
        st[("ev", h % 2)] = ev
        sqv = mk.em("vector", [ev, st.get("knorm_mm"), zr],
                    lambda e, ktile=ktile: e.tensor_tensor(ksq, ktile, ktile, ALU.mult),
                    sig=mk.vE)

        def nfn(e):
            e.matmul(mk.prow[6][:], mk.ones_col_b[:], ksq[:, 0:512], start=True, stop=True)
            return e.matmul(mk.prow[7][:], mk.ones_col_b[:], ksq[:, 512:1024], start=True, stop=True)
        nm = mk.em("tensor", [sqv, st.get("knorm_rd")], nfn, sig=mk.tE)
        st["knorm_mm"] = nm

        def rfn(e, h=h):
            e.tensor_reduce(mk.rowf2[0:1, 48 + h:49 + h], mk.prow[6][:], AXX, ALU.max)
            e.tensor_reduce(mk.rowf2[0:1, 60:61], mk.prow[7][:], AXX, ALU.max)
            e.drain()
            return e.tensor_tensor(mk.rowf2[0:1, 48 + h:49 + h], mk.rowf2[0:1, 48 + h:49 + h],
                                   mk.rowf2[0:1, 60:61], ALU.max)
        st["knorm_rd"] = mk.em("vector", [nm], rfn, sig=mk.vE)
        st[("dma", h % 2)] = mk.dma("gpsimd", mk.kin1.ap()[h * 128:(h + 1) * 128, :],
                                    ktile, [ev])
    mk.kin1_done = mk.dG.snap()
    mk.knorms1_done = mk.vE.snap()
    g4 = [[0, 1, 2, 3], [4, 5, 6, 7]]
    g8 = [list(range(8))]
    mk.em("gpsimd", [mk.kin1_done],
          lambda e: e.collective_compute("AllGather", ALU.bypass, replica_groups=g4,
                                         ins=[mk.kin1.ap().opt()], outs=[mk.kout1.ap().opt()]),
          sig=mk.cc)
    mk.cc_agk1 = mk.cc.snap()
    arld = mk.dma("gpsimd", mk.ar1_in.ap(), mk.rowf2[0:1, 48:56], [mk.knorms1_done])
    mk.em("gpsimd", [arld],
          lambda e: e.collective_compute("AllReduce", ALU.max, replica_groups=g8,
                                         ins=[mk.ar1_in.ap().opt()], outs=[mk.ar1_out.ap().opt()]),
          sig=mk.cc)
    mk.cc_ar1 = mk.cc.snap()

    vt_off = (36864, 37888)
    for tt in range(8):
        def vfn(e, tt=tt):
            e.matmul(mk.pk[2 + tt % 2][:], mk.ones_row_b[:], mk.PB[0:1, 19968:20480],
                     start=True, stop=False)
            last = None
            for ic in range(16):
                last = e.matmul(mk.pk[2 + tt % 2][:],
                                xnv[:, ic, tt * 128:(tt + 1) * 128],
                                wv_sb[:, ic * 512:(ic + 1) * 512],
                                start=False, stop=(ic == 15))
            return last
        mm = mk.em("tensor", [wvl, cv1, st.get(("vev", tt % 2))], vfn, sig=mk.tE)
        vtile = mk.pb(vt_off[tt % 2], 512)
        ev = mk.em("scalar", [mm, st.get(("vdma", tt % 2))],
                   lambda e, vtile=vtile, tt=tt: e.activation(vtile, mk.pk[2 + tt % 2][:], AF.Copy),
                   sig=mk.sE)
        st[("vev", tt % 2)] = ev
        st[("vdma", tt % 2)] = mk.dma("gpsimd", mk.vin1.ap()[tt * 128:(tt + 1) * 128, :],
                                      vtile, [ev])
    mk.vin1_done = mk.dG.snap()
    mk.em("gpsimd", [mk.vin1_done],
          lambda e: e.collective_compute("AllGather", ALU.bypass, replica_groups=g4,
                                         ins=[mk.vin1.ap().opt()], outs=[mk.vout1.ap().opt()]),
          sig=mk.cc)
    mk.cc_agv1 = mk.cc.snap()
    mk.ar1_row_loaded = mk.dma("sync", mk.rowf2[0:1, 32:40], mk.ar1_out.ap(), [mk.cc_ar1])
    mk.pool_entry_t = mk.tE.snap()
    mk.pool_entry_v = mk.vE.snap()


def ph_ffn1(mk):
    xnv = mk.xnb[:, :].rearrange("p (c t) -> p c t", c=NCH)
    bar = mk.barrier()
    xv = mk.x[:, :].rearrange("p (c t) -> p c t", c=NCH)
    w1v = mk.w1T.ap().rearrange("(c p) o -> p c o", p=128)
    w3v = mk.w3T.ap().rearrange("(c p) o -> p c o", p=128)
    w1bv = mk.w1b.ap().rearrange("(c p) o -> p c o", p=128)
    w3bv = mk.w3b.ap().rearrange("(c p) o -> p c o", p=128)
    w2v = mk.w2T.ap().rearrange("(c p) o -> p c o", p=128)
    w2bv = mk.w2b.ap().rearrange("(c p) o -> p c o", p=128)
    w1t = [mk.pb(0, 2048), mk.pb(4096, 2048)]
    w3t = [mk.pb(8192, 2048), mk.pb(12288, 2048)]
    t1 = mk.pb(16384, 512)
    hpre = mk.pb(17408, 512)
    tmpt = mk.pb(18432, 512)
    f1_t0 = mk.tE.snap()
    f1_v0 = mk.vE.snap()
    last_dma = None
    bf_tmps = [mk.pf(4096 + 2048 * j, 512) for j in range(4)]
    for p in range(2):
        st = {}
        bf_applied = {}
        bf_state = {"last": None}

        def bf_ready(limit, bf_applied=bf_applied):
            out = []
            touched = set()
            for c in sorted(bf_applied):
                if len(out) >= limit:
                    break
                k = bf_applied[c]
                if k >= 6 or c in touched:
                    continue
                pp = c ^ (1 << k)
                if pp > c and pp not in touched and bf_applied.get(pp) == k:
                    out.append((c, pp, k))
                    touched.add(c)
                    touched.add(pp)
            for c, pp, k in out:
                bf_applied[c] += 1
                bf_applied[pp] += 1
            return out

        def bf_emit(limit, st=st, bf_applied=bf_applied, bf_state=bf_state):
            while True:
                batch = bf_ready(min(limit, 4))
                if not batch:
                    return
                waits = [st[("hevc", c)] for c, pp, k in batch if k == 0] +                         [st[("hevc", pp)] for c, pp, k in batch if k == 0]

                def bfb(e, batch=batch):
                    e.drain()
                    for j, (aa, bb, k) in enumerate(batch):
                        e.tensor_tensor(bf_tmps[j], mk.hid(aa), mk.hid(bb), ALU.subtract)
                        e.tensor_tensor(mk.hid(aa), mk.hid(aa), mk.hid(bb), ALU.add)
                    e.drain()
                    last = None
                    for j, (aa, bb, k) in enumerate(batch):
                        last = e.tensor_copy(mk.hid(bb), bf_tmps[j])
                    return last
                bf_state["last"] = mk.em("vector", waits, bfb, sig=mk.vE)
                limit -= len(batch)
                if limit <= 0:
                    return

        def emit_h128(ht2, st=st):
            r2 = ht2 % 2
            pk_h2 = mk.pk[2] if r2 == 0 else mk.pk[5]
            hm = mk.em("tensor", [st["hp"], st.get(("hev", r2))],
                       lambda e, pk_h2=pk_h2: e.matmul(pk_h2[:], mk.h128[:], hpre[:],
                                                       start=True, stop=True),
                       sig=mk.tE)
            st["h128_last"] = hm
            hev = mk.em("scalar", [hm],
                        lambda e, ht2=ht2, pk_h2=pk_h2: e.activation(mk.hid(ht2), pk_h2[:],
                                                                     AF.Copy),
                        sig=mk.sE)
            st[("hev", r2)] = hev
            st[("hevc", ht2)] = hev
            bf_applied[ht2] = 0

        for ht in range(HCH):
            r = ht % 2
            if ht < 2:
                first = bar + [mk.x_after_attn] if p == 0 else [f1_t0, f1_v0]
            else:
                first = []
            w1l = mk.dma("gpsimd", w1t[r], w1bv[:, :, ht * 128:(ht + 1) * 128],
                         [st.get(("ug", r))] + first)
            w3l = mk.dma("sync", w3t[r], w3bv[:, :, ht * 128:(ht + 1) * 128],
                         [st.get(("ul", r))] + first)

            pk_ug, pk_ul, pk_h = (mk.pk[0], mk.pk[1], mk.pk[2]) if r == 0 else \
                                 (mk.pk[3], mk.pk[4], mk.pk[5])

            def ugf(e, r=r, p=p, pk_ug=pk_ug):
                last = None
                for ic in range(16):
                    last = e.matmul(pk_ug[:], w1t[r][:, ic * 128:(ic + 1) * 128],
                                    xnv[:, ic, p * 512:(p + 1) * 512],
                                    start=(ic == 0), stop=(ic == 15))
                return last
            ug = mk.em("tensor", [w1l, st.get(("silu", r))], ugf, sig=mk.tE)
            st[("ug", r)] = ug

            def ulf(e, r=r, p=p, pk_ul=pk_ul):
                last = None
                for ic in range(16):
                    last = e.matmul(pk_ul[:], w3t[r][:, ic * 128:(ic + 1) * 128],
                                    xnv[:, ic, p * 512:(p + 1) * 512],
                                    start=(ic == 0), stop=(ic == 15))
                return last
            ul = mk.em("tensor", [w3l, st.get(("hpre", r))], ulf, sig=mk.tE)
            st[("ul", r)] = ul

            if ht > 0:
                emit_h128(ht - 1)

            silu = mk.em("scalar", [ug, st.get("hp")],
                         lambda e, pk_ug=pk_ug: e.activation(t1[:], pk_ug[:], AF.Silu),
                         sig=mk.sE)
            st[("silu", r)] = silu
            hp = mk.em("vector", [silu, ul, st.get("h128_last")],
                       lambda e, pk_ul=pk_ul: e.tensor_tensor(hpre[:], t1[:], pk_ul[:], ALU.mult),
                       sig=mk.vE)
            st[("hpre", r)] = hp
            st["hp"] = hp
            bf_emit(4)
        emit_h128(HCH - 1)
        bf_emit(10 ** 9)
        hid_fwht = bf_state["last"]

        w2t = [mk.pb(0, 4096), mk.pb(8192, 4096)]
        osum = mk.pf(0, 512)
        ott = mk.pf(2048 // 4, 512)
        st2 = {}
        for ot in range(NCH):
            mms = None
            for half in range(2):
                r = half
                first = [f1_t0, hid_fwht] if ot == 0 else []
                w2l = mk.dma("gpsimd" if half == 0 else "sync", w2t[r],
                             w2bv[:, half * 32:(half + 1) * 32, ot * 128:(ot + 1) * 128],
                             [st2.get(("mm", r))] + first)

                def mfn(e, half=half, r=r, ot=ot):
                    last = None
                    for hc in range(32):
                        last = e.matmul(mk.pk[6][:], w2t[r][:, hc * 128:(hc + 1) * 128],
                                        mk.hid(half * 32 + hc),
                                        start=(half == 0 and hc == 0),
                                        stop=(half == 1 and hc == 31))
                    return last
                mms = mk.em("tensor", [w2l, hid_fwht if ot == 0 else None,
                                       st2.get("stt") if half == 0 else None], mfn,
                            sig=mk.tE)
                st2[("mm", r)] = mms

            def sfn(e, ot=ot, p=p):
                return e.scalar_tensor_tensor(osum[:], mk.pk[6][:], INV_SQRT_HFF,
                                              xv[:, ot, p * 512:(p + 1) * 512],
                                              ALU.mult, ALU.add)
            stt = mk.em("vector", [mms, st2.get("oev")], sfn, sig=mk.vE)
            st2["stt"] = stt

            def tfn(e):
                last = None
                for tt in range(4):
                    last = e.transpose(mk.pk[7][:, tt * 128:(tt + 1) * 128],
                                       osum[:, tt * 128:(tt + 1) * 128], mk.idf[:])
                return last
            tr = mk.em("tensor", [stt, st2.get("oev")], tfn, sig=mk.tE)

            oev = mk.em("vector", [tr, st2.get("odma")],
                        lambda e: e.tensor_copy(ott[:], mk.pk[7][:]), sig=mk.vE)
            st2["oev"] = oev
            odst = mk.out.ap().rearrange("(t p) d -> p t d", p=128)[
                :, p * 4:(p + 1) * 4, ot * 128:(ot + 1) * 128]
            st2["odma"] = mk.dma("gpsimd", odst,
                                 ott[:].rearrange("p (t d) -> p t d", t=4), [oev])
        last_dma = st2["odma"]
        f1_t0 = mk.tE.snap()
        f1_v0 = mk.vE.snap()
    mk.em("gpsimd", [mk.dG.snap()], lambda e: None)


def ph_tern_tasks(mk):
    aslot = [mk.pf(4096, 1024), mk.pf(8192, 1024)]
    tb = mk.pb(51200, 1024)
    bar = mk.barrier()
    st = mk._tern_st = {}

    def task(it, hf):
        def run(it=it, hf=hf):
            r = (it * 2 + hf) % 2
            a = aslot[r]
            la = mk.dma("sync", a,
                        mk.w0T.ap()[it * 128:(it + 1) * 128, hf * 1024:(hf + 1) * 1024],
                        [st.get(("rd", r))] + ([] if st else bar))

            def tfn(e, a=a):
                e.tensor_scalar(a, a, mk.misc_pp[:, 0:1], None, ALU.mult)
                e.drain()
                e.tensor_scalar(a, a, 1.0, -1.0, ALU.min, ALU.max)
                e.drain()
                return e.tensor_scalar(tb, a, MAGIC, -MAGIC, ALU.add, ALU.add)
            tern = mk.em("vector", [la, mk.ffn0_scales_ready, st.get("spill")],
                         tfn, sig=mk.vE)
            st[("rd", r)] = tern
            st["spill"] = mk.dma(
                "sync", mk.w0qb.ap()[it * 128:(it + 1) * 128, hf * 1024:(hf + 1) * 1024],
                tb, [tern])
        return run

    for it in range(16):
        for hf in range(2):
            mk.bg_tasks.append(task(it, hf))


def ph_wcast(mk):
    for i in range(8):
        mk.dma("gpsimd", mk.w1b.ap()[i * 256:(i + 1) * 256, :],
               mk.w1T.ap()[i * 256:(i + 1) * 256, :],
               [mk.cc_agv0] if i == 0 else [])
        mk.dma("gpsimd", mk.w3b.ap()[i * 256:(i + 1) * 256, :],
               mk.w3T.ap()[i * 256:(i + 1) * 256, :], [])
        mk.dma("gpsimd", mk.w2b.ap()[i * 1024:(i + 1) * 1024, :],
               mk.w2T.ap()[i * 1024:(i + 1) * 1024, :], [])


def ph_epilogue_dev(mk):
    xv = mk.x[:, :].rearrange("p (c t) -> p c t", c=NCH)
    bar = mk.barrier()
    t0 = mk.tE.snap()
    v0 = mk.vE.snap()
    ott = mk.pf(0, 1024)
    st = {}
    for ot in range(NCH):
        def tfn(e, ot=ot):
            last = None
            for tt in range(8):
                bank = mk.pk[2 + tt // 4]
                last = e.transpose(bank[:, (tt % 4) * 128:(tt % 4 + 1) * 128],
                                   xv[:, ot, tt * 128:(tt + 1) * 128], mk.idf[:])
            return last
        tr = mk.em("tensor", [mk.x_final, st.get("oev")] + (bar if ot == 0 else []),
                   tfn, sig=mk.tE)

        def oef(e):
            e.tensor_copy(ott[:, 0:512], mk.pk[2][:])
            return e.tensor_copy(ott[:, 512:1024], mk.pk[3][:])
        oev = mk.em("vector", [tr, st.get("odma")] + (bar if ot == 0 else []), oef, sig=mk.vE)
        st["oev"] = oev
        odst = mk.out.ap().rearrange("(t p) d -> p t d", p=128)[:, :, ot * 128:(ot + 1) * 128]
        st["odma"] = mk.dma("gpsimd", odst,
                            ott[:, :].rearrange("p (t d) -> p t d", t=8), [oev])
    mk.em("gpsimd", [mk.dG.snap()], lambda e: None)



_CACHE = {}


def build(stage=STAGE):
    if stage in _CACHE:
        return _CACHE[stage]
    mk = MK(stage)
    ph_prologue(mk)
    ph_ln(mk, 0)
    ph_quantpre(mk)
    ph_wq0prep_tasks(mk)
    pa = ph_passA_tasks(mk)
    mixed = []
    wq = list(mk.bg_tasks)
    mk.bg_tasks.clear()
    for i in range(64):
        mixed.append(wq[i])
        if i % 2 == 0:
            mixed.append(pa[i // 2])
    mixed += pa[32:]
    mk.bg_tasks.extend(mixed)
    ph_kvproj0(mk)
    while mk.bg_tasks:
        mk.bg_tasks.popleft()()
    mk.wq0b_done = mk.dG.snap()
    mk.passA_acc = mk._pa_st["acc"]
    ph_quantfin(mk)
    ph_ffn0passA(mk)
    if stage >= 4:
        ph_wcast(mk)
    if stage >= 2:
        ph_tern_tasks(mk)
    ph_attn(mk, 0)
    while mk.bg_tasks:
        mk.bg_tasks.popleft()()
    mk.w0qb_done = mk.dS.snap()
    ph_outproj(mk, 0)
    if stage >= 2:
        ph_ffn0mm(mk)
    if stage >= 3:
        ph_ln(mk, 1)
        ph_kvproj1(mk)
        ph_attn(mk, 1)
        ph_outproj(mk, 1)
    if stage >= 4:
        ph_ffn1(mk)
    else:
        mk.x_final = mk.vE.snap()
        ph_epilogue_dev(mk)
    nc = mk.finalize()
    _CACHE[stage] = nc
    return nc


def _consts():
    bf = ml_dtypes.bfloat16
    i = np.arange(128)
    had = np.where((np.bitwise_count(i[:, None] & i[None, :]) % 2) == 0, 1.0, -1.0)
    return {
        "c_idf": np.eye(128, dtype=np.float32),
        "c_ones_col_b": np.ones((128, 1), bf),
        "c_ones_row_b": np.ones((1, 128), bf),
        "c_ones_row_f": np.ones((1, 128), np.float32),
        "c_h128": had.astype(bf),
    }


def _fm_pack(vec):
    n = np.asarray(vec).shape[0] // 128
    return np.ascontiguousarray(np.asarray(vec, np.float32).reshape(n, 128).T)


def _prep_inputs(inp):
    x = np.asarray(inp["L0_attn_x"], np.float32)
    cst = _consts()
    rep = lambda s: np.ascontiguousarray(np.repeat(np.asarray(s, np.float32), 32, axis=0))
    tr = lambda w: np.ascontiguousarray(np.asarray(w, np.float32).T)
    bias0 = np.zeros((128, 64), np.float32)
    bias0[:, 0:16] = _fm_pack(inp["L0_attn_bq"])
    bias0[:, 16:32] = _fm_pack(inp["L0_attn_bk"])
    bias0[:, 32:48] = _fm_pack(inp["L0_attn_bv"])
    bias0[:, 48:64] = _fm_pack(inp["L0_attn_bo"])
    bias1 = np.zeros((128, 48), np.float32)
    bias1[:, 0:16] = _fm_pack(inp["L1_attn_bq"])
    bias1[:, 16:20] = _fm_pack(inp["L1_attn_bk"])
    bias1[:, 32:48] = _fm_pack(inp["L1_attn_bo"])
    shared = dict(
        wq0c=np.asarray(inp["L0_attn_wq_codes"], np.float32),
        wk0c=np.asarray(inp["L0_attn_wk_codes"], np.float32),
        wv0c=np.asarray(inp["L0_attn_wv_codes"], np.float32),
        wo0c=np.asarray(inp["L0_attn_wo_codes"], np.float32),
        sq0=rep(inp["L0_attn_wq_scales"]), sk0=rep(inp["L0_attn_wk_scales"]),
        sv0=rep(inp["L0_attn_wv_scales"]), so0=rep(inp["L0_attn_wo_scales"]),
        w0T=tr(inp["L0_ffn_weight"]),
        wq1=np.asarray(inp["L1_attn_wq"], np.float32),
        wk1=np.asarray(inp["L1_attn_wk"], np.float32),
        wv1=np.asarray(inp["L1_attn_wv"], np.float32),
        wo1=np.asarray(inp["L1_attn_wo"], np.float32),
        w1T=tr(inp["L1_ffn_W1"]), w3T=tr(inp["L1_ffn_W3"]), w2T=tr(inp["L1_ffn_W2"]),
        c_bias0=bias0, c_bias1=bias1,
        c_gamma=_fm_pack(inp["L0_ffn_norm_weight"]),
        c_bvrow0=np.asarray(inp["L0_attn_bv"], np.float32).reshape(1, 2048),
        c_bvrow1=np.asarray(inp["L1_attn_bv"], np.float32).reshape(1, 512),
        **cst,
    )
    in_maps = []
    for c in range(8):
        b, tk = c // 4, (c % 4) * TPC
        m = dict(shared)
        m["x_in"] = np.ascontiguousarray(x[b, tk:tk + TPC, :])
        in_maps.append(m)
    return in_maps


def kernel(**inp):
    nc = build(STAGE)
    in_maps = _prep_inputs(inp)
    res = run_bass_kernel_spmd(nc, in_maps, core_ids=list(range(8)))
    shards = [res.results[c]["out"] for c in range(8)]
    return np.stack([np.concatenate(shards[0:4], 0),
                     np.concatenate(shards[4:8], 0)]).reshape(2, 4096, 2048)


# version 12
# speedup vs baseline: 1.0032x; 1.0032x over previous
import os
from contextlib import ExitStack

import numpy as np
import ml_dtypes

import concourse.bass as bass
import concourse.mybir as mybir
from concourse.bass_utils import run_bass_kernel_spmd

F32 = mybir.dt.float32
BF16 = mybir.dt.bfloat16
AF = mybir.ActivationFunctionType
ALU = mybir.AluOpType
AXX = mybir.AxisListType.X

MAGIC = 12582912.0
TPC = 1024
NCH = 16
D = 2048
SEQ = 4096
NH = 16
HFF = 8192
HCH = 64
ISH = float(1.0 / np.sqrt(128.0))
NEG_HALF_ISH = float(-0.5 / np.sqrt(128.0))
INV_SQRT_HFF = float(1.0 / np.sqrt(8192.0))
LN_EPS = 1e-5
RMS_EPS = 1e-6

STAGE = int(os.environ.get("MK_STAGE", "4"))


class Ctr:
    def __init__(self, sem, is_dma=False):
        self.sem = sem
        self.n = 0
        self.step = 16 if is_dma else 1

    def snap(self):
        return (self.sem, self.n)


class MK:
    def __init__(self, stage=STAGE):
        self.stage = stage
        self.nc = bass.Bass("TRN2", target_bir_lowering=False)
        self.es = ExitStack()
        self.streams = {k: [] for k in ("sync", "gpsimd", "tensor", "scalar", "vector")}
        from collections import deque
        self.bg_tasks = deque()
        self._declare()
        self._alloc()

    def _declare(self):
        nc = self.nc
        dt = nc.dram_tensor
        I = lambda n, s, d=F32: dt(n, s, d, kind="ExternalInput")
        self.x_in = I("x_in", [TPC, D])
        self.wq0c = I("wq0c", [D, D]); self.wk0c = I("wk0c", [D, D])
        self.wv0c = I("wv0c", [D, D]); self.wo0c = I("wo0c", [D, D])
        self.sq0 = I("sq0", [D, D]); self.sk0 = I("sk0", [D, D])
        self.sv0 = I("sv0", [D, D]); self.so0 = I("so0", [D, D])
        self.w0T = I("w0T", [D, D])
        self.wq1 = I("wq1", [D, D]); self.wk1 = I("wk1", [D, 512])
        self.wv1 = I("wv1", [D, 512]); self.wo1 = I("wo1", [D, D])
        self.w1T = I("w1T", [D, HFF]); self.w3T = I("w3T", [D, HFF])
        self.w2T = I("w2T", [HFF, D])
        self.c_idf = I("c_idf", [128, 128])
        self.c_ones_col_b = I("c_ones_col_b", [128, 1], BF16)
        self.c_ones_row_b = I("c_ones_row_b", [1, 128], BF16)
        self.c_ones_row_f = I("c_ones_row_f", [1, 128])
        self.c_h128 = I("c_h128", [128, 128], BF16)
        self.c_bias0 = I("c_bias0", [128, 64])
        self.c_bias1 = I("c_bias1", [128, 48])
        self.c_gamma = I("c_gamma", [128, 16])
        self.c_bvrow0 = I("c_bvrow0", [1, 2048])
        self.c_bvrow1 = I("c_bvrow1", [1, 512])

        self.out = dt("out", [TPC, D], F32, kind="ExternalOutput")

        self.wq0b = dt("wq0b", [D, D], BF16)
        self.w0qb = dt("w0qb", [D, D], BF16)
        self.kin0 = dt("kin0", [2048, 1024], BF16)
        self.kout0g = [dt(f"kout0g{g}", [2048, 1024], BF16) for g in range(4)]
        self.vin0g = [dt(f"vin0g{g}", [1024, 512], BF16) for g in range(4)]
        self.vout0g = [dt(f"vout0g{g}", [4096, 512], BF16) for g in range(4)]
        self.kin1 = dt("kin1", [512, 1024], BF16)
        self.kout1 = dt("kout1", [2048, 1024], BF16)
        self.vin1 = dt("vin1", [1024, 512], BF16)
        self.vout1 = dt("vout1", [4096, 512], BF16)
        self.ar0_in = dt("ar0_in", [1, 32], F32)
        self.ar0_out = dt("ar0_out", [1, 32], F32)
        self.ar1_in = dt("ar1_in", [1, 8], F32)
        self.ar1_out = dt("ar1_out", [1, 8], F32)
        self.xq_sp = dt("xq_sp", [2048, 1024], BF16)
        self.z2_sp = dt("z2_sp", [2048, 1024], F32)
        self.w1b = dt("w1b", [D, HFF], BF16)
        self.w3b = dt("w3b", [D, HFF], BF16)
        self.w2b = dt("w2b", [HFF, D], BF16)

    def _alloc(self):
        nc, E = self.nc, self.es.enter_context
        T = lambda n, s, d: E(nc.sbuf_tensor(n, s, d))
        self.idf = T("idf", [128, 128], F32)
        self.ones_col_b = T("ones_col_b", [128, 1], BF16)
        self.ones_row_b = T("ones_row_b", [1, 128], BF16)
        self.ones_row_f = T("ones_row_f", [1, 128], F32)
        self.h128 = T("h128", [128, 128], BF16)
        self.bias0 = T("bias0", [128, 64], F32)
        self.bias1 = T("bias1", [128, 48], F32)
        self.gamma = T("gamma", [128, 16], F32)
        self.rowf = T("rowf", [1, 2048], F32)
        self.rowf2 = T("rowf2", [1, 128], F32)
        self.misc_pp = T("misc_pp", [128, 8], F32)
        self.negb = T("negb", [128, 16], F32)
        self.wacc = T("wacc", [128, 2], F32)

        self.x = T("x", [128, NCH * TPC], F32)
        self.xnb = T("xnb", [128, NCH * TPC], BF16)
        self.att = T("att", [128, NCH * TPC], BF16)
        self.PF = T("PF", [128, 3072], F32)
        self.PB = T("PB", [128, 26624], BF16)
        self.gscaled = T("gscaled", [128, 16], F32)

        self.pk = [E(nc.psum_tensor(f"pk{i}", [128, 512], F32)) for i in range(8)]
        self.prow = {}
        for i in (4, 5, 6, 7):
            v = nc.virtual_psum_tensor(f"prow{i}", [1, 512], F32)
            nc.back_virtual_tensor_with(v, self.pk[i])
            self.prow[i] = v

        self.dS = Ctr(E(nc.semaphore("dS")), True)
        self.dQ = Ctr(E(nc.semaphore("dQ")), True)
        self.dG = Ctr(E(nc.semaphore("dG")), True)
        self.rS = Ctr(E(nc.semaphore("rS")))
        self.rQ = Ctr(E(nc.semaphore("rQ")))
        self.rG = Ctr(E(nc.semaphore("rG")))
        self._qmap = {self.dS.sem: ("sync", self.dS, self.rS),
                      self.dQ.sem: ("scalar", self.dQ, self.rQ),
                      self.dG.sem: ("gpsimd", self.dG, self.rG)}
        self._flush_cache = {}
        self.cc = Ctr(E(nc.semaphore("cc")))
        self.tE = Ctr(E(nc.semaphore("tE")))
        self.sE = Ctr(E(nc.semaphore("sE")))
        self.vE = Ctr(E(nc.semaphore("vE")))

    def flush(self, qeng, at=None):
        qsem, ctr, relay = {"sync": (self.dS.sem, self.dS, self.rS),
                            "scalar": (self.dQ.sem, self.dQ, self.rQ),
                            "gpsimd": (self.dG.sem, self.dG, self.rG)}[qeng]
        need = ctr.n if at is None else at
        for cnt, snap in self._flush_cache.setdefault(qeng, []):
            if cnt >= need:
                return snap
        val = ctr.n
        relay.n += 1
        rsnap = (relay.sem, relay.n)

        def run(engine):
            engine.wait_ge(qsem, val)
            engine.nop().then_inc(relay.sem, 1)
        self.streams[qeng].append(run)
        self._flush_cache[qeng].append((val, rsnap))
        return rsnap

    def em(self, eng, waits, fn, sig=None):
        ws = []
        for w in waits:
            if w is None or w[1] <= 0:
                continue
            q = self._qmap.get(w[0])
            if q is not None and q[0] != eng:
                ws.append(self.flush(q[0], at=w[1]))
            elif q is not None:
                ws.append((w[0], q[1].n))
            else:
                ws.append(w)
        ws = [w for w in ws if w[1] > 0]
        if sig is not None:
            sig.n += sig.step
            sem, val, step = sig.sem, sig.n, sig.step

        def run(engine):
            for s, v in ws:
                engine.wait_ge(s, v)
            inst = fn(engine)
            if sig is not None:
                assert inst is not None
                inst.then_inc(sem, step)
        self.streams[eng].append(run)
        return sig.snap() if sig is not None else None

    def dma(self, eng, out, in_, waits=()):
        ctr = {"sync": self.dS, "scalar": self.dQ, "gpsimd": self.dG}[eng]
        return self.em(eng, waits, lambda e: e.dma_start(out=out, in_=in_), sig=ctr)

    def barrier(self):
        return [self.dS.snap(), self.dQ.snap(), self.dG.snap(),
                self.tE.snap(), self.sE.snap(), self.vE.snap(), self.cc.snap()]

    def pf(self, off, cols):
        assert off % 4 == 0
        return self.PF[:, off // 4: off // 4 + cols]

    def pb(self, off, cols):
        assert off % 2 == 0
        return self.PB[:, off // 2: off // 2 + cols]

    def bvrow0(self):
        return self.PB[0:1, 14336:16384]

    def bvrow1(self):
        return self.PB[0:1, 16384:16896]

    def hid(self, c):
        if c < 32:
            return self.pb(20480 + c * 1024, 512)
        return self.att[:, (c - 32) * 512:(c - 32 + 1) * 512]

    def finalize(self):
        nc = self.nc
        with self.es:
            with nc.Block() as block:
                @block.sync
                def _(e):
                    for f in self.streams["sync"]:
                        f(e)

                @block.gpsimd
                def _(e):
                    for f in self.streams["gpsimd"]:
                        f(e)

                @block.tensor
                def _(e):
                    for f in self.streams["tensor"]:
                        f(e)

                @block.scalar
                def _(e):
                    for f in self.streams["scalar"]:
                        f(e)

                @block.vector
                def _(e):
                    for f in self.streams["vector"]:
                        f(e)
        return nc



def ph_prologue(mk):
    for dst, src in [
        (mk.idf[:], mk.c_idf.ap()),
        (mk.ones_col_b[:], mk.c_ones_col_b.ap()),
        (mk.ones_row_b[:], mk.c_ones_row_b.ap()),
        (mk.ones_row_f[:], mk.c_ones_row_f.ap()),
        (mk.h128[:], mk.c_h128.ap()),
        (mk.bias0[:], mk.c_bias0.ap()), (mk.bias1[:], mk.c_bias1.ap()),
        (mk.gamma[:], mk.c_gamma.ap()),
    ]:
        mk.consts_ready = mk.dma("sync", dst, src)
    mk.em("vector", [], lambda e: e.memset(mk.rowf2[0:1, 0:128], 0.0), sig=mk.vE)
    mk.ar_zeroed = mk.vE.snap()

    xr = mk.x_in.ap().rearrange("(t p) d -> t p d", p=128)
    xv = mk.x[:, :].rearrange("p (c t) -> p c t", c=NCH)
    st = {}
    for tt in range(8):
        stage = mk.pf(0, 2048)
        ld = mk.dma("scalar", stage, xr[tt], [st.get("ev")])
        for cg in range(4):
            bank = mk.pk[cg % 2]

            def tfn(e, stage=stage, cg=cg, bank=bank):
                last = None
                for k in range(4):
                    last = e.transpose(bank[:, k * 128:(k + 1) * 128],
                                       stage[:, (cg * 4 + k) * 128:(cg * 4 + k + 1) * 128],
                                       mk.idf[:])
                return last
            ts = mk.em("tensor", [ld, mk.consts_ready, st.get(("bank", cg % 2))],
                       tfn, sig=mk.tE)
            dst3 = xv[:, cg * 4:(cg + 1) * 4, tt * 128:(tt + 1) * 128]
            ev = mk.em("vector", [ts],
                       lambda e, dst3=dst3, bank=bank: e.tensor_copy(dst3, bank[:, 0:512]),
                       sig=mk.vE)
            st[("bank", cg % 2)] = ev
        st["ev"] = ev
    mk.x_ready = mk.vE.snap()


def _wjit(mk, codes, scexp, col0, ncols, dst_bf, extra_waits, qa="sync", qb="scalar",
          pf_b=False):
    assert ncols == 128
    st = mk._wjit_state = getattr(mk, "_wjit_state", {})
    cap = codes.ap().rearrange("(c p) o -> p c o", p=128)[:, :, col0:col0 + ncols]
    sap = scexp.ap().rearrange("(c p) o -> p c o", p=128)[:, :, col0:col0 + ncols]
    dst3 = dst_bf if len(dst_bf.shape) == 3 else dst_bf.rearrange("p (c o) -> p c o", c=16)
    a = mk.pf(0, 1024)
    bsl = [mk.pb(47104, 1024), mk.pb(49152, 1024)]
    tt = None
    for ich in range(2):
        ew = list(extra_waits) if ich == 0 else []
        la = mk.dma(qa, a, cap[:, ich * 8:(ich + 1) * 8, :], [st.get("tt")] + ew)
        if pf_b:
            bb = mk.pf(4096, 1024)
            lb = mk.dma(qb, bb, sap[:, ich * 8:(ich + 1) * 8, :], [st.get("tt")] + ew)
        else:
            bb = bsl[ich].rearrange("p (c o) -> p c o", c=8)
            lb = mk.dma("gpsimd", bsl[ich], sap[:, ich * 8:(ich + 1) * 8, :],
                        [st.get("tt")] + ew)
        tt = mk.em("vector", [la, lb],
                   lambda e, bb=bb, ich=ich: e.tensor_tensor(
                       dst3[:, ich * 8:(ich + 1) * 8, :], a, bb, ALU.mult), sig=mk.vE)
        st["tt"] = tt
    return tt


def ph_ln(mk, layer):
    xv = mk.x[:, :].rearrange("p (c t) -> p c t", c=NCH)
    bar = mk.barrier()
    prev_mm = None
    for c in range(16):
        xb = mk.pb((c % 2) * 2048, 1024)
        sq = mk.pb(4096 + (c % 2) * 2048, 1024)
        wx = ([mk.x_ready] if layer == 0 else [mk.ln1_start, mk.ln1_start_t])
        if c < 2:
            wx = wx + bar
        cast = mk.em("vector", wx + [prev_mm],
                     lambda e, xb=xb, c=c: e.tensor_copy(xb, xv[:, c, :]), sig=mk.vE)
        sqd = mk.em("scalar", wx + [prev_mm],
                    lambda e, sq=sq, c=c: e.activation(sq, xv[:, c, :], AF.Square),
                    sig=mk.sE)

        def mfn(e, xb=xb, sq=sq, c=c):
            last = None
            for th in range(2):
                last = e.matmul(mk.prow[4 + th][:], mk.ones_col_b[:],
                                xb[:, th * 512:(th + 1) * 512],
                                start=(c == 0), stop=(c == 15))
                last = e.matmul(mk.prow[6 + th][:], mk.ones_col_b[:],
                                sq[:, th * 512:(th + 1) * 512],
                                start=(c == 0), stop=(c == 15))
            return last
        prev_mm = mk.em("tensor", [cast, sqd, mk.consts_ready], mfn, sig=mk.tE)
    def s1(e):
        e.activation(mk.rowf[0:1, 0:512], mk.prow[4][:], AF.Copy, scale=1.0 / 2048)
        e.activation(mk.rowf[0:1, 512:1024], mk.prow[5][:], AF.Copy, scale=1.0 / 2048)
        e.activation(mk.rowf[0:1, 1024:1536], mk.prow[6][:], AF.Copy, scale=1.0 / 2048)
        return e.activation(mk.rowf[0:1, 1536:2048], mk.prow[7][:], AF.Copy, scale=1.0 / 2048)
    rows = mk.em("scalar", [prev_mm], s1, sig=mk.sE)

    def v1(e):
        e.tensor_tensor(mk.PF[0:1, 0:1024], mk.rowf[0:1, 0:1024], mk.rowf[0:1, 0:1024], ALU.mult)
        e.drain()
        e.tensor_tensor(mk.rowf[0:1, 1024:2048], mk.rowf[0:1, 1024:2048],
                        mk.PF[0:1, 0:1024], ALU.subtract)
        e.drain()
        return e.tensor_scalar(mk.rowf[0:1, 1024:2048], mk.rowf[0:1, 1024:2048],
                               LN_EPS, None, ALU.add)
    var = mk.em("vector", [rows], v1, sig=mk.vE)

    def s2(e):
        e.activation(mk.rowf[0:1, 1024:2048], mk.rowf[0:1, 1024:2048], AF.Ln)
        e.drain()
        return e.activation(mk.rowf[0:1, 1024:2048], mk.rowf[0:1, 1024:2048], AF.Exp, scale=-0.5)
    istd = mk.em("scalar", [var], s2, sig=mk.sE)

    mbc = mk.pf(0, 1024)
    ibc = mk.pf(4096, 1024)

    def b1(e):
        e.matmul(mk.pk[0][:], mk.ones_row_f[:], mk.rowf[0:1, 0:512], start=True, stop=True)
        e.matmul(mk.pk[1][:], mk.ones_row_f[:], mk.rowf[0:1, 512:1024], start=True, stop=True)
        e.matmul(mk.pk[2][:], mk.ones_row_f[:], mk.rowf[0:1, 1024:1536], start=True, stop=True)
        return e.matmul(mk.pk[3][:], mk.ones_row_f[:], mk.rowf[0:1, 1536:2048], start=True, stop=True)
    bc = mk.em("tensor", [istd, rows, var], b1, sig=mk.tE)

    def e1(e):
        e.tensor_copy(mbc[:, 0:512], mk.pk[0][:])
        e.tensor_copy(mbc[:, 512:1024], mk.pk[1][:])
        e.tensor_copy(ibc[:, 0:512], mk.pk[2][:])
        return e.tensor_copy(ibc[:, 512:1024], mk.pk[3][:])
    bce = mk.em("vector", [bc], e1, sig=mk.vE)

    st = {}
    prev_mm = None
    for c in range(16):
        if layer == 0:
            z2r = mk.pf(8192, 1024)
            sq = mk.pb(4096 + (c % 2) * 2048, 1024)

            def a1(e, c=c, z2r=z2r):
                e.tensor_tensor(z2r, xv[:, c, :], mbc, ALU.subtract)
                e.drain()
                return e.tensor_tensor(z2r, z2r, ibc, ALU.mult)
            ap = mk.em("vector", [bce, st.get("dma"), st.get("mm"),
                                  st.get("sc")], a1, sig=mk.vE)

            def c1(e, c=c, z2r=z2r, sq=sq):
                e.activation(mk.xnb[:, c * TPC:(c + 1) * TPC], z2r, AF.Copy)
                return e.activation(sq, z2r, AF.Square)
            sc = mk.em("scalar", [ap, rows], c1, sig=mk.sE)
            st["sc"] = sc

            def m1(e, sq=sq, c=c):
                e.matmul(mk.prow[4][:], mk.ones_col_b[:], sq[:, 0:512],
                         start=(c == 0), stop=(c == 15))
                return e.matmul(mk.prow[5][:], mk.ones_col_b[:], sq[:, 512:1024],
                                start=(c == 0), stop=(c == 15))
            prev_mm = mk.em("tensor", [sc], m1, sig=mk.tE)
            st["mm"] = prev_mm
            st["dma"] = mk.dma("gpsimd", mk.z2_sp.ap()[c * 128:(c + 1) * 128, :],
                               z2r, [sc])
        else:
            z2r = mk.pf(8192, 1024)
            st["dummy"] = None

            def a2(e, c=c, z2r=z2r):
                e.drain()
                e.tensor_tensor(z2r, xv[:, c, :], mbc, ALU.subtract)
                e.drain()
                return e.tensor_tensor(mk.xnb[:, c * TPC:(c + 1) * TPC], z2r, ibc, ALU.mult)
            mk.em("vector", [bce], a2, sig=mk.vE)
    mk.xn_ready_v = mk.vE.snap()
    mk.xn_ready_s = mk.sE.snap()
    if layer == 0:
        mk.rms_mm = prev_mm
        mk.z2_spilled = mk.dG.snap()


def ph_quantpre(mk):
    bar = mk.barrier()
    def s1(e):
        e.activation(mk.rowf[0:1, 0:512], mk.prow[4][:], AF.Copy, scale=1.0 / 2048)
        return e.activation(mk.rowf[0:1, 512:1024], mk.prow[5][:], AF.Copy, scale=1.0 / 2048)
    ms = mk.em("scalar", [mk.rms_mm], s1, sig=mk.sE)
    eps = mk.em("vector", [ms],
                lambda e: e.tensor_scalar(mk.rowf[0:1, 0:1024], mk.rowf[0:1, 0:1024],
                                          RMS_EPS, None, ALU.add), sig=mk.vE)

    def s2(e):
        e.activation(mk.rowf[0:1, 0:1024], mk.rowf[0:1, 0:1024], AF.Ln)
        e.drain()
        return e.activation(mk.rowf[0:1, 0:1024], mk.rowf[0:1, 0:1024], AF.Exp, scale=-0.5)
    irow = mk.em("scalar", [eps], s2, sig=mk.sE)
    irbc = mk.pf(8192, 1024)

    def b1(e):
        e.matmul(mk.pk[0][:], mk.ones_row_f[:], mk.rowf[0:1, 0:512], start=True, stop=True)
        return e.matmul(mk.pk[1][:], mk.ones_row_f[:], mk.rowf[0:1, 512:1024], start=True, stop=True)
    bc = mk.em("tensor", [irow, mk.z2_spilled], b1, sig=mk.tE)

    def e1(e):
        e.tensor_copy(irbc[:, 0:512], mk.pk[0][:])
        return e.tensor_copy(irbc[:, 512:1024], mk.pk[1][:])
    bce = mk.em("vector", [bc], e1, sig=mk.vE)

    lm = None
    st = {}
    for c in range(16):
        z2r = mk.pf(4096, 1024)
        ld = mk.dma("sync", z2r, mk.z2_sp.ap()[c * 128:(c + 1) * 128, :],
                    [mk.z2_spilled, st.get("rd")] + (bar if c == 0 else []))

        def q1(e, c=c, z2r=z2r):
            e.scalar_tensor_tensor(z2r, z2r, mk.gamma[:, c:c + 1], irbc, ALU.mult, ALU.mult)
            e.drain()
            e.tensor_reduce(mk.misc_pp[:, 3:4], z2r, AXX, ALU.max, apply_absolute_value=True)
            e.drain()
            if c == 0:
                return e.tensor_copy(mk.wacc[:, 0:1], mk.misc_pp[:, 3:4])
            return e.tensor_tensor(mk.wacc[:, 0:1], mk.wacc[:, 0:1], mk.misc_pp[:, 3:4], ALU.max)
        lm = mk.em("vector", [ld, bce], q1, sig=mk.vE)
        st["rd"] = lm
    tr = mk.em("tensor", [lm],
               lambda e: e.transpose(mk.pk[2][0:1, 0:128], mk.wacc[:, 0:1], mk.idf[:]),
               sig=mk.tE)
    mk.z2_done = mk.em("vector", [tr, mk.ar_zeroed],
                       lambda e: e.tensor_reduce(mk.rowf2[0:1, 0:1], mk.pk[2][0:1, 0:128],
                                                 AXX, ALU.max), sig=mk.vE)
    mk.quantpre_v = mk.vE.snap()


def ph_kvproj0(mk):
    xnv = mk.xnb[:, :].rearrange("p (c t) -> p c t", c=NCH)
    bar = mk.barrier()
    bvs = mk.dma("sync", mk.PF[0:1, 0:2048], mk.c_bvrow0.ap(), bar)
    cv0 = mk.em("vector", [bvs],
                lambda e: e.tensor_copy(mk.bvrow0(), mk.PF[0:1, 0:2048]), sig=mk.vE)
    bvs1 = mk.dma("sync", mk.PF[0:1, 2048:2560], mk.c_bvrow1.ap(), [cv0])
    cv1 = mk.em("vector", [bvs1],
                lambda e: e.tensor_copy(mk.bvrow1(), mk.PF[0:1, 2048:2560]), sig=mk.vE)
    mk.bvrows_ready = mk.vE.snap()

    wkp = mk.pb(0, 2048)
    kt_off = (20480, 22528)
    ksq = mk.pb(26624, 1024)
    g4 = [[0, 1, 2, 3], [4, 5, 6, 7]]
    g8 = [list(range(8))]
    wvg_full = mk.pb(4096, 16 * 512)
    wvg3 = wvg_full.rearrange("p (c o) -> p c o", c=16)
    vt_off = (24576, 25600)
    mk.cc_agk0g, mk.cc_agv0g = [], []
    st = {}
    for g in range(4):
        for h in range(4 * g, 4 * g + 4):
            jit = _wjit(mk, mk.wk0c, mk.sk0, h * 128, 128, wkp,
                        [st.get("wk_used")] + (bar + [mk.bvrows_ready] if h == 0 else []))
            ktile = mk.pb(kt_off[h % 2], 1024)

            def mfn(e, h=h):
                last = None
                for th in range(2):
                    for ic in range(16):
                        last = e.matmul(mk.pk[th][:], wkp[:, ic * 128:(ic + 1) * 128],
                                        xnv[:, ic, th * 512:(th + 1) * 512],
                                        start=(ic == 0), stop=(ic == 15))
                return last
            mm = mk.em("tensor", [jit, mk.xn_ready_s, st.get(("ev", h % 2))], mfn, sig=mk.tE)
            st["wk_used"] = mm

            def efn(e, h=h, ktile=ktile):
                e.activation(ktile[:, 0:512], mk.pk[0][:], AF.Identity,
                             bias=mk.bias0[:, 16 + h:17 + h])
                return e.activation(ktile[:, 512:1024], mk.pk[1][:], AF.Identity,
                                    bias=mk.bias0[:, 16 + h:17 + h])
            ev = mk.em("scalar", [mm, st.get(("dma", h % 2))], efn, sig=mk.sE)
            st[("ev", h % 2)] = ev
            sqv = mk.em("vector", [ev, st.get("knorm_mm")],
                        lambda e, ktile=ktile: e.tensor_tensor(ksq, ktile, ktile, ALU.mult),
                        sig=mk.vE)

            def nfn(e):
                e.matmul(mk.prow[6][:], mk.ones_col_b[:], ksq[:, 0:512], start=True, stop=True)
                return e.matmul(mk.prow[7][:], mk.ones_col_b[:], ksq[:, 512:1024],
                                start=True, stop=True)
            nm = mk.em("tensor", [sqv, st.get("knorm_rd")], nfn, sig=mk.tE)
            st["knorm_mm"] = nm

            def rfn(e, h=h):
                e.tensor_reduce(mk.rowf2[0:1, 1 + h:2 + h], mk.prow[6][:], AXX, ALU.max)
                e.tensor_reduce(mk.rowf2[0:1, 60:61], mk.prow[7][:], AXX, ALU.max)
                e.drain()
                return e.tensor_tensor(mk.rowf2[0:1, 1 + h:2 + h], mk.rowf2[0:1, 1 + h:2 + h],
                                       mk.rowf2[0:1, 60:61], ALU.max)
            st["knorm_rd"] = mk.em("vector", [nm], rfn, sig=mk.vE)
            st[("dma", h % 2)] = mk.dma("gpsimd", mk.kin0.ap()[h * 128:(h + 1) * 128, :],
                                        ktile, [ev])
            for _ in range(2):
                if mk.bg_tasks:
                    mk.bg_tasks.popleft()()
        kg_done = mk.dG.snap()
        mk.em("gpsimd", [kg_done],
              lambda e, g=g: e.collective_compute(
                  "AllGather", ALU.bypass, replica_groups=g4,
                  ins=[mk.kin0.ap()[g * 512:(g + 1) * 512, :].opt()],
                  outs=[mk.kout0g[g].ap().opt()]),
              sig=mk.cc)
        mk.cc_agk0g.append(mk.cc.snap())

        if g == 3:
            mk.knorms_done = mk.vE.snap()
            arld = mk.dma("gpsimd", mk.ar0_in.ap(), mk.rowf2[0:1, 0:32],
                          [mk.z2_done, mk.knorms_done])
            mk.em("gpsimd", [arld],
                  lambda e: e.collective_compute("AllReduce", ALU.max, replica_groups=g8,
                                                 ins=[mk.ar0_in.ap().opt()],
                                                 outs=[mk.ar0_out.ap().opt()]),
                  sig=mk.cc)
            mk.cc_ar0 = mk.cc.snap()

        jits = []
        for sf in range(4):
            dst = wvg3[:, :, sf * 128:(sf + 1) * 128]
            jits.append(_wjit(mk, mk.wv0c, mk.sv0, g * 512 + sf * 128, 128, dst,
                              [st.get("wv_used") or st.get("wk_used")]))
        for tt in range(8):
            def mfn(e, g=g, tt=tt):
                e.matmul(mk.pk[2 + tt % 2][:], mk.ones_row_b[:],
                         mk.bvrow0()[0:1, g * 512:(g + 1) * 512], start=True, stop=False)
                last = None
                for ic in range(16):
                    last = e.matmul(mk.pk[2 + tt % 2][:],
                                    xnv[:, ic, tt * 128:(tt + 1) * 128],
                                    wvg_full[:, ic * 512:(ic + 1) * 512],
                                    start=False, stop=(ic == 15))
                return last
            mm = mk.em("tensor", jits + [mk.bvrows_ready, st.get(("vev", tt % 2))],
                       mfn, sig=mk.tE)
            st["wv_used"] = mm
            vtile = mk.pb(vt_off[tt % 2], 512)
            ev = mk.em("scalar", [mm, st.get(("vdma", tt % 2))],
                       lambda e, vtile=vtile, tt=tt: e.activation(vtile, mk.pk[2 + tt % 2][:],
                                                                  AF.Copy),
                       sig=mk.sE)
            st[("vev", tt % 2)] = ev
            st[("vdma", tt % 2)] = mk.dma(
                "gpsimd", mk.vin0g[g].ap()[tt * 128:(tt + 1) * 128, :],
                vtile, [ev])
            for _ in range(2):
                if mk.bg_tasks:
                    mk.bg_tasks.popleft()()
        vg_done = mk.dG.snap()
        mk.em("gpsimd", [vg_done],
              lambda e, g=g: e.collective_compute(
                  "AllGather", ALU.bypass, replica_groups=g4,
                  ins=[mk.vin0g[g].ap().opt()], outs=[mk.vout0g[g].ap().opt()]),
              sig=mk.cc)
        mk.cc_agv0g.append(mk.cc.snap())
    mk.cc_agv0 = mk.cc_agv0g[3]
    mk.pool_entry_t = mk.tE.snap()
    mk.pool_entry_v = mk.vE.snap()
    mk.pool_entry_s = mk.sE.snap()


def ph_quantfin(mk):
    bar = mk.barrier()
    ld = mk.dma("sync", mk.rowf2[0:1, 0:32], mk.ar0_out.ap(), [mk.cc_ar0])
    mk.ar0_row_loaded = ld

    def v1(e):
        e.tensor_scalar(mk.rowf2[0:1, 40:41], mk.rowf2[0:1, 0:1], 1e-5, None, ALU.max)
        e.drain()
        e.reciprocal(mk.rowf2[0:1, 41:42], mk.rowf2[0:1, 40:41])
        e.drain()
        return e.tensor_scalar(mk.rowf2[0:1, 41:42], mk.rowf2[0:1, 41:42], 127.0, None, ALU.mult)
    r127 = mk.em("vector", [ld], v1, sig=mk.vE)
    bc = mk.em("tensor", [r127, mk.pool_entry_v, mk.pool_entry_s],
               lambda e: e.matmul(mk.pk[3][:, 0:1], mk.ones_row_f[:],
                                  mk.rowf2[0:1, 41:42], start=True, stop=True), sig=mk.tE)
    bce = mk.em("vector", [bc],
                lambda e: e.tensor_copy(mk.misc_pp[:, 2:3], mk.pk[3][:, 0:1]), sig=mk.vE)
    mk.quantfin_bce = bce
    gsc = mk.em("vector", [bce],
                lambda e: e.tensor_scalar(mk.gscaled[:, :], mk.gamma[:, :],
                                          mk.misc_pp[:, 2:3], None, ALU.mult), sig=mk.vE)
    irbc = mk.pf(8192, 1024)

    def b1(e):
        e.matmul(mk.pk[0][:], mk.ones_row_f[:], mk.rowf[0:1, 0:512], start=True, stop=True)
        return e.matmul(mk.pk[1][:], mk.ones_row_f[:], mk.rowf[0:1, 512:1024], start=True, stop=True)
    bc2 = mk.em("tensor", [mk.pool_entry_v, r127] + bar, b1, sig=mk.tE)

    def e1(e):
        e.tensor_copy(irbc[:, 0:512], mk.pk[0][:])
        return e.tensor_copy(irbc[:, 512:1024], mk.pk[1][:])
    bce2 = mk.em("vector", [bc2] + bar, e1, sig=mk.vE)

    st = {}
    z2s = [mk.pf(0, 1024), mk.pf(4096, 1024)]
    stgs = [mk.pb(47104, 1024), mk.pb(49152, 1024)]
    for c in range(16):
        z2r = z2s[c % 2]
        stg = stgs[c % 2]
        ld = mk.dma("gpsimd", z2r, mk.z2_sp.ap()[c * 128:(c + 1) * 128, :],
                    [st.get(("q", c % 2))] + (bar if c < 2 else []))

        def qfn(e, c=c, z2r=z2r, stg=stg):
            e.scalar_tensor_tensor(z2r, z2r, mk.gscaled[:, c:c + 1], irbc, ALU.mult, ALU.mult)
            e.drain()
            e.tensor_scalar(z2r, z2r, 127.0, -128.0, ALU.min, ALU.max)
            e.drain()
            return e.tensor_scalar(stg, z2r, MAGIC, -MAGIC, ALU.add, ALU.add)
        qd = mk.em("vector", [ld, gsc, bce2, st.get(("dma", c % 2)),
                              mk.pool_entry_t if c < 2 else None], qfn, sig=mk.vE)
        st[("q", c % 2)] = qd
        st[("dma", c % 2)] = mk.dma("gpsimd", mk.xq_sp.ap()[c * 128:(c + 1) * 128, :], stg, [qd])
    mk.xq_spilled = mk.dG.snap()
    mk.flush("gpsimd")
    mk.quantfin_v = mk.vE.snap()


def ph_wq0prep_tasks(mk):
    bar = mk.barrier()
    aslot = [mk.pf(8192, 512), mk.pf(10240, 512)]
    bslot = [mk.pb(45056, 512), mk.pb(46080, 512)]
    wslot = [mk.pb(36864, 2048), mk.pb(40960, 2048)]
    st = mk._wq_st = {}

    def task(ic, q):
        def run(ic=ic, q=q):
            r = q % 2
            first = bar if (ic == 0 and q < 2) else []
            la = mk.dma("sync", aslot[r],
                        mk.wq0c.ap()[ic * 128:(ic + 1) * 128, q * 512:(q + 1) * 512],
                        [st.get(("tt", r))] + first)
            lb = mk.dma("gpsimd", bslot[r],
                        mk.sq0.ap()[ic * 128:(ic + 1) * 128, q * 512:(q + 1) * 512],
                        [st.get(("tt", r))] + first)
            w = wslot[ic % 2]
            tt = mk.em("vector", [la, lb, st.get(("wsp", ic % 2)) if q == 0 else None],
                       lambda e, w=w, q=q, r=r: e.tensor_tensor(
                           w[:, q * 512:(q + 1) * 512], aslot[r], bslot[r], ALU.mult),
                       sig=mk.vE)
            st[("tt", r)] = tt
            if q == 3:
                st[("wsp", ic % 2)] = mk.dma(
                    "gpsimd", mk.wq0b.ap()[ic * 128:(ic + 1) * 128, :], w, [tt])
        return run

    for ic in range(16):
        for q in range(4):
            mk.bg_tasks.append(task(ic, q))


def ph_passA_tasks(mk):
    bar = mk.barrier()
    st = mk._pa_st = {}

    def task(it):
        def run(it=it):
            r = it % 2
            a = mk.pf(4096 + 2048 * r, 512)
            la = mk.dma("scalar", a,
                        mk.w0T.ap()[(it // 4) * 128:(it // 4 + 1) * 128,
                                    (it % 4) * 512:(it % 4 + 1) * 512],
                        [st.get(("rd", r))] + (bar if it < 2 else []))

            def rfn(e, it=it, a=a):
                e.tensor_reduce(mk.misc_pp[:, 3:4], a, AXX, ALU.add,
                                apply_absolute_value=True)
                e.drain()
                if it == 0:
                    return e.tensor_copy(mk.wacc[:, 1:2], mk.misc_pp[:, 3:4])
                return e.tensor_tensor(mk.wacc[:, 1:2], mk.wacc[:, 1:2],
                                       mk.misc_pp[:, 3:4], ALU.add)
            st[("rd", r)] = mk.em("vector", [la, st.get("acc")], rfn, sig=mk.vE)
            st["acc"] = st[("rd", r)]
        return run

    return [task(it) for it in range(64)]


def ph_ffn0passA(mk):
    st = mk._pa_st
    tr = mk.em("tensor", [st["acc"], mk.quantfin_bce],
               lambda e: e.transpose(mk.pk[3][0:1, 0:128], mk.wacc[:, 1:2], mk.idf[:]),
               sig=mk.tE)

    def v2(e):
        e.tensor_reduce(mk.rowf2[0:1, 42:43], mk.pk[3][0:1, 0:128], AXX, ALU.add)
        e.drain()
        e.tensor_scalar(mk.rowf2[0:1, 42:43], mk.rowf2[0:1, 42:43], 1.0 / (2048.0 * 2048.0),
                        None, ALU.mult)
        e.drain()
        e.tensor_scalar(mk.rowf2[0:1, 42:43], mk.rowf2[0:1, 42:43], 1e-5, None, ALU.max)
        e.drain()
        e.reciprocal(mk.rowf2[0:1, 43:44], mk.rowf2[0:1, 42:43])
        e.tensor_tensor(mk.rowf2[0:1, 44:45], mk.rowf2[0:1, 40:41], mk.rowf2[0:1, 42:43], ALU.mult)
        e.drain()
        return e.tensor_scalar(mk.rowf2[0:1, 44:45], mk.rowf2[0:1, 44:45], 1.0 / 127.0,
                               None, ALU.mult)
    sc = mk.em("vector", [tr], v2, sig=mk.vE)

    def b1(e):
        e.matmul(mk.pk[3][:, 0:1], mk.ones_row_f[:], mk.rowf2[0:1, 43:44], start=True, stop=True)
        return e.matmul(mk.pk[3][:, 1:2], mk.ones_row_f[:], mk.rowf2[0:1, 44:45], start=True, stop=True)
    bc = mk.em("tensor", [sc], b1, sig=mk.tE)

    def e1(e):
        e.tensor_copy(mk.misc_pp[:, 0:1], mk.pk[3][:, 0:1])
        return e.tensor_copy(mk.misc_pp[:, 1:2], mk.pk[3][:, 1:2])
    mk.ffn0_scales_ready = mk.em("vector", [bc], e1, sig=mk.vE)


def ph_attn(mk, layer):
    xnv = mk.xnb[:, :].rearrange("p (c t) -> p c t", c=NCH)
    bar = mk.barrier()
    if layer == 0:
        bar = [mk.wq0b_done, mk.pool_entry_t, mk.xn_ready_s]
    ksb = [mk.pb(0, 4096), mk.pb(8192, 4096)]
    vsb = [mk.pb(16384, 4096), mk.pb(24576, 4096)]
    P = [mk.pb(32768 + 1024 * i, 512) for i in range(4)]
    qsb = [mk.pb(36864, 1024), mk.pb(38912, 1024)]
    qw = [mk.pb(40960, 2048), mk.pb(47104, 2048)]
    qsq = mk.pb(45056, 1024)
    sbc = mk.pf(0, 1024)

    if layer == 0:
        kv_of = lambda h: h
        kviews = [t.ap().rearrange("(r h p) k -> r h p k", r=4, h=4) for t in mk.kout0g]
        vviews = [t.ap().rearrange("(r t p) f -> r p t f", r=4, t=8) for t in mk.vout0g]
        kview_of = lambda h: kviews[h // 4][:, h % 4]
        vview_of = lambda h: vviews[h // 4][:, :, :, (h % 4) * 128:(h % 4 + 1) * 128]
        ccK_of = lambda h: mk.cc_agk0g[h // 4]
        ccV_of = lambda h: mk.cc_agv0g[h // 4]
        kk_slot = lambda h: 1 + h
        bias_q = lambda h: mk.bias0[:, h:h + 1]
        ar_ld = mk.ar0_row_loaded
    else:
        kv_of = lambda h: h // 4
        kview1 = mk.kout1.ap().rearrange("(r h p) k -> r h p k", r=4, h=4)
        vview1 = mk.vout1.ap().rearrange("(r t p) f -> r p t f", r=4, t=8)
        kview_of = lambda h: kview1[:, h // 4]
        vview_of = lambda h: vview1[:, :, :, (h // 4) * 128:(h // 4 + 1) * 128]
        ccK_of = lambda h: mk.cc_agk1
        ccV_of = lambda h: mk.cc_agv1
        kk_slot = lambda h: 32 + h // 4
        bias_q = lambda h: mk.bias1[:, h:h + 1]
        ar_ld = mk.ar1_row_loaded

    st = {}

    def prep_loads(h):
        kvh = kv_of(h)
        kb = (h % 2) if layer == 0 else (kvh % 2)
        new_kv = (layer == 0) or (h % 4 == 0)
        if new_kv:
            entry = [] if ("kfree", kb) in st else bar
            kview = kview_of(h)
            vview = vview_of(h)
            kload = None
            for r in range(4):
                kload = mk.dma("sync", ksb[kb][:, r * 1024:(r + 1) * 1024],
                               kview[r], [ccK_of(h), st.get(("kfree", kb))] + entry)
            vload = None
            for r in range(4):
                dst = vsb[kb][:, r * 1024:(r + 1) * 1024].rearrange("p (t d) -> p t d", t=8)
                vload = mk.dma("sync", dst, vview[r],
                               [ccV_of(h), st.get(("vfree", kb))] + entry)
            st[("kload", kb)] = kload
            st[("vload", kb)] = vload
        qsl = lambda w: w.ap().rearrange("(c p) o -> p c o", p=128)[:, :, h * 128:(h + 1) * 128]
        entry = [] if h >= 2 else bar
        qfree = st.get(("qw_used", h % 2))
        if layer == 0:
            st[("qwl", h % 2)] = mk.dma("sync", qw[h % 2], qsl(mk.wq0b),
                                        [mk.wq0b_done, qfree,
                                         mk.xq_spilled if h == 1 else None] + entry)
        else:
            st[("qwl", h % 2)] = mk.dma("gpsimd", qw[h % 2], qsl(mk.wq1), [qfree] + entry)
            mk.flush("gpsimd")
        mk.flush("sync")

    def prep_qproj(h):
        def qfn(e, h=h):
            last = None
            for th in range(2):
                for ic in range(16):
                    last = e.matmul(mk.pk[6 + th][:], qw[h % 2][:, ic * 128:(ic + 1) * 128],
                                    xnv[:, ic, th * 512:(th + 1) * 512],
                                    start=(ic == 0), stop=(ic == 15))
            return last
        qmm = mk.em("tensor", [st[("qwl", h % 2)], st.get("qq_read"),
                               st.get("sbe_tok")], qfn, sig=mk.tE)
        st[("qw_used", h % 2)] = qmm
        st["qmm"] = qmm

    def prep_qev(h):
        def qev(e, h=h):
            e.activation(qsb[h % 2][:, 0:512], mk.pk[6][:], AF.Identity, bias=bias_q(h))
            return e.activation(qsb[h % 2][:, 512:1024], mk.pk[7][:], AF.Identity,
                                bias=bias_q(h))
        st[("qe", h % 2)] = mk.em("scalar", [st["qmm"], st.get(("qfree", h % 2))],
                                  qev, sig=mk.sE)

    def prep_bound(h):
        qe = st[("qe", h % 2)]
        qq1 = mk.em("vector", [qe, st.get("qsq_read")],
                    lambda e, h=h: e.tensor_tensor(qsq, qsb[h % 2], qsb[h % 2], ALU.mult),
                    sig=mk.vE)

        def qqm(e):
            e.matmul(mk.prow[6][:], mk.ones_col_b[:], qsq[:, 0:512], start=True, stop=True)
            return e.matmul(mk.prow[7][:], mk.ones_col_b[:], qsq[:, 512:1024], start=True, stop=True)
        qq2 = mk.em("tensor", [qq1, qe], qqm, sig=mk.tE)
        st["qsq_read"] = qq2

        def qq3(e, h=h):
            e.tensor_reduce(mk.rowf2[0:1, 45:46], mk.prow[6][:], AXX, ALU.max)
            e.tensor_reduce(mk.rowf2[0:1, 46:47], mk.prow[7][:], AXX, ALU.max)
            e.drain()
            e.tensor_tensor(mk.rowf2[0:1, 45:46], mk.rowf2[0:1, 45:46],
                            mk.rowf2[0:1, 46:47], ALU.max)
            e.drain()
            e.tensor_tensor(mk.rowf2[0:1, 45:46], mk.rowf2[0:1, 45:46],
                            mk.rowf2[0:1, kk_slot(h):kk_slot(h) + 1], ALU.add)
            e.drain()
            return e.tensor_scalar(mk.rowf2[0:1, 47:48], mk.rowf2[0:1, 45:46],
                                   NEG_HALF_ISH, None, ALU.mult)
        qq4 = mk.em("vector", [qq2, ar_ld], qq3, sig=mk.vE)
        nb1 = mk.em("tensor", [qq4],
                    lambda e: e.matmul(mk.pk[6][:, 0:1], mk.ones_row_f[:],
                                       mk.rowf2[0:1, 47:48], start=True, stop=True), sig=mk.tE)
        nb2 = mk.em("vector", [nb1],
                    lambda e, h=h: e.tensor_copy(mk.negb[:, h:h + 1], mk.pk[6][:, 0:1]),
                    sig=mk.vE)
        st["qq_read"] = nb2
        st[("nb", h % 2)] = nb2

    prep_loads(0)
    prep_qproj(0)
    prep_qev(0)
    prep_bound(0)

    for h in range(NH):
        kvh = kv_of(h)
        kb = (h % 2) if layer == 0 else (kvh % 2)
        kload, vload = st[("kload", kb)], st[("vload", kb)]
        qe, nb2 = st[("qe", h % 2)], st[("nb", h % 2)]

        s_snap, e_snap, p_snap = {}, {}, {}
        srow = [mk.pk[5][0:1, :], mk.pk[5][32:33, :]]

        def emit_S(i):
            kt, qh = i // 2, i % 2

            def sfn(e, kt=kt, qh=qh, kb=kb, h=h, i=i):
                return e.matmul(mk.pk[i % 3][:], ksb[kb][:, kt * 128:(kt + 1) * 128],
                                qsb[h % 2][:, qh * 512:(qh + 1) * 512],
                                start=True, stop=True)
            sw = [e_snap.get(i - 3)]
            if i < 3:
                sw += [kload, qe, st.get(("exp_tail", i))]
            s_snap[i] = mk.em("tensor", sw, sfn, sig=mk.tE)

        def emit_exp(i):
            def efn(e, i=i, h=h):
                return e.activation(P[i % 4][:], mk.pk[i % 3][:], AF.Exp,
                                    bias=mk.negb[:, h:h + 1], scale=ISH)
            ew = [s_snap[i], p_snap.get(i - 4)]
            if i == 0:
                ew.append(nb2)
            e_snap[i] = mk.em("scalar", ew, efn, sig=mk.sE)

        def emit_PV(i):
            kt, qh = i // 2, i % 2

            def pfn(e, kt=kt, qh=qh, i=i, kb=kb):
                e.matmul(mk.pk[3 + qh][:], vsb[kb][:, kt * 128:(kt + 1) * 128],
                         P[i % 4][:], start=(kt == 0), stop=(kt == 31))
                return e.matmul(srow[qh], mk.ones_col_b[:],
                                P[i % 4][:], start=(kt == 0), stop=(kt == 31))
            pw = [e_snap[i]]
            if kt == 0:
                pw += [vload, st.get("pv_free"), st.get("sums_free")]
            p_snap[i] = mk.em("tensor", pw, pfn, sig=mk.tE)

        emit_S(0)
        emit_exp(0)
        for i in range(1, 64):
            emit_S(i)
            emit_exp(i)
            emit_PV(i - 1)

            if h + 1 < NH:
                if i == 16:
                    prep_loads(h + 1)
                elif i == 24:
                    prep_qproj(h + 1)
                elif i == 28:
                    prep_qev(h + 1)
                elif i == 32:
                    prep_bound(h + 1)
            if h >= 2 and i in (8, 20, 36, 52) and mk.bg_tasks:
                mk.bg_tasks.popleft()()
        emit_PV(63)

        def sg1(e):
            e.tensor_copy(mk.rowf[0:1, 1024:1536], srow[0])
            e.tensor_copy(mk.rowf[0:1, 1536:2048], srow[1])
            e.drain()
            return e.reciprocal(mk.rowf[0:1, 1024:2048], mk.rowf[0:1, 1024:2048])
        sg = mk.em("vector", [p_snap[63]], sg1, sig=mk.vE)
        st["sums_free"] = sg

        def sb1(e):
            e.matmul(mk.pk[6][:], mk.ones_row_f[:], mk.rowf[0:1, 1024:1536],
                     start=True, stop=True)
            return e.matmul(mk.pk[7][:], mk.ones_row_f[:], mk.rowf[0:1, 1536:2048],
                            start=True, stop=True)
        sbm = mk.em("tensor", [sg], sb1, sig=mk.tE)

        def sb2(e):
            e.tensor_copy(sbc[:, 0:512], mk.pk[6][:])
            return e.tensor_copy(sbc[:, 512:1024], mk.pk[7][:])
        sbe = mk.em("vector", [sbm], sb2, sig=mk.vE)
        st["sbe_tok"] = sbe

        def nrm(e, h=h):
            e.tensor_tensor(mk.att[:, h * TPC:h * TPC + 512], mk.pk[3][:],
                            sbc[:, 0:512], ALU.mult)
            return e.tensor_tensor(mk.att[:, h * TPC + 512:(h + 1) * TPC], mk.pk[4][:],
                                   sbc[:, 512:1024], ALU.mult)
        nr = mk.em("vector", [sbe], nrm, sig=mk.vE)
        st["pv_free"] = nr
        st[("kfree", kb)] = s_snap[63]
        st[("vfree", kb)] = p_snap[63]
        st[("qfree", h % 2)] = s_snap[63]
        st[("exp_tail", 0)] = e_snap[63]
        st[("exp_tail", 1)] = e_snap[61]
        st[("exp_tail", 2)] = e_snap[62]
    mk.attn_done = mk.vE.snap()


def ph_outproj(mk, layer):
    attv = mk.att[:, :].rearrange("p (c t) -> p c t", c=NCH)
    bar = mk.barrier()
    xv = mk.x[:, :].rearrange("p (c t) -> p c t", c=NCH)
    wo_sb = mk.pb(0, 2048)
    st = {}
    for oc in range(16):
        if layer == 0:
            jit = _wjit(mk, mk.wo0c, mk.so0, oc * 128, 128, wo_sb, pf_b=True,
                        extra_waits=
                        [st.get("wo_used")] + (bar if oc == 0 else []))
        else:
            wsl = mk.wo1.ap().rearrange("(c p) o -> p c o", p=128)[:, :, oc * 128:(oc + 1) * 128]
            jit = mk.dma("gpsimd", wo_sb, wsl,
                         [st.get("wo_used")] + (bar if oc == 0 else []))

        def mfn(e):
            last = None
            for th in range(2):
                for ic in range(16):
                    last = e.matmul(mk.pk[th][:], wo_sb[:, ic * 128:(ic + 1) * 128],
                                    attv[:, ic, th * 512:(th + 1) * 512],
                                    start=(ic == 0), stop=(ic == 15))
            return last
        mm = mk.em("tensor", [jit, st.get("stt"),
                              mk.attn_done if oc == 0 else None], mfn, sig=mk.tE)
        st["wo_used"] = mm
        bcol = (mk.bias0[:, 48 + oc:49 + oc] if layer == 0
                else mk.bias1[:, 32 + oc:33 + oc])

        def afn(e, oc=oc, bcol=bcol):
            e.scalar_tensor_tensor(xv[:, oc, 0:512], mk.pk[0][:], bcol,
                                   xv[:, oc, 0:512], ALU.add, ALU.add)
            return e.scalar_tensor_tensor(xv[:, oc, 512:1024], mk.pk[1][:], bcol,
                                          xv[:, oc, 512:1024], ALU.add, ALU.add)
        st["stt"] = mk.em("vector", [mm], afn, sig=mk.vE)
    mk.x_after_attn = mk.vE.snap()
    mk.outproj_t = mk.tE.snap()


def ph_ffn0mm(mk):
    xv = mk.x[:, :].rearrange("p (c t) -> p c t", c=NCH)
    bar = mk.barrier()
    xqv = mk.xq_sp.ap().rearrange("(c p) t -> p c t", p=128)
    xnb3 = mk.xnb[:, :].rearrange("p (c t) -> p c t", c=NCH)
    xqld = mk.dma("sync", xnb3, xqv, [mk.xq_spilled, mk.outproj_t])
    st = {}
    w0qb_done = mk.w0qb_done
    wv = mk.w0qb.ap().rearrange("(c p) o -> p c o", p=128)
    for otp in range(8):
        wq_sb = mk.pb(8192 + (otp % 2) * 8192, 16 * 256)
        ld = mk.dma("sync", wq_sb, wv[:, :, otp * 256:(otp + 1) * 256],
                    [w0qb_done, st.get(("used", otp % 2))] + (bar if otp == 0 else []))
        for oc2 in range(2):
            oc = otp * 2 + oc2

            def mfn(e, wq_sb=wq_sb, oc2=oc2):
                last = None
                for th in range(2):
                    for ic in range(16):
                        last = e.matmul(mk.pk[th][:],
                                        wq_sb[:, ic * 256 + oc2 * 128:ic * 256 + oc2 * 128 + 128],
                                        xnb3[:, ic, th * 512:(th + 1) * 512],
                                        start=(ic == 0), stop=(ic == 15))
                return last
            mm = mk.em("tensor", [ld, xqld, st.get("stt2")], mfn, sig=mk.tE)
            st[("used", otp % 2)] = mm

            def afn(e, oc=oc):
                e.scalar_tensor_tensor(xv[:, oc, 0:512], mk.pk[0][:], mk.misc_pp[:, 1:2],
                                       xv[:, oc, 0:512], ALU.mult, ALU.add)
                return e.scalar_tensor_tensor(xv[:, oc, 512:1024], mk.pk[1][:],
                                              mk.misc_pp[:, 1:2],
                                              xv[:, oc, 512:1024], ALU.mult, ALU.add)
            st["stt2"] = mk.em("vector", [mm], afn, sig=mk.vE)
    mk.ln1_start = mk.vE.snap()
    mk.ln1_start_t = mk.tE.snap()


def ph_kvproj1(mk):
    xnv = mk.xnb[:, :].rearrange("p (c t) -> p c t", c=NCH)
    bar = mk.barrier()
    t0 = mk.tE.snap()
    v0 = mk.vE.snap()
    wk_sb = mk.pb(0, 16 * 512)
    wv_sb = mk.pb(16384, 16 * 512)
    wkl = mk.dma("gpsimd", wk_sb,
                 mk.wk1.ap().rearrange("(c p) o -> p c o", p=128), bar)
    wvl = mk.dma("gpsimd", wv_sb,
                 mk.wv1.ap().rearrange("(c p) o -> p c o", p=128), bar)
    bvs1 = mk.dma("sync", mk.PF[0:1, 0:512], mk.c_bvrow1.ap(), bar)
    cv1 = mk.em("vector", [bvs1],
                lambda e: e.tensor_copy(mk.PB[0:1, 19968:20480], mk.PF[0:1, 0:512]), sig=mk.vE)
    zr = mk.em("vector", [], lambda e: e.memset(mk.rowf2[0:1, 48:56], 0.0), sig=mk.vE)
    ksq = mk.pb(38912, 1024) if False else mk.pb(40960, 1024)
    kt_off = (32768, 34816)
    st = {}
    for h in range(4):
        ktile = mk.pb(kt_off[h % 2], 1024)

        def mfn(e, h=h):
            last = None
            for th in range(2):
                for ic in range(16):
                    last = e.matmul(mk.pk[th][:],
                                    wk_sb[:, ic * 512 + h * 128:ic * 512 + h * 128 + 128],
                                    xnv[:, ic, th * 512:(th + 1) * 512],
                                    start=(ic == 0), stop=(ic == 15))
            return last
        mm = mk.em("tensor", [wkl, mk.xn_ready_v, st.get(("ev", h % 2))], mfn, sig=mk.tE)

        def efn(e, h=h, ktile=ktile):
            e.activation(ktile[:, 0:512], mk.pk[0][:], AF.Identity, bias=mk.bias1[:, 16 + h:17 + h])
            return e.activation(ktile[:, 512:1024], mk.pk[1][:], AF.Identity,
                                bias=mk.bias1[:, 16 + h:17 + h])
        ev = mk.em("scalar", [mm, st.get(("dma", h % 2))], efn, sig=mk.sE)
        st[("ev", h % 2)] = ev
        sqv = mk.em("vector", [ev, st.get("knorm_mm"), zr],
                    lambda e, ktile=ktile: e.tensor_tensor(ksq, ktile, ktile, ALU.mult),
                    sig=mk.vE)

        def nfn(e):
            e.matmul(mk.prow[6][:], mk.ones_col_b[:], ksq[:, 0:512], start=True, stop=True)
            return e.matmul(mk.prow[7][:], mk.ones_col_b[:], ksq[:, 512:1024], start=True, stop=True)
        nm = mk.em("tensor", [sqv, st.get("knorm_rd")], nfn, sig=mk.tE)
        st["knorm_mm"] = nm

        def rfn(e, h=h):
            e.tensor_reduce(mk.rowf2[0:1, 48 + h:49 + h], mk.prow[6][:], AXX, ALU.max)
            e.tensor_reduce(mk.rowf2[0:1, 60:61], mk.prow[7][:], AXX, ALU.max)
            e.drain()
            return e.tensor_tensor(mk.rowf2[0:1, 48 + h:49 + h], mk.rowf2[0:1, 48 + h:49 + h],
                                   mk.rowf2[0:1, 60:61], ALU.max)
        st["knorm_rd"] = mk.em("vector", [nm], rfn, sig=mk.vE)
        st[("dma", h % 2)] = mk.dma("gpsimd", mk.kin1.ap()[h * 128:(h + 1) * 128, :],
                                    ktile, [ev])
    mk.kin1_done = mk.dG.snap()
    mk.knorms1_done = mk.vE.snap()
    g4 = [[0, 1, 2, 3], [4, 5, 6, 7]]
    g8 = [list(range(8))]
    mk.em("gpsimd", [mk.kin1_done],
          lambda e: e.collective_compute("AllGather", ALU.bypass, replica_groups=g4,
                                         ins=[mk.kin1.ap().opt()], outs=[mk.kout1.ap().opt()]),
          sig=mk.cc)
    mk.cc_agk1 = mk.cc.snap()
    arld = mk.dma("gpsimd", mk.ar1_in.ap(), mk.rowf2[0:1, 48:56], [mk.knorms1_done])
    mk.em("gpsimd", [arld],
          lambda e: e.collective_compute("AllReduce", ALU.max, replica_groups=g8,
                                         ins=[mk.ar1_in.ap().opt()], outs=[mk.ar1_out.ap().opt()]),
          sig=mk.cc)
    mk.cc_ar1 = mk.cc.snap()

    vt_off = (36864, 37888)
    for tt in range(8):
        def vfn(e, tt=tt):
            e.matmul(mk.pk[2 + tt % 2][:], mk.ones_row_b[:], mk.PB[0:1, 19968:20480],
                     start=True, stop=False)
            last = None
            for ic in range(16):
                last = e.matmul(mk.pk[2 + tt % 2][:],
                                xnv[:, ic, tt * 128:(tt + 1) * 128],
                                wv_sb[:, ic * 512:(ic + 1) * 512],
                                start=False, stop=(ic == 15))
            return last
        mm = mk.em("tensor", [wvl, cv1, st.get(("vev", tt % 2))], vfn, sig=mk.tE)
        vtile = mk.pb(vt_off[tt % 2], 512)
        ev = mk.em("scalar", [mm, st.get(("vdma", tt % 2))],
                   lambda e, vtile=vtile, tt=tt: e.activation(vtile, mk.pk[2 + tt % 2][:], AF.Copy),
                   sig=mk.sE)
        st[("vev", tt % 2)] = ev
        st[("vdma", tt % 2)] = mk.dma("gpsimd", mk.vin1.ap()[tt * 128:(tt + 1) * 128, :],
                                      vtile, [ev])
    mk.vin1_done = mk.dG.snap()
    mk.em("gpsimd", [mk.vin1_done],
          lambda e: e.collective_compute("AllGather", ALU.bypass, replica_groups=g4,
                                         ins=[mk.vin1.ap().opt()], outs=[mk.vout1.ap().opt()]),
          sig=mk.cc)
    mk.cc_agv1 = mk.cc.snap()
    mk.ar1_row_loaded = mk.dma("sync", mk.rowf2[0:1, 32:40], mk.ar1_out.ap(), [mk.cc_ar1])
    mk.pool_entry_t = mk.tE.snap()
    mk.pool_entry_v = mk.vE.snap()


def ph_ffn1(mk):
    xnv = mk.xnb[:, :].rearrange("p (c t) -> p c t", c=NCH)
    bar = mk.barrier()
    xv = mk.x[:, :].rearrange("p (c t) -> p c t", c=NCH)
    w1v = mk.w1T.ap().rearrange("(c p) o -> p c o", p=128)
    w3v = mk.w3T.ap().rearrange("(c p) o -> p c o", p=128)
    w1bv = mk.w1b.ap().rearrange("(c p) o -> p c o", p=128)
    w3bv = mk.w3b.ap().rearrange("(c p) o -> p c o", p=128)
    w2v = mk.w2T.ap().rearrange("(c p) o -> p c o", p=128)
    w2bv = mk.w2b.ap().rearrange("(c p) o -> p c o", p=128)
    w1t = [mk.pb(0, 2048), mk.pb(4096, 2048)]
    w3t = [mk.pb(8192, 2048), mk.pb(12288, 2048)]
    t1 = mk.pb(16384, 512)
    hpre = mk.pb(17408, 512)
    tmpt = mk.pb(18432, 512)
    f1_t0 = mk.tE.snap()
    f1_v0 = mk.vE.snap()
    last_dma = None
    bf_tmps = [mk.pf(4096 + 2048 * j, 512) for j in range(4)]
    for p in range(2):
        st = {}
        bf_applied = {}
        bf_state = {"last": None}

        def bf_ready(limit, bf_applied=bf_applied):
            out = []
            touched = set()
            for c in sorted(bf_applied):
                if len(out) >= limit:
                    break
                k = bf_applied[c]
                if k >= 6 or c in touched:
                    continue
                pp = c ^ (1 << k)
                if pp > c and pp not in touched and bf_applied.get(pp) == k:
                    out.append((c, pp, k))
                    touched.add(c)
                    touched.add(pp)
            for c, pp, k in out:
                bf_applied[c] += 1
                bf_applied[pp] += 1
            return out

        def bf_emit(limit, st=st, bf_applied=bf_applied, bf_state=bf_state):
            while True:
                batch = bf_ready(min(limit, 4))
                if not batch:
                    return
                waits = [st[("hevc", c)] for c, pp, k in batch if k == 0] +                         [st[("hevc", pp)] for c, pp, k in batch if k == 0]

                def bfb(e, batch=batch):
                    e.drain()
                    for j, (aa, bb, k) in enumerate(batch):
                        e.tensor_tensor(bf_tmps[j], mk.hid(aa), mk.hid(bb), ALU.subtract)
                        e.tensor_tensor(mk.hid(aa), mk.hid(aa), mk.hid(bb), ALU.add)
                    e.drain()
                    last = None
                    for j, (aa, bb, k) in enumerate(batch):
                        last = e.tensor_copy(mk.hid(bb), bf_tmps[j])
                    return last
                bf_state["last"] = mk.em("vector", waits, bfb, sig=mk.vE)
                limit -= len(batch)
                if limit <= 0:
                    return

        def emit_h128(ht2, st=st):
            r2 = ht2 % 2
            pk_h2 = mk.pk[2] if r2 == 0 else mk.pk[5]
            hm = mk.em("tensor", [st["hp"], st.get(("hev", r2))],
                       lambda e, pk_h2=pk_h2: e.matmul(pk_h2[:], mk.h128[:], hpre[:],
                                                       start=True, stop=True),
                       sig=mk.tE)
            st["h128_last"] = hm
            hev = mk.em("scalar", [hm],
                        lambda e, ht2=ht2, pk_h2=pk_h2: e.activation(mk.hid(ht2), pk_h2[:],
                                                                     AF.Copy),
                        sig=mk.sE)
            st[("hev", r2)] = hev
            st[("hevc", ht2)] = hev
            bf_applied[ht2] = 0

        for ht in range(HCH):
            r = ht % 2
            if ht < 2:
                first = bar + [mk.x_after_attn] if p == 0 else [f1_t0, f1_v0]
            else:
                first = []
            w1l = mk.dma("gpsimd", w1t[r], w1bv[:, :, ht * 128:(ht + 1) * 128],
                         [st.get(("ug", r))] + first)
            w3l = mk.dma("sync", w3t[r], w3bv[:, :, ht * 128:(ht + 1) * 128],
                         [st.get(("ul", r))] + first)

            pk_ug, pk_ul, pk_h = (mk.pk[0], mk.pk[1], mk.pk[2]) if r == 0 else \
                                 (mk.pk[3], mk.pk[4], mk.pk[5])

            def ugf(e, r=r, p=p, pk_ug=pk_ug):
                last = None
                for ic in range(16):
                    last = e.matmul(pk_ug[:], w1t[r][:, ic * 128:(ic + 1) * 128],
                                    xnv[:, ic, p * 512:(p + 1) * 512],
                                    start=(ic == 0), stop=(ic == 15))
                return last
            ug = mk.em("tensor", [w1l, st.get(("silu", r))], ugf, sig=mk.tE)
            st[("ug", r)] = ug

            def ulf(e, r=r, p=p, pk_ul=pk_ul):
                last = None
                for ic in range(16):
                    last = e.matmul(pk_ul[:], w3t[r][:, ic * 128:(ic + 1) * 128],
                                    xnv[:, ic, p * 512:(p + 1) * 512],
                                    start=(ic == 0), stop=(ic == 15))
                return last
            ul = mk.em("tensor", [w3l, st.get(("hpre", r))], ulf, sig=mk.tE)
            st[("ul", r)] = ul

            if ht > 0:
                emit_h128(ht - 1)

            silu = mk.em("scalar", [ug, st.get("hp")],
                         lambda e, pk_ug=pk_ug: e.activation(t1[:], pk_ug[:], AF.Silu),
                         sig=mk.sE)
            st[("silu", r)] = silu
            hp = mk.em("vector", [silu, ul, st.get("h128_last")],
                       lambda e, pk_ul=pk_ul: e.tensor_tensor(hpre[:], t1[:], pk_ul[:], ALU.mult),
                       sig=mk.vE)
            st[("hpre", r)] = hp
            st["hp"] = hp
            bf_emit(4)
        emit_h128(HCH - 1)
        bf_emit(10 ** 9)
        hid_fwht = bf_state["last"]

        w2t = [mk.pb(0, 4096), mk.pb(8192, 4096)]
        osum = mk.pf(0, 512)
        ott = mk.pf(2048 // 4, 512)
        st2 = {}
        for ot in range(NCH):
            mms = None
            for half in range(2):
                r = half
                first = [f1_t0, hid_fwht] if ot == 0 else []
                w2l = mk.dma("gpsimd" if half == 0 else "sync", w2t[r],
                             w2bv[:, half * 32:(half + 1) * 32, ot * 128:(ot + 1) * 128],
                             [st2.get(("mm", r))] + first)

                def mfn(e, half=half, r=r, ot=ot):
                    last = None
                    for hc in range(32):
                        last = e.matmul(mk.pk[6][:], w2t[r][:, hc * 128:(hc + 1) * 128],
                                        mk.hid(half * 32 + hc),
                                        start=(half == 0 and hc == 0),
                                        stop=(half == 1 and hc == 31))
                    return last
                mms = mk.em("tensor", [w2l, hid_fwht if ot == 0 else None,
                                       st2.get("stt") if half == 0 else None], mfn,
                            sig=mk.tE)
                st2[("mm", r)] = mms

            def sfn(e, ot=ot, p=p):
                return e.scalar_tensor_tensor(osum[:], mk.pk[6][:], INV_SQRT_HFF,
                                              xv[:, ot, p * 512:(p + 1) * 512],
                                              ALU.mult, ALU.add)
            stt = mk.em("vector", [mms, st2.get("oev")], sfn, sig=mk.vE)
            st2["stt"] = stt

            def tfn(e):
                last = None
                for tt in range(4):
                    last = e.transpose(mk.pk[7][:, tt * 128:(tt + 1) * 128],
                                       osum[:, tt * 128:(tt + 1) * 128], mk.idf[:])
                return last
            tr = mk.em("tensor", [stt, st2.get("oev")], tfn, sig=mk.tE)

            oev = mk.em("vector", [tr, st2.get("odma")],
                        lambda e: e.tensor_copy(ott[:], mk.pk[7][:]), sig=mk.vE)
            st2["oev"] = oev
            odst = mk.out.ap().rearrange("(t p) d -> p t d", p=128)[
                :, p * 4:(p + 1) * 4, ot * 128:(ot + 1) * 128]
            st2["odma"] = mk.dma("gpsimd", odst,
                                 ott[:].rearrange("p (t d) -> p t d", t=4), [oev])
        last_dma = st2["odma"]
        f1_t0 = mk.tE.snap()
        f1_v0 = mk.vE.snap()
    mk.em("gpsimd", [mk.dG.snap()], lambda e: None)


def ph_tern_tasks(mk):
    aslot = [mk.pf(4096, 1024), mk.pf(8192, 1024)]
    tb = mk.pb(51200, 1024)
    bar = mk.barrier()
    st = mk._tern_st = {}

    def task(it, hf):
        def run(it=it, hf=hf):
            r = (it * 2 + hf) % 2
            a = aslot[r]
            la = mk.dma("sync", a,
                        mk.w0T.ap()[it * 128:(it + 1) * 128, hf * 1024:(hf + 1) * 1024],
                        [st.get(("rd", r))] + ([] if st else bar))

            def tfn(e, a=a):
                e.tensor_scalar(a, a, mk.misc_pp[:, 0:1], None, ALU.mult)
                e.drain()
                e.tensor_scalar(a, a, 1.0, -1.0, ALU.min, ALU.max)
                e.drain()
                return e.tensor_scalar(tb, a, MAGIC, -MAGIC, ALU.add, ALU.add)
            tern = mk.em("vector", [la, mk.ffn0_scales_ready, st.get("spill")],
                         tfn, sig=mk.vE)
            st[("rd", r)] = tern
            st["spill"] = mk.dma(
                "sync", mk.w0qb.ap()[it * 128:(it + 1) * 128, hf * 1024:(hf + 1) * 1024],
                tb, [tern])
        return run

    for it in range(16):
        for hf in range(2):
            mk.bg_tasks.append(task(it, hf))


def ph_wcast(mk):
    for i in range(8):
        mk.dma("gpsimd", mk.w1b.ap()[i * 256:(i + 1) * 256, :],
               mk.w1T.ap()[i * 256:(i + 1) * 256, :],
               [mk.cc_agv0] if i == 0 else [])
        mk.dma("gpsimd", mk.w3b.ap()[i * 256:(i + 1) * 256, :],
               mk.w3T.ap()[i * 256:(i + 1) * 256, :], [])
        mk.dma("gpsimd", mk.w2b.ap()[i * 1024:(i + 1) * 1024, :],
               mk.w2T.ap()[i * 1024:(i + 1) * 1024, :], [])


def ph_epilogue_dev(mk):
    xv = mk.x[:, :].rearrange("p (c t) -> p c t", c=NCH)
    bar = mk.barrier()
    t0 = mk.tE.snap()
    v0 = mk.vE.snap()
    ott = mk.pf(0, 1024)
    st = {}
    for ot in range(NCH):
        def tfn(e, ot=ot):
            last = None
            for tt in range(8):
                bank = mk.pk[2 + tt // 4]
                last = e.transpose(bank[:, (tt % 4) * 128:(tt % 4 + 1) * 128],
                                   xv[:, ot, tt * 128:(tt + 1) * 128], mk.idf[:])
            return last
        tr = mk.em("tensor", [mk.x_final, st.get("oev")] + (bar if ot == 0 else []),
                   tfn, sig=mk.tE)

        def oef(e):
            e.tensor_copy(ott[:, 0:512], mk.pk[2][:])
            return e.tensor_copy(ott[:, 512:1024], mk.pk[3][:])
        oev = mk.em("vector", [tr, st.get("odma")] + (bar if ot == 0 else []), oef, sig=mk.vE)
        st["oev"] = oev
        odst = mk.out.ap().rearrange("(t p) d -> p t d", p=128)[:, :, ot * 128:(ot + 1) * 128]
        st["odma"] = mk.dma("gpsimd", odst,
                            ott[:, :].rearrange("p (t d) -> p t d", t=8), [oev])
    mk.em("gpsimd", [mk.dG.snap()], lambda e: None)



_CACHE = {}


def build(stage=STAGE):
    if stage in _CACHE:
        return _CACHE[stage]
    mk = MK(stage)
    ph_prologue(mk)
    ph_ln(mk, 0)
    ph_quantpre(mk)
    ph_wq0prep_tasks(mk)
    pa = ph_passA_tasks(mk)
    mixed = []
    wq = list(mk.bg_tasks)
    mk.bg_tasks.clear()
    for i in range(64):
        mixed.append(wq[i])
        if i % 2 == 0:
            mixed.append(pa[i // 2])
    mixed += pa[32:]
    mk.bg_tasks.extend(mixed)
    ph_kvproj0(mk)
    while mk.bg_tasks:
        mk.bg_tasks.popleft()()
    mk.wq0b_done = mk.dG.snap()
    mk.flush("gpsimd")
    mk.passA_acc = mk._pa_st["acc"]
    ph_quantfin(mk)
    ph_ffn0passA(mk)
    if stage >= 4:
        ph_wcast(mk)
    if stage >= 2:
        ph_tern_tasks(mk)
    ph_attn(mk, 0)
    while mk.bg_tasks:
        mk.bg_tasks.popleft()()
    mk.w0qb_done = mk.dS.snap()
    ph_outproj(mk, 0)
    if stage >= 2:
        ph_ffn0mm(mk)
    if stage >= 3:
        ph_ln(mk, 1)
        ph_kvproj1(mk)
        ph_attn(mk, 1)
        ph_outproj(mk, 1)
    if stage >= 4:
        ph_ffn1(mk)
    else:
        mk.x_final = mk.vE.snap()
        ph_epilogue_dev(mk)
    nc = mk.finalize()
    _CACHE[stage] = nc
    return nc


def _consts():
    bf = ml_dtypes.bfloat16
    i = np.arange(128)
    had = np.where((np.bitwise_count(i[:, None] & i[None, :]) % 2) == 0, 1.0, -1.0)
    return {
        "c_idf": np.eye(128, dtype=np.float32),
        "c_ones_col_b": np.ones((128, 1), bf),
        "c_ones_row_b": np.ones((1, 128), bf),
        "c_ones_row_f": np.ones((1, 128), np.float32),
        "c_h128": had.astype(bf),
    }


def _fm_pack(vec):
    n = np.asarray(vec).shape[0] // 128
    return np.ascontiguousarray(np.asarray(vec, np.float32).reshape(n, 128).T)


def _prep_inputs(inp):
    x = np.asarray(inp["L0_attn_x"], np.float32)
    cst = _consts()
    rep = lambda s: np.ascontiguousarray(np.repeat(np.asarray(s, np.float32), 32, axis=0))
    tr = lambda w: np.ascontiguousarray(np.asarray(w, np.float32).T)
    bias0 = np.zeros((128, 64), np.float32)
    bias0[:, 0:16] = _fm_pack(inp["L0_attn_bq"])
    bias0[:, 16:32] = _fm_pack(inp["L0_attn_bk"])
    bias0[:, 32:48] = _fm_pack(inp["L0_attn_bv"])
    bias0[:, 48:64] = _fm_pack(inp["L0_attn_bo"])
    bias1 = np.zeros((128, 48), np.float32)
    bias1[:, 0:16] = _fm_pack(inp["L1_attn_bq"])
    bias1[:, 16:20] = _fm_pack(inp["L1_attn_bk"])
    bias1[:, 32:48] = _fm_pack(inp["L1_attn_bo"])
    shared = dict(
        wq0c=np.asarray(inp["L0_attn_wq_codes"], np.float32),
        wk0c=np.asarray(inp["L0_attn_wk_codes"], np.float32),
        wv0c=np.asarray(inp["L0_attn_wv_codes"], np.float32),
        wo0c=np.asarray(inp["L0_attn_wo_codes"], np.float32),
        sq0=rep(inp["L0_attn_wq_scales"]), sk0=rep(inp["L0_attn_wk_scales"]),
        sv0=rep(inp["L0_attn_wv_scales"]), so0=rep(inp["L0_attn_wo_scales"]),
        w0T=tr(inp["L0_ffn_weight"]),
        wq1=np.asarray(inp["L1_attn_wq"], np.float32),
        wk1=np.asarray(inp["L1_attn_wk"], np.float32),
        wv1=np.asarray(inp["L1_attn_wv"], np.float32),
        wo1=np.asarray(inp["L1_attn_wo"], np.float32),
        w1T=tr(inp["L1_ffn_W1"]), w3T=tr(inp["L1_ffn_W3"]), w2T=tr(inp["L1_ffn_W2"]),
        c_bias0=bias0, c_bias1=bias1,
        c_gamma=_fm_pack(inp["L0_ffn_norm_weight"]),
        c_bvrow0=np.asarray(inp["L0_attn_bv"], np.float32).reshape(1, 2048),
        c_bvrow1=np.asarray(inp["L1_attn_bv"], np.float32).reshape(1, 512),
        **cst,
    )
    in_maps = []
    for c in range(8):
        b, tk = c // 4, (c % 4) * TPC
        m = dict(shared)
        m["x_in"] = np.ascontiguousarray(x[b, tk:tk + TPC, :])
        in_maps.append(m)
    return in_maps


def kernel(**inp):
    nc = build(STAGE)
    in_maps = _prep_inputs(inp)
    res = run_bass_kernel_spmd(nc, in_maps, core_ids=list(range(8)))
    shards = [res.results[c]["out"] for c in range(8)]
    return np.stack([np.concatenate(shards[0:4], 0),
                     np.concatenate(shards[4:8], 0)]).reshape(2, 4096, 2048)


# version 13
# speedup vs baseline: 1.0169x; 1.0137x over previous
import os
from contextlib import ExitStack

import numpy as np
import ml_dtypes

import concourse.bass as bass
import concourse.mybir as mybir
from concourse.bass_utils import run_bass_kernel_spmd

F32 = mybir.dt.float32
BF16 = mybir.dt.bfloat16
AF = mybir.ActivationFunctionType
ALU = mybir.AluOpType
AXX = mybir.AxisListType.X

MAGIC = 12582912.0
TPC = 1024
NCH = 16
D = 2048
SEQ = 4096
NH = 16
HFF = 8192
HCH = 64
ISH = float(1.0 / np.sqrt(128.0))
NEG_HALF_ISH = float(-0.5 / np.sqrt(128.0))
INV_SQRT_HFF = float(1.0 / np.sqrt(8192.0))
LN_EPS = 1e-5
RMS_EPS = 1e-6

STAGE = int(os.environ.get("MK_STAGE", "4"))


class Ctr:
    def __init__(self, sem, is_dma=False):
        self.sem = sem
        self.n = 0
        self.step = 16 if is_dma else 1

    def snap(self):
        return (self.sem, self.n)


class MK:
    def __init__(self, stage=STAGE):
        self.stage = stage
        self.nc = bass.Bass("TRN2", target_bir_lowering=False)
        self.es = ExitStack()
        self.streams = {k: [] for k in ("sync", "gpsimd", "tensor", "scalar", "vector")}
        from collections import deque
        self.bg_tasks = deque()
        self.wcast_tasks = deque()
        self._declare()
        self._alloc()

    def _declare(self):
        nc = self.nc
        dt = nc.dram_tensor
        I = lambda n, s, d=F32: dt(n, s, d, kind="ExternalInput")
        self.x_in = I("x_in", [TPC, D])
        self.wq0c = I("wq0c", [D, D]); self.wk0c = I("wk0c", [D, D])
        self.wv0c = I("wv0c", [D, D]); self.wo0c = I("wo0c", [D, D])
        self.sq0 = I("sq0", [D, D]); self.sk0 = I("sk0", [D, D])
        self.sv0 = I("sv0", [D, D]); self.so0 = I("so0", [D, D])
        self.w0T = I("w0T", [D, D])
        self.wq1 = I("wq1", [D, D]); self.wk1 = I("wk1", [D, 512])
        self.wv1 = I("wv1", [D, 512]); self.wo1 = I("wo1", [D, D])
        self.w1T = I("w1T", [D, HFF]); self.w3T = I("w3T", [D, HFF])
        self.w2T = I("w2T", [HFF, D])
        self.c_idf = I("c_idf", [128, 128])
        self.c_ones_col_b = I("c_ones_col_b", [128, 1], BF16)
        self.c_ones_row_b = I("c_ones_row_b", [1, 128], BF16)
        self.c_ones_row_f = I("c_ones_row_f", [1, 128])
        self.c_h128 = I("c_h128", [128, 128], BF16)
        self.c_bias0 = I("c_bias0", [128, 64])
        self.c_bias1 = I("c_bias1", [128, 48])
        self.c_gamma = I("c_gamma", [128, 16])
        self.c_bvrow0 = I("c_bvrow0", [1, 2048])
        self.c_bvrow1 = I("c_bvrow1", [1, 512])

        self.out = dt("out", [TPC, D], F32, kind="ExternalOutput")

        self.wq0b = dt("wq0b", [D, D], BF16)
        self.w0qb = dt("w0qb", [D, D], BF16)
        self.kin0 = dt("kin0", [2048, 1024], BF16)
        self.kout0g = [dt(f"kout0g{g}", [2048, 1024], BF16) for g in range(4)]
        self.vin0g = [dt(f"vin0g{g}", [1024, 512], BF16) for g in range(4)]
        self.vout0g = [dt(f"vout0g{g}", [4096, 512], BF16) for g in range(4)]
        self.kin1 = dt("kin1", [512, 1024], BF16)
        self.kout1 = dt("kout1", [2048, 1024], BF16)
        self.vin1 = dt("vin1", [1024, 512], BF16)
        self.vout1 = dt("vout1", [4096, 512], BF16)
        self.ar0_in = dt("ar0_in", [1, 32], F32)
        self.ar0_out = dt("ar0_out", [1, 32], F32)
        self.ar1_in = dt("ar1_in", [1, 8], F32)
        self.ar1_out = dt("ar1_out", [1, 8], F32)
        self.xq_sp = dt("xq_sp", [2048, 1024], BF16)
        self.z2_sp = dt("z2_sp", [2048, 1024], F32)
        self.w1b = dt("w1b", [D, HFF], BF16)
        self.w3b = dt("w3b", [D, HFF], BF16)
        self.w2b = dt("w2b", [HFF, D], BF16)

    def _alloc(self):
        nc, E = self.nc, self.es.enter_context
        T = lambda n, s, d: E(nc.sbuf_tensor(n, s, d))
        self.idf = T("idf", [128, 128], F32)
        self.ones_col_b = T("ones_col_b", [128, 1], BF16)
        self.ones_row_b = T("ones_row_b", [1, 128], BF16)
        self.ones_row_f = T("ones_row_f", [1, 128], F32)
        self.h128 = T("h128", [128, 128], BF16)
        self.bias0 = T("bias0", [128, 64], F32)
        self.bias1 = T("bias1", [128, 48], F32)
        self.gamma = T("gamma", [128, 16], F32)
        self.rowf = T("rowf", [1, 2048], F32)
        self.rowf2 = T("rowf2", [1, 128], F32)
        self.misc_pp = T("misc_pp", [128, 8], F32)
        self.negb = T("negb", [128, 16], F32)
        self.wacc = T("wacc", [128, 2], F32)

        self.x = T("x", [128, NCH * TPC], F32)
        self.xnb = T("xnb", [128, NCH * TPC], BF16)
        self.att = T("att", [128, NCH * TPC], BF16)
        self.PF = T("PF", [128, 3072], F32)
        self.PB = T("PB", [128, 26624], BF16)
        self.gscaled = T("gscaled", [128, 16], F32)

        self.pk = [E(nc.psum_tensor(f"pk{i}", [128, 512], F32)) for i in range(8)]
        self.prow = {}
        for i in (4, 5, 6, 7):
            v = nc.virtual_psum_tensor(f"prow{i}", [1, 512], F32)
            nc.back_virtual_tensor_with(v, self.pk[i])
            self.prow[i] = v

        self.dS = Ctr(E(nc.semaphore("dS")), True)
        self.dQ = Ctr(E(nc.semaphore("dQ")), True)
        self.dG = Ctr(E(nc.semaphore("dG")), True)
        self.rS = Ctr(E(nc.semaphore("rS")))
        self.rQ = Ctr(E(nc.semaphore("rQ")))
        self.rG = Ctr(E(nc.semaphore("rG")))
        self._qmap = {self.dS.sem: ("sync", self.dS, self.rS),
                      self.dQ.sem: ("scalar", self.dQ, self.rQ),
                      self.dG.sem: ("gpsimd", self.dG, self.rG)}
        self._flush_cache = {}
        self.cc = Ctr(E(nc.semaphore("cc")))
        self.tE = Ctr(E(nc.semaphore("tE")))
        self.sE = Ctr(E(nc.semaphore("sE")))
        self.vE = Ctr(E(nc.semaphore("vE")))

    def flush(self, qeng, at=None):
        qsem, ctr, relay = {"sync": (self.dS.sem, self.dS, self.rS),
                            "scalar": (self.dQ.sem, self.dQ, self.rQ),
                            "gpsimd": (self.dG.sem, self.dG, self.rG)}[qeng]
        need = ctr.n if at is None else at
        for cnt, snap in self._flush_cache.setdefault(qeng, []):
            if cnt >= need:
                return snap
        val = ctr.n
        relay.n += 1
        rsnap = (relay.sem, relay.n)

        def run(engine):
            engine.wait_ge(qsem, val)
            engine.nop().then_inc(relay.sem, 1)
        self.streams[qeng].append(run)
        self._flush_cache[qeng].append((val, rsnap))
        return rsnap

    def em(self, eng, waits, fn, sig=None):
        ws = []
        for w in waits:
            if w is None or w[1] <= 0:
                continue
            q = self._qmap.get(w[0])
            if q is not None and q[0] != eng:
                ws.append(self.flush(q[0], at=w[1]))
            elif q is not None:
                ws.append((w[0], q[1].n))
            else:
                ws.append(w)
        ws = [w for w in ws if w[1] > 0]
        if sig is not None:
            sig.n += sig.step
            sem, val, step = sig.sem, sig.n, sig.step

        def run(engine):
            for s, v in ws:
                engine.wait_ge(s, v)
            inst = fn(engine)
            if sig is not None:
                assert inst is not None
                inst.then_inc(sem, step)
        self.streams[eng].append(run)
        return sig.snap() if sig is not None else None

    def dma(self, eng, out, in_, waits=()):
        ctr = {"sync": self.dS, "scalar": self.dQ, "gpsimd": self.dG}[eng]
        return self.em(eng, waits, lambda e: e.dma_start(out=out, in_=in_), sig=ctr)

    def barrier(self):
        return [self.dS.snap(), self.dQ.snap(), self.dG.snap(),
                self.tE.snap(), self.sE.snap(), self.vE.snap(), self.cc.snap()]

    def pf(self, off, cols):
        assert off % 4 == 0
        return self.PF[:, off // 4: off // 4 + cols]

    def pb(self, off, cols):
        assert off % 2 == 0
        return self.PB[:, off // 2: off // 2 + cols]

    def bvrow0(self):
        return self.PB[0:1, 14336:16384]

    def bvrow1(self):
        return self.PB[0:1, 16384:16896]

    def hid(self, c):
        if c < 32:
            return self.pb(20480 + c * 1024, 512)
        return self.att[:, (c - 32) * 512:(c - 32 + 1) * 512]

    def finalize(self):
        nc = self.nc
        with self.es:
            with nc.Block() as block:
                @block.sync
                def _(e):
                    for f in self.streams["sync"]:
                        f(e)

                @block.gpsimd
                def _(e):
                    for f in self.streams["gpsimd"]:
                        f(e)

                @block.tensor
                def _(e):
                    for f in self.streams["tensor"]:
                        f(e)

                @block.scalar
                def _(e):
                    for f in self.streams["scalar"]:
                        f(e)

                @block.vector
                def _(e):
                    for f in self.streams["vector"]:
                        f(e)
        return nc



def ph_prologue(mk):
    for dst, src in [
        (mk.idf[:], mk.c_idf.ap()),
        (mk.ones_col_b[:], mk.c_ones_col_b.ap()),
        (mk.ones_row_b[:], mk.c_ones_row_b.ap()),
        (mk.ones_row_f[:], mk.c_ones_row_f.ap()),
        (mk.h128[:], mk.c_h128.ap()),
        (mk.bias0[:], mk.c_bias0.ap()), (mk.bias1[:], mk.c_bias1.ap()),
        (mk.gamma[:], mk.c_gamma.ap()),
    ]:
        mk.consts_ready = mk.dma("sync", dst, src)
    mk.em("vector", [], lambda e: e.memset(mk.rowf2[0:1, 0:128], 0.0), sig=mk.vE)
    mk.ar_zeroed = mk.vE.snap()

    xr = mk.x_in.ap().rearrange("(t p) d -> t p d", p=128)
    xv = mk.x[:, :].rearrange("p (c t) -> p c t", c=NCH)
    st = {}
    for tt in range(8):
        stage = mk.pf(0, 2048)
        ld = mk.dma("scalar", stage, xr[tt], [st.get("ev")])
        for cg in range(4):
            bank = mk.pk[cg % 2]

            def tfn(e, stage=stage, cg=cg, bank=bank):
                last = None
                for k in range(4):
                    last = e.transpose(bank[:, k * 128:(k + 1) * 128],
                                       stage[:, (cg * 4 + k) * 128:(cg * 4 + k + 1) * 128],
                                       mk.idf[:])
                return last
            ts = mk.em("tensor", [ld, mk.consts_ready, st.get(("bank", cg % 2))],
                       tfn, sig=mk.tE)
            dst3 = xv[:, cg * 4:(cg + 1) * 4, tt * 128:(tt + 1) * 128]
            ev = mk.em("vector", [ts],
                       lambda e, dst3=dst3, bank=bank: e.tensor_copy(dst3, bank[:, 0:512]),
                       sig=mk.vE)
            st[("bank", cg % 2)] = ev
        st["ev"] = ev
    mk.x_ready = mk.vE.snap()


def _wjit(mk, codes, scexp, col0, ncols, dst_bf, extra_waits, qa="sync", qb="scalar",
          pf_b=False):
    assert ncols == 128
    st = mk._wjit_state = getattr(mk, "_wjit_state", {})
    cap = codes.ap().rearrange("(c p) o -> p c o", p=128)[:, :, col0:col0 + ncols]
    sap = scexp.ap().rearrange("(c p) o -> p c o", p=128)[:, :, col0:col0 + ncols]
    dst3 = dst_bf if len(dst_bf.shape) == 3 else dst_bf.rearrange("p (c o) -> p c o", c=16)
    a = mk.pf(0, 1024)
    bsl = [mk.pb(47104, 1024), mk.pb(49152, 1024)]
    tt = None
    for ich in range(2):
        ew = list(extra_waits) if ich == 0 else []
        la = mk.dma(qa, a, cap[:, ich * 8:(ich + 1) * 8, :], [st.get("tt")] + ew)
        if pf_b:
            bb = mk.pf(4096, 1024)
            lb = mk.dma(qb, bb, sap[:, ich * 8:(ich + 1) * 8, :], [st.get("tt")] + ew)
        else:
            bb = bsl[ich].rearrange("p (c o) -> p c o", c=8)
            lb = mk.dma("gpsimd", bsl[ich], sap[:, ich * 8:(ich + 1) * 8, :],
                        [st.get("tt")] + ew)
        tt = mk.em("vector", [la, lb],
                   lambda e, bb=bb, ich=ich: e.tensor_tensor(
                       dst3[:, ich * 8:(ich + 1) * 8, :], a, bb, ALU.mult), sig=mk.vE)
        st["tt"] = tt
    return tt


def ph_ln(mk, layer):
    xv = mk.x[:, :].rearrange("p (c t) -> p c t", c=NCH)
    bar = mk.barrier()
    prev_mm = None
    for c in range(16):
        xb = mk.pb((c % 2) * 2048, 1024)
        sq = mk.pb(4096 + (c % 2) * 2048, 1024)
        wx = ([mk.x_ready] if layer == 0 else [mk.ln1_start, mk.ln1_start_t])
        if c < 2:
            wx = wx + bar
        cast = mk.em("vector", wx + [prev_mm],
                     lambda e, xb=xb, c=c: e.tensor_copy(xb, xv[:, c, :]), sig=mk.vE)
        sqd = mk.em("scalar", wx + [prev_mm],
                    lambda e, sq=sq, c=c: e.activation(sq, xv[:, c, :], AF.Square),
                    sig=mk.sE)

        def mfn(e, xb=xb, sq=sq, c=c):
            last = None
            for th in range(2):
                last = e.matmul(mk.prow[4 + th][:], mk.ones_col_b[:],
                                xb[:, th * 512:(th + 1) * 512],
                                start=(c == 0), stop=(c == 15))
                last = e.matmul(mk.prow[6 + th][:], mk.ones_col_b[:],
                                sq[:, th * 512:(th + 1) * 512],
                                start=(c == 0), stop=(c == 15))
            return last
        prev_mm = mk.em("tensor", [cast, sqd, mk.consts_ready], mfn, sig=mk.tE)
    def s1(e):
        e.activation(mk.rowf[0:1, 0:512], mk.prow[4][:], AF.Copy, scale=1.0 / 2048)
        e.activation(mk.rowf[0:1, 512:1024], mk.prow[5][:], AF.Copy, scale=1.0 / 2048)
        e.activation(mk.rowf[0:1, 1024:1536], mk.prow[6][:], AF.Copy, scale=1.0 / 2048)
        return e.activation(mk.rowf[0:1, 1536:2048], mk.prow[7][:], AF.Copy, scale=1.0 / 2048)
    rows = mk.em("scalar", [prev_mm], s1, sig=mk.sE)

    def v1(e):
        e.tensor_tensor(mk.PF[0:1, 0:1024], mk.rowf[0:1, 0:1024], mk.rowf[0:1, 0:1024], ALU.mult)
        e.drain()
        e.tensor_tensor(mk.rowf[0:1, 1024:2048], mk.rowf[0:1, 1024:2048],
                        mk.PF[0:1, 0:1024], ALU.subtract)
        e.drain()
        return e.tensor_scalar(mk.rowf[0:1, 1024:2048], mk.rowf[0:1, 1024:2048],
                               LN_EPS, None, ALU.add)
    var = mk.em("vector", [rows], v1, sig=mk.vE)

    def s2(e):
        e.activation(mk.rowf[0:1, 1024:2048], mk.rowf[0:1, 1024:2048], AF.Ln)
        e.drain()
        return e.activation(mk.rowf[0:1, 1024:2048], mk.rowf[0:1, 1024:2048], AF.Exp, scale=-0.5)
    istd = mk.em("scalar", [var], s2, sig=mk.sE)

    mbc = mk.pf(0, 1024)
    ibc = mk.pf(4096, 1024)

    def b1(e):
        e.matmul(mk.pk[0][:], mk.ones_row_f[:], mk.rowf[0:1, 0:512], start=True, stop=True)
        e.matmul(mk.pk[1][:], mk.ones_row_f[:], mk.rowf[0:1, 512:1024], start=True, stop=True)
        e.matmul(mk.pk[2][:], mk.ones_row_f[:], mk.rowf[0:1, 1024:1536], start=True, stop=True)
        return e.matmul(mk.pk[3][:], mk.ones_row_f[:], mk.rowf[0:1, 1536:2048], start=True, stop=True)
    bc = mk.em("tensor", [istd, rows, var], b1, sig=mk.tE)

    def e1(e):
        e.tensor_copy(mbc[:, 0:512], mk.pk[0][:])
        e.tensor_copy(mbc[:, 512:1024], mk.pk[1][:])
        e.tensor_copy(ibc[:, 0:512], mk.pk[2][:])
        return e.tensor_copy(ibc[:, 512:1024], mk.pk[3][:])
    bce = mk.em("vector", [bc], e1, sig=mk.vE)

    st = {}
    prev_mm = None
    for c in range(16):
        if layer == 0:
            z2r = mk.pf(8192, 1024)
            sq = mk.pb(4096 + (c % 2) * 2048, 1024)

            def a1(e, c=c, z2r=z2r):
                e.tensor_tensor(z2r, xv[:, c, :], mbc, ALU.subtract)
                e.drain()
                return e.tensor_tensor(z2r, z2r, ibc, ALU.mult)
            ap = mk.em("vector", [bce, st.get("dma"), st.get("mm"),
                                  st.get("sc")], a1, sig=mk.vE)

            def c1(e, c=c, z2r=z2r, sq=sq):
                e.activation(mk.xnb[:, c * TPC:(c + 1) * TPC], z2r, AF.Copy)
                return e.activation(sq, z2r, AF.Square)
            sc = mk.em("scalar", [ap, rows], c1, sig=mk.sE)
            st["sc"] = sc

            def m1(e, sq=sq, c=c):
                e.matmul(mk.prow[4][:], mk.ones_col_b[:], sq[:, 0:512],
                         start=(c == 0), stop=(c == 15))
                return e.matmul(mk.prow[5][:], mk.ones_col_b[:], sq[:, 512:1024],
                                start=(c == 0), stop=(c == 15))
            prev_mm = mk.em("tensor", [sc], m1, sig=mk.tE)
            st["mm"] = prev_mm
            st["dma"] = mk.dma("gpsimd", mk.z2_sp.ap()[c * 128:(c + 1) * 128, :],
                               z2r, [sc])
        else:
            z2r = mk.pf(8192, 1024)
            st["dummy"] = None

            def a2(e, c=c, z2r=z2r):
                e.drain()
                e.tensor_tensor(z2r, xv[:, c, :], mbc, ALU.subtract)
                e.drain()
                return e.tensor_tensor(mk.xnb[:, c * TPC:(c + 1) * TPC], z2r, ibc, ALU.mult)
            mk.em("vector", [bce], a2, sig=mk.vE)
    mk.xn_ready_v = mk.vE.snap()
    mk.xn_ready_s = mk.sE.snap()
    if layer == 0:
        mk.rms_mm = prev_mm
        mk.z2_spilled = mk.dG.snap()


def ph_quantpre(mk):
    bar = mk.barrier()
    def s1(e):
        e.activation(mk.rowf[0:1, 0:512], mk.prow[4][:], AF.Copy, scale=1.0 / 2048)
        return e.activation(mk.rowf[0:1, 512:1024], mk.prow[5][:], AF.Copy, scale=1.0 / 2048)
    ms = mk.em("scalar", [mk.rms_mm], s1, sig=mk.sE)
    eps = mk.em("vector", [ms],
                lambda e: e.tensor_scalar(mk.rowf[0:1, 0:1024], mk.rowf[0:1, 0:1024],
                                          RMS_EPS, None, ALU.add), sig=mk.vE)

    def s2(e):
        e.activation(mk.rowf[0:1, 0:1024], mk.rowf[0:1, 0:1024], AF.Ln)
        e.drain()
        return e.activation(mk.rowf[0:1, 0:1024], mk.rowf[0:1, 0:1024], AF.Exp, scale=-0.5)
    irow = mk.em("scalar", [eps], s2, sig=mk.sE)
    irbc = mk.pf(8192, 1024)

    def b1(e):
        e.matmul(mk.pk[0][:], mk.ones_row_f[:], mk.rowf[0:1, 0:512], start=True, stop=True)
        return e.matmul(mk.pk[1][:], mk.ones_row_f[:], mk.rowf[0:1, 512:1024], start=True, stop=True)
    bc = mk.em("tensor", [irow, mk.z2_spilled], b1, sig=mk.tE)

    def e1(e):
        e.tensor_copy(irbc[:, 0:512], mk.pk[0][:])
        return e.tensor_copy(irbc[:, 512:1024], mk.pk[1][:])
    bce = mk.em("vector", [bc], e1, sig=mk.vE)

    lm = None
    st = {}
    for c in range(16):
        z2r = mk.pf(4096, 1024)
        ld = mk.dma("sync", z2r, mk.z2_sp.ap()[c * 128:(c + 1) * 128, :],
                    [mk.z2_spilled, st.get("rd")] + (bar if c == 0 else []))

        def q1(e, c=c, z2r=z2r):
            e.scalar_tensor_tensor(z2r, z2r, mk.gamma[:, c:c + 1], irbc, ALU.mult, ALU.mult)
            e.drain()
            e.tensor_reduce(mk.misc_pp[:, 3:4], z2r, AXX, ALU.max, apply_absolute_value=True)
            e.drain()
            if c == 0:
                return e.tensor_copy(mk.wacc[:, 0:1], mk.misc_pp[:, 3:4])
            return e.tensor_tensor(mk.wacc[:, 0:1], mk.wacc[:, 0:1], mk.misc_pp[:, 3:4], ALU.max)
        lm = mk.em("vector", [ld, bce], q1, sig=mk.vE)
        st["rd"] = lm
    tr = mk.em("tensor", [lm],
               lambda e: e.transpose(mk.pk[2][0:1, 0:128], mk.wacc[:, 0:1], mk.idf[:]),
               sig=mk.tE)
    mk.z2_done = mk.em("vector", [tr, mk.ar_zeroed],
                       lambda e: e.tensor_reduce(mk.rowf2[0:1, 0:1], mk.pk[2][0:1, 0:128],
                                                 AXX, ALU.max), sig=mk.vE)
    mk.quantpre_v = mk.vE.snap()


def ph_kvproj0(mk):
    xnv = mk.xnb[:, :].rearrange("p (c t) -> p c t", c=NCH)
    bar = mk.barrier()
    bvs = mk.dma("sync", mk.PF[0:1, 0:2048], mk.c_bvrow0.ap(), bar)
    cv0 = mk.em("vector", [bvs],
                lambda e: e.tensor_copy(mk.bvrow0(), mk.PF[0:1, 0:2048]), sig=mk.vE)
    bvs1 = mk.dma("sync", mk.PF[0:1, 2048:2560], mk.c_bvrow1.ap(), [cv0])
    cv1 = mk.em("vector", [bvs1],
                lambda e: e.tensor_copy(mk.bvrow1(), mk.PF[0:1, 2048:2560]), sig=mk.vE)
    mk.bvrows_ready = mk.vE.snap()

    wkp = mk.pb(0, 2048)
    kt_off = (20480, 22528)
    ksq = mk.pb(26624, 1024)
    g4 = [[0, 1, 2, 3], [4, 5, 6, 7]]
    g8 = [list(range(8))]
    wvg_full = mk.pb(4096, 16 * 512)
    wvg3 = wvg_full.rearrange("p (c o) -> p c o", c=16)
    vt_off = (24576, 25600)
    mk.cc_agk0g, mk.cc_agv0g = [], []
    st = {}
    for g in range(4):
        for h in range(4 * g, 4 * g + 4):
            jit = _wjit(mk, mk.wk0c, mk.sk0, h * 128, 128, wkp,
                        [st.get("wk_used")] + (bar + [mk.bvrows_ready] if h == 0 else []))
            ktile = mk.pb(kt_off[h % 2], 1024)

            def mfn(e, h=h):
                last = None
                for th in range(2):
                    for ic in range(16):
                        last = e.matmul(mk.pk[th][:], wkp[:, ic * 128:(ic + 1) * 128],
                                        xnv[:, ic, th * 512:(th + 1) * 512],
                                        start=(ic == 0), stop=(ic == 15))
                return last
            mm = mk.em("tensor", [jit, mk.xn_ready_s, st.get(("ev", h % 2))], mfn, sig=mk.tE)
            st["wk_used"] = mm

            def efn(e, h=h, ktile=ktile):
                e.activation(ktile[:, 0:512], mk.pk[0][:], AF.Identity,
                             bias=mk.bias0[:, 16 + h:17 + h])
                return e.activation(ktile[:, 512:1024], mk.pk[1][:], AF.Identity,
                                    bias=mk.bias0[:, 16 + h:17 + h])
            ev = mk.em("scalar", [mm, st.get(("dma", h % 2))], efn, sig=mk.sE)
            st[("ev", h % 2)] = ev
            sqv = mk.em("vector", [ev, st.get("knorm_mm")],
                        lambda e, ktile=ktile: e.tensor_tensor(ksq, ktile, ktile, ALU.mult),
                        sig=mk.vE)

            def nfn(e):
                e.matmul(mk.prow[6][:], mk.ones_col_b[:], ksq[:, 0:512], start=True, stop=True)
                return e.matmul(mk.prow[7][:], mk.ones_col_b[:], ksq[:, 512:1024],
                                start=True, stop=True)
            nm = mk.em("tensor", [sqv, st.get("knorm_rd")], nfn, sig=mk.tE)
            st["knorm_mm"] = nm

            def rfn(e, h=h):
                e.tensor_reduce(mk.rowf2[0:1, 1 + h:2 + h], mk.prow[6][:], AXX, ALU.max)
                e.tensor_reduce(mk.rowf2[0:1, 60:61], mk.prow[7][:], AXX, ALU.max)
                e.drain()
                return e.tensor_tensor(mk.rowf2[0:1, 1 + h:2 + h], mk.rowf2[0:1, 1 + h:2 + h],
                                       mk.rowf2[0:1, 60:61], ALU.max)
            st["knorm_rd"] = mk.em("vector", [nm], rfn, sig=mk.vE)
            st[("dma", h % 2)] = mk.dma("gpsimd", mk.kin0.ap()[h * 128:(h + 1) * 128, :],
                                        ktile, [ev])
            for _ in range(2):
                if mk.bg_tasks:
                    mk.bg_tasks.popleft()()
        kg_done = mk.dG.snap()
        mk.em("gpsimd", [kg_done],
              lambda e, g=g: e.collective_compute(
                  "AllGather", ALU.bypass, replica_groups=g4,
                  ins=[mk.kin0.ap()[g * 512:(g + 1) * 512, :].opt()],
                  outs=[mk.kout0g[g].ap().opt()]),
              sig=mk.cc)
        mk.cc_agk0g.append(mk.cc.snap())

        if g == 3:
            mk.knorms_done = mk.vE.snap()
            arld = mk.dma("gpsimd", mk.ar0_in.ap(), mk.rowf2[0:1, 0:32],
                          [mk.z2_done, mk.knorms_done])
            mk.em("gpsimd", [arld],
                  lambda e: e.collective_compute("AllReduce", ALU.max, replica_groups=g8,
                                                 ins=[mk.ar0_in.ap().opt()],
                                                 outs=[mk.ar0_out.ap().opt()]),
                  sig=mk.cc)
            mk.cc_ar0 = mk.cc.snap()

        jits = []
        for sf in range(4):
            dst = wvg3[:, :, sf * 128:(sf + 1) * 128]
            jits.append(_wjit(mk, mk.wv0c, mk.sv0, g * 512 + sf * 128, 128, dst,
                              [st.get("wv_used") or st.get("wk_used")]))
        for tt in range(8):
            def mfn(e, g=g, tt=tt):
                e.matmul(mk.pk[2 + tt % 2][:], mk.ones_row_b[:],
                         mk.bvrow0()[0:1, g * 512:(g + 1) * 512], start=True, stop=False)
                last = None
                for ic in range(16):
                    last = e.matmul(mk.pk[2 + tt % 2][:],
                                    xnv[:, ic, tt * 128:(tt + 1) * 128],
                                    wvg_full[:, ic * 512:(ic + 1) * 512],
                                    start=False, stop=(ic == 15))
                return last
            mm = mk.em("tensor", jits + [mk.bvrows_ready, st.get(("vev", tt % 2))],
                       mfn, sig=mk.tE)
            st["wv_used"] = mm
            vtile = mk.pb(vt_off[tt % 2], 512)
            ev = mk.em("scalar", [mm, st.get(("vdma", tt % 2))],
                       lambda e, vtile=vtile, tt=tt: e.activation(vtile, mk.pk[2 + tt % 2][:],
                                                                  AF.Copy),
                       sig=mk.sE)
            st[("vev", tt % 2)] = ev
            st[("vdma", tt % 2)] = mk.dma(
                "gpsimd", mk.vin0g[g].ap()[tt * 128:(tt + 1) * 128, :],
                vtile, [ev])
            for _ in range(2):
                if mk.bg_tasks:
                    mk.bg_tasks.popleft()()
        vg_done = mk.dG.snap()
        mk.em("gpsimd", [vg_done],
              lambda e, g=g: e.collective_compute(
                  "AllGather", ALU.bypass, replica_groups=g4,
                  ins=[mk.vin0g[g].ap().opt()], outs=[mk.vout0g[g].ap().opt()]),
              sig=mk.cc)
        mk.cc_agv0g.append(mk.cc.snap())
    mk.cc_agv0 = mk.cc_agv0g[3]
    mk.pool_entry_t = mk.tE.snap()
    mk.pool_entry_v = mk.vE.snap()
    mk.pool_entry_s = mk.sE.snap()


def ph_quantfin(mk):
    bar = mk.barrier()
    ld = mk.dma("sync", mk.rowf2[0:1, 0:32], mk.ar0_out.ap(), [mk.cc_ar0])
    mk.ar0_row_loaded = ld

    def v1(e):
        e.tensor_scalar(mk.rowf2[0:1, 40:41], mk.rowf2[0:1, 0:1], 1e-5, None, ALU.max)
        e.drain()
        e.reciprocal(mk.rowf2[0:1, 41:42], mk.rowf2[0:1, 40:41])
        e.drain()
        return e.tensor_scalar(mk.rowf2[0:1, 41:42], mk.rowf2[0:1, 41:42], 127.0, None, ALU.mult)
    r127 = mk.em("vector", [ld], v1, sig=mk.vE)
    bc = mk.em("tensor", [r127, mk.pool_entry_v, mk.pool_entry_s],
               lambda e: e.matmul(mk.pk[3][:, 0:1], mk.ones_row_f[:],
                                  mk.rowf2[0:1, 41:42], start=True, stop=True), sig=mk.tE)
    bce = mk.em("vector", [bc],
                lambda e: e.tensor_copy(mk.misc_pp[:, 2:3], mk.pk[3][:, 0:1]), sig=mk.vE)
    mk.quantfin_bce = bce
    gsc = mk.em("vector", [bce],
                lambda e: e.tensor_scalar(mk.gscaled[:, :], mk.gamma[:, :],
                                          mk.misc_pp[:, 2:3], None, ALU.mult), sig=mk.vE)
    irbc = mk.pf(8192, 1024)

    def b1(e):
        e.matmul(mk.pk[0][:], mk.ones_row_f[:], mk.rowf[0:1, 0:512], start=True, stop=True)
        return e.matmul(mk.pk[1][:], mk.ones_row_f[:], mk.rowf[0:1, 512:1024], start=True, stop=True)
    bc2 = mk.em("tensor", [mk.pool_entry_v, r127] + bar, b1, sig=mk.tE)

    def e1(e):
        e.tensor_copy(irbc[:, 0:512], mk.pk[0][:])
        return e.tensor_copy(irbc[:, 512:1024], mk.pk[1][:])
    bce2 = mk.em("vector", [bc2] + bar, e1, sig=mk.vE)

    st = {}
    z2s = [mk.pf(0, 1024), mk.pf(4096, 1024)]
    stgs = [mk.pb(47104, 1024), mk.pb(49152, 1024)]
    for c in range(16):
        z2r = z2s[c % 2]
        stg = stgs[c % 2]
        ld = mk.dma("gpsimd", z2r, mk.z2_sp.ap()[c * 128:(c + 1) * 128, :],
                    [st.get(("q", c % 2))] + (bar if c < 2 else []))

        def qfn(e, c=c, z2r=z2r, stg=stg):
            e.scalar_tensor_tensor(z2r, z2r, mk.gscaled[:, c:c + 1], irbc, ALU.mult, ALU.mult)
            e.drain()
            e.tensor_scalar(z2r, z2r, 127.0, -128.0, ALU.min, ALU.max)
            e.drain()
            return e.tensor_scalar(stg, z2r, MAGIC, -MAGIC, ALU.add, ALU.add)
        qd = mk.em("vector", [ld, gsc, bce2, st.get(("dma", c % 2)),
                              mk.pool_entry_t if c < 2 else None], qfn, sig=mk.vE)
        st[("q", c % 2)] = qd
        st[("dma", c % 2)] = mk.dma("gpsimd", mk.xq_sp.ap()[c * 128:(c + 1) * 128, :], stg, [qd])
    mk.xq_spilled = mk.dG.snap()
    mk.flush("gpsimd")
    mk.quantfin_v = mk.vE.snap()


def ph_wq0prep_tasks(mk):
    bar = mk.barrier()
    aslot = [mk.pf(8192, 512), mk.pf(10240, 512)]
    bslot = [mk.pb(45056, 512), mk.pb(46080, 512)]
    wslot = [mk.pb(36864, 2048), mk.pb(40960, 2048)]
    st = mk._wq_st = {}

    def task(ic, q):
        def run(ic=ic, q=q):
            r = q % 2
            first = bar if (ic == 0 and q < 2) else []
            la = mk.dma("sync", aslot[r],
                        mk.wq0c.ap()[ic * 128:(ic + 1) * 128, q * 512:(q + 1) * 512],
                        [st.get(("tt", r))] + first)
            lb = mk.dma("gpsimd", bslot[r],
                        mk.sq0.ap()[ic * 128:(ic + 1) * 128, q * 512:(q + 1) * 512],
                        [st.get(("tt", r))] + first)
            w = wslot[ic % 2]
            tt = mk.em("vector", [la, lb, st.get(("wsp", ic % 2)) if q == 0 else None],
                       lambda e, w=w, q=q, r=r: e.tensor_tensor(
                           w[:, q * 512:(q + 1) * 512], aslot[r], bslot[r], ALU.mult),
                       sig=mk.vE)
            st[("tt", r)] = tt
            if q == 3:
                st[("wsp", ic % 2)] = mk.dma(
                    "gpsimd", mk.wq0b.ap()[ic * 128:(ic + 1) * 128, :], w, [tt])
        return run

    for ic in range(16):
        for q in range(4):
            mk.bg_tasks.append(task(ic, q))


def ph_passA_tasks(mk):
    bar = mk.barrier()
    st = mk._pa_st = {}

    def task(it):
        def run(it=it):
            r = it % 2
            a = mk.pf(4096 + 2048 * r, 512)
            la = mk.dma("scalar", a,
                        mk.w0T.ap()[(it // 4) * 128:(it // 4 + 1) * 128,
                                    (it % 4) * 512:(it % 4 + 1) * 512],
                        [st.get(("rd", r))] + (bar if it < 2 else []))

            def rfn(e, it=it, a=a):
                e.tensor_reduce(mk.misc_pp[:, 3:4], a, AXX, ALU.add,
                                apply_absolute_value=True)
                e.drain()
                if it == 0:
                    return e.tensor_copy(mk.wacc[:, 1:2], mk.misc_pp[:, 3:4])
                return e.tensor_tensor(mk.wacc[:, 1:2], mk.wacc[:, 1:2],
                                       mk.misc_pp[:, 3:4], ALU.add)
            st[("rd", r)] = mk.em("vector", [la, st.get("acc")], rfn, sig=mk.vE)
            st["acc"] = st[("rd", r)]
        return run

    return [task(it) for it in range(64)]


def ph_ffn0passA(mk):
    st = mk._pa_st
    tr = mk.em("tensor", [st["acc"], mk.quantfin_bce],
               lambda e: e.transpose(mk.pk[3][0:1, 0:128], mk.wacc[:, 1:2], mk.idf[:]),
               sig=mk.tE)

    def v2(e):
        e.tensor_reduce(mk.rowf2[0:1, 42:43], mk.pk[3][0:1, 0:128], AXX, ALU.add)
        e.drain()
        e.tensor_scalar(mk.rowf2[0:1, 42:43], mk.rowf2[0:1, 42:43], 1.0 / (2048.0 * 2048.0),
                        None, ALU.mult)
        e.drain()
        e.tensor_scalar(mk.rowf2[0:1, 42:43], mk.rowf2[0:1, 42:43], 1e-5, None, ALU.max)
        e.drain()
        e.reciprocal(mk.rowf2[0:1, 43:44], mk.rowf2[0:1, 42:43])
        e.tensor_tensor(mk.rowf2[0:1, 44:45], mk.rowf2[0:1, 40:41], mk.rowf2[0:1, 42:43], ALU.mult)
        e.drain()
        return e.tensor_scalar(mk.rowf2[0:1, 44:45], mk.rowf2[0:1, 44:45], 1.0 / 127.0,
                               None, ALU.mult)
    sc = mk.em("vector", [tr], v2, sig=mk.vE)

    def b1(e):
        e.matmul(mk.pk[3][:, 0:1], mk.ones_row_f[:], mk.rowf2[0:1, 43:44], start=True, stop=True)
        return e.matmul(mk.pk[3][:, 1:2], mk.ones_row_f[:], mk.rowf2[0:1, 44:45], start=True, stop=True)
    bc = mk.em("tensor", [sc], b1, sig=mk.tE)

    def e1(e):
        e.tensor_copy(mk.misc_pp[:, 0:1], mk.pk[3][:, 0:1])
        return e.tensor_copy(mk.misc_pp[:, 1:2], mk.pk[3][:, 1:2])
    mk.ffn0_scales_ready = mk.em("vector", [bc], e1, sig=mk.vE)


def ph_attn(mk, layer):
    xnv = mk.xnb[:, :].rearrange("p (c t) -> p c t", c=NCH)
    bar = mk.barrier()
    if layer == 0:
        bar = [mk.wq0b_done, mk.pool_entry_t, mk.xn_ready_s]
    ksb = [mk.pb(0, 4096), mk.pb(8192, 4096)]
    vsb = [mk.pb(16384, 4096), mk.pb(24576, 4096)]
    P = [mk.pb(32768 + 1024 * i, 512) for i in range(4)]
    qsb = [mk.pb(36864, 1024), mk.pb(38912, 1024)]
    qw = [mk.pb(40960, 2048), mk.pb(47104, 2048)]
    qsq = mk.pb(45056, 1024)
    sbc = mk.pf(0, 1024)

    if layer == 0:
        kv_of = lambda h: h
        kviews = [t.ap().rearrange("(r h p) k -> r h p k", r=4, h=4) for t in mk.kout0g]
        vviews = [t.ap().rearrange("(r t p) f -> r p t f", r=4, t=8) for t in mk.vout0g]
        kview_of = lambda h: kviews[h // 4][:, h % 4]
        vview_of = lambda h: vviews[h // 4][:, :, :, (h % 4) * 128:(h % 4 + 1) * 128]
        ccK_of = lambda h: mk.cc_agk0g[h // 4]
        ccV_of = lambda h: mk.cc_agv0g[h // 4]
        kk_slot = lambda h: 1 + h
        bias_q = lambda h: mk.bias0[:, h:h + 1]
        ar_ld = mk.ar0_row_loaded
    else:
        kv_of = lambda h: h // 4
        kview1 = mk.kout1.ap().rearrange("(r h p) k -> r h p k", r=4, h=4)
        vview1 = mk.vout1.ap().rearrange("(r t p) f -> r p t f", r=4, t=8)
        kview_of = lambda h: kview1[:, h // 4]
        vview_of = lambda h: vview1[:, :, :, (h // 4) * 128:(h // 4 + 1) * 128]
        ccK_of = lambda h: mk.cc_agk1
        ccV_of = lambda h: mk.cc_agv1
        kk_slot = lambda h: 32 + h // 4
        bias_q = lambda h: mk.bias1[:, h:h + 1]
        ar_ld = mk.ar1_row_loaded

    st = {}

    def prep_loads(h):
        kvh = kv_of(h)
        kb = (h % 2) if layer == 0 else (kvh % 2)
        new_kv = (layer == 0) or (h % 4 == 0)
        if new_kv:
            entry = [] if ("kfree", kb) in st else bar
            kview = kview_of(h)
            vview = vview_of(h)
            kload = None
            for r in range(4):
                kload = mk.dma("sync", ksb[kb][:, r * 1024:(r + 1) * 1024],
                               kview[r], [ccK_of(h), st.get(("kfree", kb))] + entry)
            vload = None
            for r in range(4):
                dst = vsb[kb][:, r * 1024:(r + 1) * 1024].rearrange("p (t d) -> p t d", t=8)
                vload = mk.dma("sync", dst, vview[r],
                               [ccV_of(h), st.get(("vfree", kb))] + entry)
            st[("kload", kb)] = kload
            st[("vload", kb)] = vload
        qsl = lambda w: w.ap().rearrange("(c p) o -> p c o", p=128)[:, :, h * 128:(h + 1) * 128]
        entry = [] if h >= 2 else bar
        qfree = st.get(("qw_used", h % 2))
        if layer == 0:
            st[("qwl", h % 2)] = mk.dma("sync", qw[h % 2], qsl(mk.wq0b),
                                        [mk.wq0b_done, qfree,
                                         mk.xq_spilled if h == 1 else None] + entry)
        else:
            st[("qwl", h % 2)] = mk.dma("gpsimd", qw[h % 2], qsl(mk.wq1), [qfree] + entry)
            mk.flush("gpsimd")
        mk.flush("sync")

    def prep_qproj(h):
        def qfn(e, h=h):
            last = None
            for th in range(2):
                for ic in range(16):
                    last = e.matmul(mk.pk[6 + th][:], qw[h % 2][:, ic * 128:(ic + 1) * 128],
                                    xnv[:, ic, th * 512:(th + 1) * 512],
                                    start=(ic == 0), stop=(ic == 15))
            return last
        qmm = mk.em("tensor", [st[("qwl", h % 2)], st.get("qq_read"),
                               st.get("sbe_tok")], qfn, sig=mk.tE)
        st[("qw_used", h % 2)] = qmm
        st["qmm"] = qmm

    def prep_qev(h):
        def qev(e, h=h):
            e.activation(qsb[h % 2][:, 0:512], mk.pk[6][:], AF.Identity, bias=bias_q(h))
            return e.activation(qsb[h % 2][:, 512:1024], mk.pk[7][:], AF.Identity,
                                bias=bias_q(h))
        st[("qe", h % 2)] = mk.em("scalar", [st["qmm"], st.get(("qfree", h % 2))],
                                  qev, sig=mk.sE)

    def prep_bound(h):
        qe = st[("qe", h % 2)]
        qq1 = mk.em("vector", [qe, st.get("qsq_read")],
                    lambda e, h=h: e.tensor_tensor(qsq, qsb[h % 2], qsb[h % 2], ALU.mult),
                    sig=mk.vE)

        def qqm(e):
            e.matmul(mk.prow[6][:], mk.ones_col_b[:], qsq[:, 0:512], start=True, stop=True)
            return e.matmul(mk.prow[7][:], mk.ones_col_b[:], qsq[:, 512:1024], start=True, stop=True)
        qq2 = mk.em("tensor", [qq1, qe], qqm, sig=mk.tE)
        st["qsq_read"] = qq2

        def qq3(e, h=h):
            e.tensor_reduce(mk.rowf2[0:1, 45:46], mk.prow[6][:], AXX, ALU.max)
            e.tensor_reduce(mk.rowf2[0:1, 46:47], mk.prow[7][:], AXX, ALU.max)
            e.drain()
            e.tensor_tensor(mk.rowf2[0:1, 45:46], mk.rowf2[0:1, 45:46],
                            mk.rowf2[0:1, 46:47], ALU.max)
            e.drain()
            e.tensor_tensor(mk.rowf2[0:1, 45:46], mk.rowf2[0:1, 45:46],
                            mk.rowf2[0:1, kk_slot(h):kk_slot(h) + 1], ALU.add)
            e.drain()
            return e.tensor_scalar(mk.rowf2[0:1, 47:48], mk.rowf2[0:1, 45:46],
                                   NEG_HALF_ISH, None, ALU.mult)
        qq4 = mk.em("vector", [qq2, ar_ld], qq3, sig=mk.vE)
        nb1 = mk.em("tensor", [qq4],
                    lambda e: e.matmul(mk.pk[6][:, 0:1], mk.ones_row_f[:],
                                       mk.rowf2[0:1, 47:48], start=True, stop=True), sig=mk.tE)
        nb2 = mk.em("vector", [nb1],
                    lambda e, h=h: e.tensor_copy(mk.negb[:, h:h + 1], mk.pk[6][:, 0:1]),
                    sig=mk.vE)
        st["qq_read"] = nb2
        st[("nb", h % 2)] = nb2

    prep_loads(0)
    prep_qproj(0)
    prep_qev(0)
    prep_bound(0)

    for h in range(NH):
        kvh = kv_of(h)
        kb = (h % 2) if layer == 0 else (kvh % 2)
        kload, vload = st[("kload", kb)], st[("vload", kb)]
        qe, nb2 = st[("qe", h % 2)], st[("nb", h % 2)]

        s_snap, e_snap, p_snap = {}, {}, {}
        srow = [mk.pk[5][0:1, :], mk.pk[5][32:33, :]]

        def emit_S(i):
            kt, qh = i // 2, i % 2

            def sfn(e, kt=kt, qh=qh, kb=kb, h=h, i=i):
                return e.matmul(mk.pk[i % 3][:], ksb[kb][:, kt * 128:(kt + 1) * 128],
                                qsb[h % 2][:, qh * 512:(qh + 1) * 512],
                                start=True, stop=True)
            sw = [e_snap.get(i - 3)]
            if i < 3:
                sw += [kload, qe, st.get(("exp_tail", i))]
            s_snap[i] = mk.em("tensor", sw, sfn, sig=mk.tE)

        def emit_exp(i):
            def efn(e, i=i, h=h):
                return e.activation(P[i % 4][:], mk.pk[i % 3][:], AF.Exp,
                                    bias=mk.negb[:, h:h + 1], scale=ISH)
            ew = [s_snap[i], p_snap.get(i - 4)]
            if i == 0:
                ew.append(nb2)
            e_snap[i] = mk.em("scalar", ew, efn, sig=mk.sE)

        def emit_PV(i):
            kt, qh = i // 2, i % 2

            def pfn(e, kt=kt, qh=qh, i=i, kb=kb):
                e.matmul(mk.pk[3 + qh][:], vsb[kb][:, kt * 128:(kt + 1) * 128],
                         P[i % 4][:], start=(kt == 0), stop=(kt == 31))
                return e.matmul(srow[qh], mk.ones_col_b[:],
                                P[i % 4][:], start=(kt == 0), stop=(kt == 31))
            pw = [e_snap[i]]
            if kt == 0:
                pw += [vload, st.get("pv_free"), st.get("sums_free")]
            p_snap[i] = mk.em("tensor", pw, pfn, sig=mk.tE)

        emit_S(0)
        emit_exp(0)
        for i in range(1, 64):
            emit_S(i)
            emit_exp(i)
            emit_PV(i - 1)

            if h + 1 < NH:
                if i == 16:
                    prep_loads(h + 1)
                elif i == 24:
                    prep_qproj(h + 1)
                elif i == 28:
                    prep_qev(h + 1)
                elif i == 32:
                    prep_bound(h + 1)
            if h >= 2 and i in (8, 20, 36, 52) and mk.bg_tasks:
                mk.bg_tasks.popleft()()
            if i in (14, 34, 54) and mk.wcast_tasks:
                mk.wcast_tasks.popleft()()
        emit_PV(63)

        def sg1(e):
            e.tensor_copy(mk.rowf[0:1, 1024:1536], srow[0])
            e.tensor_copy(mk.rowf[0:1, 1536:2048], srow[1])
            e.drain()
            return e.reciprocal(mk.rowf[0:1, 1024:2048], mk.rowf[0:1, 1024:2048])
        sg = mk.em("vector", [p_snap[63]], sg1, sig=mk.vE)
        st["sums_free"] = sg

        def sb1(e):
            e.matmul(mk.pk[6][:], mk.ones_row_f[:], mk.rowf[0:1, 1024:1536],
                     start=True, stop=True)
            return e.matmul(mk.pk[7][:], mk.ones_row_f[:], mk.rowf[0:1, 1536:2048],
                            start=True, stop=True)
        sbm = mk.em("tensor", [sg], sb1, sig=mk.tE)

        def sb2(e):
            e.tensor_copy(sbc[:, 0:512], mk.pk[6][:])
            return e.tensor_copy(sbc[:, 512:1024], mk.pk[7][:])
        sbe = mk.em("vector", [sbm], sb2, sig=mk.vE)
        st["sbe_tok"] = sbe

        def nrm(e, h=h):
            e.tensor_tensor(mk.att[:, h * TPC:h * TPC + 512], mk.pk[3][:],
                            sbc[:, 0:512], ALU.mult)
            return e.tensor_tensor(mk.att[:, h * TPC + 512:(h + 1) * TPC], mk.pk[4][:],
                                   sbc[:, 512:1024], ALU.mult)
        nr = mk.em("vector", [sbe], nrm, sig=mk.vE)
        st["pv_free"] = nr
        st[("kfree", kb)] = s_snap[63]
        st[("vfree", kb)] = p_snap[63]
        st[("qfree", h % 2)] = s_snap[63]
        st[("exp_tail", 0)] = e_snap[63]
        st[("exp_tail", 1)] = e_snap[61]
        st[("exp_tail", 2)] = e_snap[62]
    mk.attn_done = mk.vE.snap()


def ph_outproj(mk, layer):
    attv = mk.att[:, :].rearrange("p (c t) -> p c t", c=NCH)
    bar = mk.barrier()
    xv = mk.x[:, :].rearrange("p (c t) -> p c t", c=NCH)
    wo_sb = mk.pb(0, 2048)
    st = {}
    for oc in range(16):
        if layer == 0:
            jit = _wjit(mk, mk.wo0c, mk.so0, oc * 128, 128, wo_sb, pf_b=True,
                        extra_waits=
                        [st.get("wo_used")] + (bar if oc == 0 else []))
        else:
            wsl = mk.wo1.ap().rearrange("(c p) o -> p c o", p=128)[:, :, oc * 128:(oc + 1) * 128]
            jit = mk.dma("gpsimd", wo_sb, wsl,
                         [st.get("wo_used")] + (bar if oc == 0 else []))

        def mfn(e):
            last = None
            for th in range(2):
                for ic in range(16):
                    last = e.matmul(mk.pk[th][:], wo_sb[:, ic * 128:(ic + 1) * 128],
                                    attv[:, ic, th * 512:(th + 1) * 512],
                                    start=(ic == 0), stop=(ic == 15))
            return last
        mm = mk.em("tensor", [jit, st.get("stt"),
                              mk.attn_done if oc == 0 else None], mfn, sig=mk.tE)
        st["wo_used"] = mm
        bcol = (mk.bias0[:, 48 + oc:49 + oc] if layer == 0
                else mk.bias1[:, 32 + oc:33 + oc])

        def afn(e, oc=oc, bcol=bcol):
            e.scalar_tensor_tensor(xv[:, oc, 0:512], mk.pk[0][:], bcol,
                                   xv[:, oc, 0:512], ALU.add, ALU.add)
            return e.scalar_tensor_tensor(xv[:, oc, 512:1024], mk.pk[1][:], bcol,
                                          xv[:, oc, 512:1024], ALU.add, ALU.add)
        st["stt"] = mk.em("vector", [mm], afn, sig=mk.vE)
    mk.x_after_attn = mk.vE.snap()
    mk.outproj_t = mk.tE.snap()


def ph_ffn0mm(mk):
    xv = mk.x[:, :].rearrange("p (c t) -> p c t", c=NCH)
    bar = mk.barrier()
    xqv = mk.xq_sp.ap().rearrange("(c p) t -> p c t", p=128)
    xnb3 = mk.xnb[:, :].rearrange("p (c t) -> p c t", c=NCH)
    xqld = mk.dma("sync", xnb3, xqv, [mk.xq_spilled, mk.outproj_t])
    st = {}
    w0qb_done = mk.w0qb_done
    wv = mk.w0qb.ap().rearrange("(c p) o -> p c o", p=128)
    for otp in range(8):
        for _ in range(4):
            if mk.wcast_tasks:
                mk.wcast_tasks.popleft()()
        wq_sb = mk.pb(8192 + (otp % 2) * 8192, 16 * 256)
        ld = mk.dma("sync", wq_sb, wv[:, :, otp * 256:(otp + 1) * 256],
                    [w0qb_done, st.get(("used", otp % 2))] + (bar if otp == 0 else []))
        for oc2 in range(2):
            oc = otp * 2 + oc2

            def mfn(e, wq_sb=wq_sb, oc2=oc2):
                last = None
                for th in range(2):
                    for ic in range(16):
                        last = e.matmul(mk.pk[th][:],
                                        wq_sb[:, ic * 256 + oc2 * 128:ic * 256 + oc2 * 128 + 128],
                                        xnb3[:, ic, th * 512:(th + 1) * 512],
                                        start=(ic == 0), stop=(ic == 15))
                return last
            mm = mk.em("tensor", [ld, xqld, st.get("stt2")], mfn, sig=mk.tE)
            st[("used", otp % 2)] = mm

            def afn(e, oc=oc):
                e.scalar_tensor_tensor(xv[:, oc, 0:512], mk.pk[0][:], mk.misc_pp[:, 1:2],
                                       xv[:, oc, 0:512], ALU.mult, ALU.add)
                return e.scalar_tensor_tensor(xv[:, oc, 512:1024], mk.pk[1][:],
                                              mk.misc_pp[:, 1:2],
                                              xv[:, oc, 512:1024], ALU.mult, ALU.add)
            st["stt2"] = mk.em("vector", [mm], afn, sig=mk.vE)
    mk.ln1_start = mk.vE.snap()
    mk.ln1_start_t = mk.tE.snap()


def ph_kvproj1(mk):
    xnv = mk.xnb[:, :].rearrange("p (c t) -> p c t", c=NCH)
    bar = mk.barrier()
    t0 = mk.tE.snap()
    v0 = mk.vE.snap()
    wk_sb = mk.pb(0, 16 * 512)
    wv_sb = mk.pb(16384, 16 * 512)
    wkl = mk.dma("gpsimd", wk_sb,
                 mk.wk1.ap().rearrange("(c p) o -> p c o", p=128), bar)
    wvl = mk.dma("gpsimd", wv_sb,
                 mk.wv1.ap().rearrange("(c p) o -> p c o", p=128), bar)
    bvs1 = mk.dma("sync", mk.PF[0:1, 0:512], mk.c_bvrow1.ap(), bar)
    cv1 = mk.em("vector", [bvs1],
                lambda e: e.tensor_copy(mk.PB[0:1, 19968:20480], mk.PF[0:1, 0:512]), sig=mk.vE)
    zr = mk.em("vector", [], lambda e: e.memset(mk.rowf2[0:1, 48:56], 0.0), sig=mk.vE)
    ksq = mk.pb(38912, 1024) if False else mk.pb(40960, 1024)
    kt_off = (32768, 34816)
    st = {}
    for h in range(4):
        ktile = mk.pb(kt_off[h % 2], 1024)

        def mfn(e, h=h):
            last = None
            for th in range(2):
                for ic in range(16):
                    last = e.matmul(mk.pk[th][:],
                                    wk_sb[:, ic * 512 + h * 128:ic * 512 + h * 128 + 128],
                                    xnv[:, ic, th * 512:(th + 1) * 512],
                                    start=(ic == 0), stop=(ic == 15))
            return last
        mm = mk.em("tensor", [wkl, mk.xn_ready_v, st.get(("ev", h % 2))], mfn, sig=mk.tE)

        def efn(e, h=h, ktile=ktile):
            e.activation(ktile[:, 0:512], mk.pk[0][:], AF.Identity, bias=mk.bias1[:, 16 + h:17 + h])
            return e.activation(ktile[:, 512:1024], mk.pk[1][:], AF.Identity,
                                bias=mk.bias1[:, 16 + h:17 + h])
        ev = mk.em("scalar", [mm, st.get(("dma", h % 2))], efn, sig=mk.sE)
        st[("ev", h % 2)] = ev
        sqv = mk.em("vector", [ev, st.get("knorm_mm"), zr],
                    lambda e, ktile=ktile: e.tensor_tensor(ksq, ktile, ktile, ALU.mult),
                    sig=mk.vE)

        def nfn(e):
            e.matmul(mk.prow[6][:], mk.ones_col_b[:], ksq[:, 0:512], start=True, stop=True)
            return e.matmul(mk.prow[7][:], mk.ones_col_b[:], ksq[:, 512:1024], start=True, stop=True)
        nm = mk.em("tensor", [sqv, st.get("knorm_rd")], nfn, sig=mk.tE)
        st["knorm_mm"] = nm

        def rfn(e, h=h):
            e.tensor_reduce(mk.rowf2[0:1, 48 + h:49 + h], mk.prow[6][:], AXX, ALU.max)
            e.tensor_reduce(mk.rowf2[0:1, 60:61], mk.prow[7][:], AXX, ALU.max)
            e.drain()
            return e.tensor_tensor(mk.rowf2[0:1, 48 + h:49 + h], mk.rowf2[0:1, 48 + h:49 + h],
                                   mk.rowf2[0:1, 60:61], ALU.max)
        st["knorm_rd"] = mk.em("vector", [nm], rfn, sig=mk.vE)
        st[("dma", h % 2)] = mk.dma("gpsimd", mk.kin1.ap()[h * 128:(h + 1) * 128, :],
                                    ktile, [ev])
    mk.kin1_done = mk.dG.snap()
    mk.knorms1_done = mk.vE.snap()
    g4 = [[0, 1, 2, 3], [4, 5, 6, 7]]
    g8 = [list(range(8))]
    mk.em("gpsimd", [mk.kin1_done],
          lambda e: e.collective_compute("AllGather", ALU.bypass, replica_groups=g4,
                                         ins=[mk.kin1.ap().opt()], outs=[mk.kout1.ap().opt()]),
          sig=mk.cc)
    mk.cc_agk1 = mk.cc.snap()
    arld = mk.dma("gpsimd", mk.ar1_in.ap(), mk.rowf2[0:1, 48:56], [mk.knorms1_done])
    mk.em("gpsimd", [arld],
          lambda e: e.collective_compute("AllReduce", ALU.max, replica_groups=g8,
                                         ins=[mk.ar1_in.ap().opt()], outs=[mk.ar1_out.ap().opt()]),
          sig=mk.cc)
    mk.cc_ar1 = mk.cc.snap()

    vt_off = (36864, 37888)
    for tt in range(8):
        def vfn(e, tt=tt):
            e.matmul(mk.pk[2 + tt % 2][:], mk.ones_row_b[:], mk.PB[0:1, 19968:20480],
                     start=True, stop=False)
            last = None
            for ic in range(16):
                last = e.matmul(mk.pk[2 + tt % 2][:],
                                xnv[:, ic, tt * 128:(tt + 1) * 128],
                                wv_sb[:, ic * 512:(ic + 1) * 512],
                                start=False, stop=(ic == 15))
            return last
        mm = mk.em("tensor", [wvl, cv1, st.get(("vev", tt % 2))], vfn, sig=mk.tE)
        vtile = mk.pb(vt_off[tt % 2], 512)
        ev = mk.em("scalar", [mm, st.get(("vdma", tt % 2))],
                   lambda e, vtile=vtile, tt=tt: e.activation(vtile, mk.pk[2 + tt % 2][:], AF.Copy),
                   sig=mk.sE)
        st[("vev", tt % 2)] = ev
        st[("vdma", tt % 2)] = mk.dma("gpsimd", mk.vin1.ap()[tt * 128:(tt + 1) * 128, :],
                                      vtile, [ev])
    mk.vin1_done = mk.dG.snap()
    mk.em("gpsimd", [mk.vin1_done],
          lambda e: e.collective_compute("AllGather", ALU.bypass, replica_groups=g4,
                                         ins=[mk.vin1.ap().opt()], outs=[mk.vout1.ap().opt()]),
          sig=mk.cc)
    mk.cc_agv1 = mk.cc.snap()
    mk.ar1_row_loaded = mk.dma("sync", mk.rowf2[0:1, 32:40], mk.ar1_out.ap(), [mk.cc_ar1])
    mk.pool_entry_t = mk.tE.snap()
    mk.pool_entry_v = mk.vE.snap()


def ph_ffn1(mk):
    xnv = mk.xnb[:, :].rearrange("p (c t) -> p c t", c=NCH)
    bar = mk.barrier()
    xv = mk.x[:, :].rearrange("p (c t) -> p c t", c=NCH)
    w1v = mk.w1T.ap().rearrange("(c p) o -> p c o", p=128)
    w3v = mk.w3T.ap().rearrange("(c p) o -> p c o", p=128)
    w1bv = mk.w1b.ap().rearrange("(c p) o -> p c o", p=128)
    w3bv = mk.w3b.ap().rearrange("(c p) o -> p c o", p=128)
    w2v = mk.w2T.ap().rearrange("(c p) o -> p c o", p=128)
    w2bv = mk.w2b.ap().rearrange("(c p) o -> p c o", p=128)
    w1t = [mk.pb(0, 2048), mk.pb(4096, 2048)]
    w3t = [mk.pb(8192, 2048), mk.pb(12288, 2048)]
    t1 = mk.pb(16384, 512)
    hpre = mk.pb(17408, 512)
    tmpt = mk.pb(18432, 512)
    f1_t0 = mk.tE.snap()
    f1_v0 = mk.vE.snap()
    last_dma = None
    bf_tmps = [mk.pf(4096 + 2048 * j, 512) for j in range(4)]
    for p in range(2):
        st = {}
        bf_applied = {}
        bf_state = {"last": None}

        def bf_ready(limit, bf_applied=bf_applied):
            out = []
            touched = set()
            for c in sorted(bf_applied):
                if len(out) >= limit:
                    break
                k = bf_applied[c]
                if k >= 6 or c in touched:
                    continue
                pp = c ^ (1 << k)
                if pp > c and pp not in touched and bf_applied.get(pp) == k:
                    out.append((c, pp, k))
                    touched.add(c)
                    touched.add(pp)
            for c, pp, k in out:
                bf_applied[c] += 1
                bf_applied[pp] += 1
            return out

        def bf_emit(limit, st=st, bf_applied=bf_applied, bf_state=bf_state):
            while True:
                batch = bf_ready(min(limit, 4))
                if not batch:
                    return
                waits = [st[("hevc", c)] for c, pp, k in batch if k == 0] +                         [st[("hevc", pp)] for c, pp, k in batch if k == 0]

                def bfb(e, batch=batch):
                    e.drain()
                    for j, (aa, bb, k) in enumerate(batch):
                        e.tensor_tensor(bf_tmps[j], mk.hid(aa), mk.hid(bb), ALU.subtract)
                        e.tensor_tensor(mk.hid(aa), mk.hid(aa), mk.hid(bb), ALU.add)
                    e.drain()
                    last = None
                    for j, (aa, bb, k) in enumerate(batch):
                        last = e.tensor_copy(mk.hid(bb), bf_tmps[j])
                    return last
                bf_state["last"] = mk.em("vector", waits, bfb, sig=mk.vE)
                limit -= len(batch)
                if limit <= 0:
                    return

        def emit_h128(ht2, st=st):
            r2 = ht2 % 2
            pk_h2 = mk.pk[2] if r2 == 0 else mk.pk[5]
            hm = mk.em("tensor", [st["hp"], st.get(("hev", r2))],
                       lambda e, pk_h2=pk_h2: e.matmul(pk_h2[:], mk.h128[:], hpre[:],
                                                       start=True, stop=True),
                       sig=mk.tE)
            st["h128_last"] = hm
            hev = mk.em("scalar", [hm],
                        lambda e, ht2=ht2, pk_h2=pk_h2: e.activation(mk.hid(ht2), pk_h2[:],
                                                                     AF.Copy),
                        sig=mk.sE)
            st[("hev", r2)] = hev
            st[("hevc", ht2)] = hev
            bf_applied[ht2] = 0

        for ht in range(HCH):
            r = ht % 2
            if ht < 2:
                first = bar + [mk.x_after_attn] if p == 0 else [f1_t0, f1_v0]
            else:
                first = []
            w1l = mk.dma("gpsimd", w1t[r], w1bv[:, :, ht * 128:(ht + 1) * 128],
                         [st.get(("ug", r))] + first)
            w3l = mk.dma("sync", w3t[r], w3bv[:, :, ht * 128:(ht + 1) * 128],
                         [st.get(("ul", r))] + first)

            pk_ug, pk_ul, pk_h = (mk.pk[0], mk.pk[1], mk.pk[2]) if r == 0 else \
                                 (mk.pk[3], mk.pk[4], mk.pk[5])

            def ugf(e, r=r, p=p, pk_ug=pk_ug):
                last = None
                for ic in range(16):
                    last = e.matmul(pk_ug[:], w1t[r][:, ic * 128:(ic + 1) * 128],
                                    xnv[:, ic, p * 512:(p + 1) * 512],
                                    start=(ic == 0), stop=(ic == 15))
                return last
            ug = mk.em("tensor", [w1l, st.get(("silu", r))], ugf, sig=mk.tE)
            st[("ug", r)] = ug

            def ulf(e, r=r, p=p, pk_ul=pk_ul):
                last = None
                for ic in range(16):
                    last = e.matmul(pk_ul[:], w3t[r][:, ic * 128:(ic + 1) * 128],
                                    xnv[:, ic, p * 512:(p + 1) * 512],
                                    start=(ic == 0), stop=(ic == 15))
                return last
            ul = mk.em("tensor", [w3l, st.get(("hpre", r))], ulf, sig=mk.tE)
            st[("ul", r)] = ul

            if ht > 0:
                emit_h128(ht - 1)

            silu = mk.em("scalar", [ug, st.get("hp")],
                         lambda e, pk_ug=pk_ug: e.activation(t1[:], pk_ug[:], AF.Silu),
                         sig=mk.sE)
            st[("silu", r)] = silu
            hp = mk.em("vector", [silu, ul, st.get("h128_last")],
                       lambda e, pk_ul=pk_ul: e.tensor_tensor(hpre[:], t1[:], pk_ul[:], ALU.mult),
                       sig=mk.vE)
            st[("hpre", r)] = hp
            st["hp"] = hp
            bf_emit(4)
        emit_h128(HCH - 1)
        bf_emit(10 ** 9)
        hid_fwht = bf_state["last"]

        w2t = [mk.pb(0, 4096), mk.pb(8192, 4096)]
        osum = mk.pf(0, 512)
        ott = mk.pf(2048 // 4, 512)
        st2 = {}
        for ot in range(NCH):
            mms = None
            for half in range(2):
                r = half
                first = [f1_t0, hid_fwht] if ot == 0 else []
                w2l = mk.dma("gpsimd" if half == 0 else "sync", w2t[r],
                             w2bv[:, half * 32:(half + 1) * 32, ot * 128:(ot + 1) * 128],
                             [st2.get(("mm", r))] + first)

                def mfn(e, half=half, r=r, ot=ot):
                    last = None
                    for hc in range(32):
                        last = e.matmul(mk.pk[6][:], w2t[r][:, hc * 128:(hc + 1) * 128],
                                        mk.hid(half * 32 + hc),
                                        start=(half == 0 and hc == 0),
                                        stop=(half == 1 and hc == 31))
                    return last
                mms = mk.em("tensor", [w2l, hid_fwht if ot == 0 else None,
                                       st2.get("stt") if half == 0 else None], mfn,
                            sig=mk.tE)
                st2[("mm", r)] = mms

            def sfn(e, ot=ot, p=p):
                return e.scalar_tensor_tensor(osum[:], mk.pk[6][:], INV_SQRT_HFF,
                                              xv[:, ot, p * 512:(p + 1) * 512],
                                              ALU.mult, ALU.add)
            stt = mk.em("vector", [mms, st2.get("oev")], sfn, sig=mk.vE)
            st2["stt"] = stt

            def tfn(e):
                last = None
                for tt in range(4):
                    last = e.transpose(mk.pk[7][:, tt * 128:(tt + 1) * 128],
                                       osum[:, tt * 128:(tt + 1) * 128], mk.idf[:])
                return last
            tr = mk.em("tensor", [stt, st2.get("oev")], tfn, sig=mk.tE)

            oev = mk.em("vector", [tr, st2.get("odma")],
                        lambda e: e.tensor_copy(ott[:], mk.pk[7][:]), sig=mk.vE)
            st2["oev"] = oev
            odst = mk.out.ap().rearrange("(t p) d -> p t d", p=128)[
                :, p * 4:(p + 1) * 4, ot * 128:(ot + 1) * 128]
            st2["odma"] = mk.dma("gpsimd", odst,
                                 ott[:].rearrange("p (t d) -> p t d", t=4), [oev])
        last_dma = st2["odma"]
        f1_t0 = mk.tE.snap()
        f1_v0 = mk.vE.snap()
    mk.em("gpsimd", [mk.dG.snap()], lambda e: None)


def ph_tern_tasks(mk):
    aslot = [mk.pf(4096, 1024), mk.pf(8192, 1024)]
    tb = mk.pb(51200, 1024)
    bar = mk.barrier()
    st = mk._tern_st = {}

    def task(it, hf):
        def run(it=it, hf=hf):
            r = (it * 2 + hf) % 2
            a = aslot[r]
            la = mk.dma("sync", a,
                        mk.w0T.ap()[it * 128:(it + 1) * 128, hf * 1024:(hf + 1) * 1024],
                        [st.get(("rd", r))] + ([] if st else bar))

            def tfn(e, a=a):
                e.tensor_scalar(a, a, mk.misc_pp[:, 0:1], None, ALU.mult)
                e.drain()
                e.tensor_scalar(a, a, 1.0, -1.0, ALU.min, ALU.max)
                e.drain()
                return e.tensor_scalar(tb, a, MAGIC, -MAGIC, ALU.add, ALU.add)
            tern = mk.em("vector", [la, mk.ffn0_scales_ready, st.get("spill")],
                         tfn, sig=mk.vE)
            st[("rd", r)] = tern
            st["spill"] = mk.dma(
                "sync", mk.w0qb.ap()[it * 128:(it + 1) * 128, hf * 1024:(hf + 1) * 1024],
                tb, [tern])
        return run

    for it in range(16):
        for hf in range(2):
            mk.bg_tasks.append(task(it, hf))


def ph_wcast(mk):
    def task(t, i):
        def run(t=t, i=i):
            mk.dma("gpsimd", t[0].ap()[i * 128:(i + 1) * 128, :],
                   t[1].ap()[i * 128:(i + 1) * 128, :], [])
        return run

    for i in range(16):
        mk.wcast_tasks.append(task((mk.w1b, mk.w1T), i))
        mk.wcast_tasks.append(task((mk.w3b, mk.w3T), i))
    for i in range(16):
        mk.wcast_tasks.append(task((mk.w2b, mk.w2T), i * 4))
        mk.wcast_tasks.append(task((mk.w2b, mk.w2T), i * 4 + 1))
        mk.wcast_tasks.append(task((mk.w2b, mk.w2T), i * 4 + 2))
        mk.wcast_tasks.append(task((mk.w2b, mk.w2T), i * 4 + 3))


def ph_epilogue_dev(mk):
    xv = mk.x[:, :].rearrange("p (c t) -> p c t", c=NCH)
    bar = mk.barrier()
    t0 = mk.tE.snap()
    v0 = mk.vE.snap()
    ott = mk.pf(0, 1024)
    st = {}
    for ot in range(NCH):
        def tfn(e, ot=ot):
            last = None
            for tt in range(8):
                bank = mk.pk[2 + tt // 4]
                last = e.transpose(bank[:, (tt % 4) * 128:(tt % 4 + 1) * 128],
                                   xv[:, ot, tt * 128:(tt + 1) * 128], mk.idf[:])
            return last
        tr = mk.em("tensor", [mk.x_final, st.get("oev")] + (bar if ot == 0 else []),
                   tfn, sig=mk.tE)

        def oef(e):
            e.tensor_copy(ott[:, 0:512], mk.pk[2][:])
            return e.tensor_copy(ott[:, 512:1024], mk.pk[3][:])
        oev = mk.em("vector", [tr, st.get("odma")] + (bar if ot == 0 else []), oef, sig=mk.vE)
        st["oev"] = oev
        odst = mk.out.ap().rearrange("(t p) d -> p t d", p=128)[:, :, ot * 128:(ot + 1) * 128]
        st["odma"] = mk.dma("gpsimd", odst,
                            ott[:, :].rearrange("p (t d) -> p t d", t=8), [oev])
    mk.em("gpsimd", [mk.dG.snap()], lambda e: None)



_CACHE = {}


def build(stage=STAGE):
    if stage in _CACHE:
        return _CACHE[stage]
    mk = MK(stage)
    ph_prologue(mk)
    ph_ln(mk, 0)
    ph_quantpre(mk)
    ph_wq0prep_tasks(mk)
    pa = ph_passA_tasks(mk)
    mixed = []
    wq = list(mk.bg_tasks)
    mk.bg_tasks.clear()
    for i in range(64):
        mixed.append(wq[i])
        if i % 2 == 0:
            mixed.append(pa[i // 2])
    mixed += pa[32:]
    mk.bg_tasks.extend(mixed)
    ph_kvproj0(mk)
    while mk.bg_tasks:
        mk.bg_tasks.popleft()()
    mk.wq0b_done = mk.dG.snap()
    mk.flush("gpsimd")
    mk.passA_acc = mk._pa_st["acc"]
    ph_quantfin(mk)
    ph_ffn0passA(mk)
    if stage >= 4:
        ph_wcast(mk)
    if stage >= 2:
        ph_tern_tasks(mk)
    ph_attn(mk, 0)
    while mk.bg_tasks:
        mk.bg_tasks.popleft()()
    mk.w0qb_done = mk.dS.snap()
    ph_outproj(mk, 0)
    if stage >= 2:
        ph_ffn0mm(mk)
    if stage >= 3:
        ph_ln(mk, 1)
        ph_kvproj1(mk)
        ph_attn(mk, 1)
        ph_outproj(mk, 1)
    while mk.wcast_tasks:
        mk.wcast_tasks.popleft()()
    if stage >= 4:
        ph_ffn1(mk)
    else:
        mk.x_final = mk.vE.snap()
        ph_epilogue_dev(mk)
    nc = mk.finalize()
    _CACHE[stage] = nc
    return nc


def _consts():
    bf = ml_dtypes.bfloat16
    i = np.arange(128)
    had = np.where((np.bitwise_count(i[:, None] & i[None, :]) % 2) == 0, 1.0, -1.0)
    return {
        "c_idf": np.eye(128, dtype=np.float32),
        "c_ones_col_b": np.ones((128, 1), bf),
        "c_ones_row_b": np.ones((1, 128), bf),
        "c_ones_row_f": np.ones((1, 128), np.float32),
        "c_h128": had.astype(bf),
    }


def _fm_pack(vec):
    n = np.asarray(vec).shape[0] // 128
    return np.ascontiguousarray(np.asarray(vec, np.float32).reshape(n, 128).T)


def _prep_inputs(inp):
    x = np.asarray(inp["L0_attn_x"], np.float32)
    cst = _consts()
    rep = lambda s: np.ascontiguousarray(np.repeat(np.asarray(s, np.float32), 32, axis=0))
    tr = lambda w: np.ascontiguousarray(np.asarray(w, np.float32).T)
    bias0 = np.zeros((128, 64), np.float32)
    bias0[:, 0:16] = _fm_pack(inp["L0_attn_bq"])
    bias0[:, 16:32] = _fm_pack(inp["L0_attn_bk"])
    bias0[:, 32:48] = _fm_pack(inp["L0_attn_bv"])
    bias0[:, 48:64] = _fm_pack(inp["L0_attn_bo"])
    bias1 = np.zeros((128, 48), np.float32)
    bias1[:, 0:16] = _fm_pack(inp["L1_attn_bq"])
    bias1[:, 16:20] = _fm_pack(inp["L1_attn_bk"])
    bias1[:, 32:48] = _fm_pack(inp["L1_attn_bo"])
    shared = dict(
        wq0c=np.asarray(inp["L0_attn_wq_codes"], np.float32),
        wk0c=np.asarray(inp["L0_attn_wk_codes"], np.float32),
        wv0c=np.asarray(inp["L0_attn_wv_codes"], np.float32),
        wo0c=np.asarray(inp["L0_attn_wo_codes"], np.float32),
        sq0=rep(inp["L0_attn_wq_scales"]), sk0=rep(inp["L0_attn_wk_scales"]),
        sv0=rep(inp["L0_attn_wv_scales"]), so0=rep(inp["L0_attn_wo_scales"]),
        w0T=tr(inp["L0_ffn_weight"]),
        wq1=np.asarray(inp["L1_attn_wq"], np.float32),
        wk1=np.asarray(inp["L1_attn_wk"], np.float32),
        wv1=np.asarray(inp["L1_attn_wv"], np.float32),
        wo1=np.asarray(inp["L1_attn_wo"], np.float32),
        w1T=tr(inp["L1_ffn_W1"]), w3T=tr(inp["L1_ffn_W3"]), w2T=tr(inp["L1_ffn_W2"]),
        c_bias0=bias0, c_bias1=bias1,
        c_gamma=_fm_pack(inp["L0_ffn_norm_weight"]),
        c_bvrow0=np.asarray(inp["L0_attn_bv"], np.float32).reshape(1, 2048),
        c_bvrow1=np.asarray(inp["L1_attn_bv"], np.float32).reshape(1, 512),
        **cst,
    )
    in_maps = []
    for c in range(8):
        b, tk = c // 4, (c % 4) * TPC
        m = dict(shared)
        m["x_in"] = np.ascontiguousarray(x[b, tk:tk + TPC, :])
        in_maps.append(m)
    return in_maps


def kernel(**inp):
    nc = build(STAGE)
    in_maps = _prep_inputs(inp)
    res = run_bass_kernel_spmd(nc, in_maps, core_ids=list(range(8)))
    shards = [res.results[c]["out"] for c in range(8)]
    return np.stack([np.concatenate(shards[0:4], 0),
                     np.concatenate(shards[4:8], 0)]).reshape(2, 4096, 2048)
